# version 26
# baseline (speedup 1.0000x reference)
import numpy as np
import concourse.bass as bass
import concourse.mybir as mybir
from concourse.bass_utils import run_bass_kernel_spmd

F32 = mybir.dt.float32
BF16 = mybir.dt.bfloat16
AF = mybir.ActivationFunctionType
ALU = mybir.AluOpType

NCORES = 8
T = 4096
D = 1024
H = 16
NT = 8
TT = 512
JB = 8
NSLOT = 8
B_EXP = -20.0
EPS = 1e-5
MULT_F = 16384
NQ = 4

ENG_NAMES = ("sync", "tensor", "vector", "scalar", "gpsimd")


class Op:
    __slots__ = ("idx", "eng", "emit", "deps", "dma", "inc_no", "needed", "rdeps")

    def __init__(self, idx, eng, emit, deps, dma):
        self.idx = idx
        self.eng = eng
        self.emit = emit
        self.deps = deps
        self.dma = dma
        self.inc_no = None
        self.needed = False


class Sched:

    def __init__(self):
        self.ops = []
        self.last_w = {}
        self.readers = {}

    def add(self, eng, emit, reads=(), writes=(), dma=False):
        deps = set()
        for t in reads:
            w = self.last_w.get(t)
            if w is not None:
                deps.add(w)
        for t in writes:
            w = self.last_w.get(t)
            if w is not None:
                deps.add(w)
            deps.update(self.readers.get(t, ()))
        op = Op(len(self.ops), eng, emit, deps, dma)
        self.ops.append(op)
        for t in reads:
            self.readers.setdefault(t, set()).add(op)
        for t in writes:
            self.last_w[t] = op
            self.readers[t] = set()
        return op


NLANE = 12


def emit_schedule(nc, sched, sems, qsems, inc_counters, qlane_counters,
                  qlane_last):
    ops = sched.ops
    for op in ops:
        if op.dma:
            lanes = qlane_counters[op.eng]
            lane = min(range(NLANE), key=lambda s: lanes[s])
            lanes[lane] += 1
            op.inc_no = (lane, lanes[lane])
            prev = qlane_last[op.eng][lane]
            if prev is not None:
                op.deps.add(prev)
            qlane_last[op.eng][lane] = op
    for op in ops:
        rd = [d for d in op.deps
              if not (d.eng == "tensor" and op.eng == "tensor"
                      and not d.dma and not op.dma)]
        best = {}
        keep = []
        for d in rd:
            if d.dma:
                key = ("q", d.eng, d.inc_no[0])
            elif d.eng == "tensor":
                key = ("pe",)
            else:
                keep.append(d)
                continue
            if key not in best or best[key].idx < d.idx:
                best[key] = d
        op.rdeps = keep + list(best.values())
        for d in op.rdeps:
            d.needed = True
    for op in ops:
        if not op.dma and op.needed:
            inc_counters[op.eng] += 1
            op.inc_no = inc_counters[op.eng]

    wait_cache = {e: {} for e in ENG_NAMES}

    def body_for(engname):
        eops = [op for op in ops if op.eng == engname]

        def body(eng):
            cache = wait_cache[engname]
            for op in eops:
                for d in op.rdeps:
                    if d.inc_no is None:
                        continue
                    if d.dma:
                        lane, cnt = d.inc_no
                        sem, val = qsems[d.eng][lane], 16 * cnt
                    else:
                        sem, val = sems[d.eng], d.inc_no
                    key = id(sem)
                    if cache.get(key, -1) >= val:
                        continue
                    eng.wait_ge(sem, val)
                    cache[key] = val
                ins = op.emit(eng)
                if op.inc_no is not None:
                    assert ins is not None, f"op {op.idx} on {engname} needs inc"
                    if op.dma:
                        lane, cnt = op.inc_no
                        ins.then_inc(qsems[op.eng][lane], 16)
                    else:
                        ins.then_inc(sems[op.eng], 1)
        return body

    with nc.Block() as block:
        for engname in ENG_NAMES:
            if any(op.eng == engname for op in ops):
                getattr(block, engname)(body_for(engname))


def build_program(ni=NT, do_attn=True, do_ph3=True, do_prep_wct=True,
                  do_rails=True, p1_gate=127, do_wcast=True, do_bias=True):
    nc = bass.Bass("TRN2", target_bir_lowering=False)

    x_in = nc.dram_tensor("x", [T, D], F32, kind="ExternalInput").ap()
    w_ins = {n: nc.dram_tensor(n, [D, D], F32, kind="ExternalInput").ap()
             for n in ("wq", "wk", "wv", "wo")}
    b_ins = {n: nc.dram_tensor(n, [D], F32, kind="ExternalInput").ap()
             for n in ("bq", "bk", "bv", "bo")}
    rm_in = nc.dram_tensor("rm", [4, 64, 16, 16], F32, kind="ExternalInput").ap()
    ru_in = nc.dram_tensor("ru", [4, 64, 64, 16], F32, kind="ExternalInput").ap()
    wcp_in = nc.dram_tensor("wcp", [D, MULT_F], F32, kind="ExternalInput").ap()
    out_hbm = nc.dram_tensor("out", [T, D], F32, kind="ExternalOutput").ap()

    wqkv_d = nc.dram_tensor("wqkv_d", [4, 8, 128, D], BF16, kind="Internal")
    wcT_d = nc.dram_tensor("wcT_d", [128, 128, D], BF16, kind="Internal")
    k_d = nc.dram_tensor("k_d", [32, 128, D], BF16, kind="Internal")
    v_d = nc.dram_tensor("v_d", [32, 128, H * 65], BF16, kind="Internal")
    x2_d = nc.dram_tensor("x2_d", [T, D], F32, kind="Internal")
    xh3_d = nc.dram_tensor("xh3_d", [8, 128, T], BF16, kind="Internal")
    rl_d = nc.dram_tensor("rl_d", [16, TT], BF16, kind="Internal")

    consts = nc.alloc_sbuf_tensor("consts", [128, 4], F32)
    ident_b = nc.alloc_sbuf_tensor("ident_b", [128, 128], BF16)
    ones_b = nc.alloc_sbuf_tensor("ones_b", [128, 128], BF16)
    masks2 = nc.alloc_sbuf_tensor("masks2", [128, 8 * TT], BF16)
    bias_all = nc.alloc_sbuf_tensor("bias_all", [128, 24], F32)
    bo_bh = [nc.alloc_sbuf_tensor(f"bo_bh{p}", [1, TT], BF16) for p in range(2)]
    statsA = nc.alloc_sbuf_tensor("statsA", [128, 12], F32)
    statsB = nc.alloc_sbuf_tensor("statsB", [128, 12], F32)
    aggrA = nc.alloc_sbuf_tensor("aggrA", [128, 2], F32)
    aggrB = nc.alloc_sbuf_tensor("aggrB", [128, 2], F32)
    rstdA = nc.alloc_sbuf_tensor("rstdA", [128, 1], F32)
    rstdB = nc.alloc_sbuf_tensor("rstdB", [128, 1], F32)
    zw = nc.alloc_sbuf_tensor("zw", [128, 32 * 128], BF16)
    wcin = [nc.alloc_sbuf_tensor(f"wcin{p}", [128, 512], F32) for p in range(3)]
    wcb = [nc.alloc_sbuf_tensor(f"wcb{p}", [128, 512], BF16) for p in range(3)]
    wcst = [nc.alloc_sbuf_tensor(f"wcst{p}", [128, 512], BF16) for p in range(1)]
    statsC = nc.alloc_sbuf_tensor("statsC", [128, 12], F32)
    statsD = nc.alloc_sbuf_tensor("statsD", [128, 12], F32)
    aggrC = nc.alloc_sbuf_tensor("aggrC", [128, 2], F32)
    aggrD = nc.alloc_sbuf_tensor("aggrD", [128, 2], F32)
    rstdC = nc.alloc_sbuf_tensor("rstdC", [128, 1], F32)
    rstdD = nc.alloc_sbuf_tensor("rstdD", [128, 1], F32)
    rmL = [nc.alloc_sbuf_tensor(f"rmL{p}", [128, 16], F32) for p in range(3)]
    rmLb = [nc.alloc_sbuf_tensor(f"rmLb{p}", [128, 16], BF16) for p in range(3)]
    rmT = [nc.alloc_sbuf_tensor(f"rmT{p}", [16, 128], BF16) for p in range(3)]
    ruL = [nc.alloc_sbuf_tensor(f"ruL{p}", [128, 16], F32) for p in range(3)]
    ruLb = [nc.alloc_sbuf_tensor(f"ruLb{p}", [128, 16], BF16) for p in range(3)]
    ruT = [nc.alloc_sbuf_tensor(f"ruT{p}", [16, 512], BF16) for p in range(3)]
    cstage = [nc.alloc_sbuf_tensor(f"cstage{p}", [16, 64], BF16) for p in range(2)]
    ltmp = nc.alloc_sbuf_tensor("ltmp", [1, TT], F32)

    cursor = 229376 - nc.sbuf_bytes_remaining
    base = (cursor + 127) & ~127
    off = [0]

    def at(name, shape, dtype, nbytes):
        t = nc.alloc_sbuf_tensor_at(name, shape, dtype, offset=base + off[0])
        off[0] += (nbytes + 127) & ~127
        return t

    wbuf = [at(f"wbuf{p}", [128, 8 * 1024], BF16, 16384) for p in range(2)]
    q_sb = at("q_sb", [128, 8 * TT], BF16, 8192)
    xh1T = at("xh1T", [128, 8 * TT], BF16, 8192)
    aoT = at("aoT", [128, 8 * TT], BF16, 8192)
    x_tile = [at(f"x_tile{s}", [128, D], F32, 4096) for s in range(2)]
    xres = [at(f"xres{s}", [128, D], F32, 4096) for s in range(2)]
    xh1 = [at(f"xh1{p}", [128, D], BF16, 2048) for p in range(2)]
    k_tile = at("k_tile", [128, 8 * TT], BF16, 8192)
    expt = k_tile
    v_tile = at("v_tile", [128, 8 * TT], BF16, 8192)
    kread = [at(f"kread{p}", [128, D], BF16, 2048) for p in range(NSLOT)]
    vread = [at(f"vread{p}", [128, H * 65], BF16, 2112) for p in range(NSLOT)]
    ao_acc = at("ao_acc", [128, 8 * TT], F32, 16384)
    l_parts = at("l_parts", [16, 2 * TT], F32, 4096)
    rl_all = at("rl_all", [16, TT], F32, 2048)
    rl_bf = at("rl_bf", [16, TT], BF16, 1024)
    v_stage = [at(f"v_stage{p}", [128, H * 65], BF16, 2112) for p in range(1)]
    x2_sb = [at(f"x2_sb{p}", [128, D], F32, 4096) for p in range(2)]
    x3h = [at(f"x3h{p}", [128, D], BF16, 2048) for p in range(4)]
    x3st = [at(f"x3st{p}", [128, 8 * 128], BF16, 2048) for p in range(1)]
    rlb = [at(f"rlb{p}", [128, TT], BF16, 2048) for p in range(2)]
    ph12_end = off[0]

    off[0] = 0
    out_q = at("out_q", [128, 8 * 1024], F32, 32768)
    h_tl = [at(f"h_tl{p}", [128, 16 * TT], BF16, 16384) for p in range(2)]
    wcbuf = [at(f"wcbuf{p}", [128, 8 * 1024], BF16, 16384) for p in range(2)]
    xh3q = at("xh3q", [128, 8 * 1024], BF16, 16384)
    x2r = [at(f"x2r{p}", [128, D], F32, 4096) for p in range(2)]
    ident_f = at("ident_f", [128, 128], F32, 512)
    outst = [at(f"outst{p}", [128, D], F32, 4096) for p in range(2)]
    ph3_end = off[0]
    assert base + max(ph12_end, ph3_end) <= 229376, (base, ph12_end, ph3_end)

    psa = nc.alloc_psum_tensor("psa", [128, 1024], F32)
    psc = nc.alloc_psum_tensor("psc", [128, 2048], F32)
    pstp2 = [nc.alloc_psum_tensor("pstpA", [128, 1024], BF16),
             nc.alloc_psum_tensor("pstpB", [128, 1024], BF16)]
    tpar = [0]

    sems = {e: nc.alloc_semaphore(f"s_{e}") for e in ENG_NAMES}
    qsems = {e: [nc.alloc_semaphore(f"q_{e}{s}") for s in range(NLANE)]
             for e in ("sync", "scalar", "gpsimd")}
    inc_c = {e: 0 for e in ENG_NAMES}
    qlane_c = {e: [0] * NLANE for e in ("sync", "scalar", "gpsimd")}
    qlane_last = {e: [None] * NLANE for e in ("sync", "scalar", "gpsimd")}

    S = Sched()

    def G(eng, emit, reads=(), writes=(), dma=False):
        S.add(eng, emit, reads, writes, dma)

    ln_sets = ((statsA, aggrA, rstdA, "A"), (statsB, aggrB, rstdB, "B"),
               (statsC, aggrC, rstdC, "C"), (statsD, aggrD, rstdD, "D"))
    ln_rot = [0]

    def ln_stats(par, src0, src1, rtags):
        st, ag, rs, p = ln_sets[ln_rot[0] % 4]
        ln_rot[0] += 1
        G("vector", lambda v, a=src0, st=st: v.bn_stats(st[:, 0:6], a),
          reads=rtags, writes=[f"st0{p}"])
        G("vector", lambda v, a=src1, st=st: v.bn_stats(st[:, 6:12], a),
          reads=rtags, writes=[f"st1{p}"])
        G("vector", lambda v, st=st, ag=ag: v.bn_aggr(ag[:], st[:]),
          reads=[f"st0{p}", f"st1{p}"], writes=[f"ag{p}"])
        G("scalar", lambda s, ag=ag, rs=rs: s.activation(
            rs[:], ag[:, 1:2], AF.Ln, bias=consts[:, 1:2]),
          reads=[f"ag{p}", "c:eps"], writes=[f"rsL{p}"])
        G("scalar", lambda s, rs=rs: s.activation(rs[:], rs[:], AF.Exp, scale=-0.5),
          reads=[f"rsL{p}"], writes=[f"rs{p}"])
        return ag, rs, (f"ag{p}", f"rs{p}")

    G("gpsimd", lambda g: g.memset(consts[:, 0:1], B_EXP), writes=["c:bexp"])
    G("gpsimd", lambda g: g.memset(consts[:, 1:2], EPS), writes=["c:eps"])
    G("gpsimd", lambda g: g.memset(ones_b[:], 1.0), writes=["ones_b"])
    G("gpsimd", lambda g: g.affine_select(ident_b[:], ones_b[:], [[1, 128]],
                                          ALU.is_equal, 0.0, base=0,
                                          channel_multiplier=-1),
      reads=["ones_b"], writes=["ident_b"])
    G("gpsimd", lambda g: g.memset(masks2[:], 1.0), writes=["m2init"])
    for d in range(4):
        G("gpsimd", lambda g, d=d: g.affine_select(
            masks2[:, d * 1024:(d + 1) * 1024],
            masks2[:, d * 1024:(d + 1) * 1024], [[0, 2], [1, TT]],
            ALU.is_ge, 0.0, base=-128 * d, channel_multiplier=-1),
          reads=["m2init"], writes=[f"mask2_{d}"])
    G("gpsimd", lambda g: g.memset(zw[:], 0.0), writes=["zw"])
    G("gpsimd", lambda g: g.memset(v_stage[0][:], 1.0), writes=["v_stage0"])

    for pi, bn in enumerate(("bq", "bk", "bv") if do_bias else ()):
        for c in range(8):
            src = b_ins[bn][c * 128:(c + 1) * 128].rearrange("(p o) -> p o", o=1)
            col = pi * 8 + c
            G("gpsimd", lambda g, src=src, col=col: g.dma_start(
                out=bias_all[:, col:col + 1], in_=src),
              writes=[f"bias{col}"], dma=True)
    for hf in range(2):
        G("gpsimd", lambda g, hf=hf: g.dma_start(
            out=ltmp[:], in_=b_ins["bo"][hf * TT:(hf + 1) * TT]
            .rearrange("(o d) -> o d", o=1)),
          writes=["ltmp"], dma=True)
        G("gpsimd", lambda g, hf=hf: g.tensor_copy(bo_bh[hf][:], ltmp[:]),
          reads=["ltmp"], writes=["bo_b"])

    for pi, wn in enumerate(("wq", "wk", "wv", "wo") if do_wcast else ()):
        for c in range(8):
            for hf in range(2):
                p = hf
                G("gpsimd", lambda g, wn=wn, c=c, hf=hf, p=p: g.dma_start(
                    out=wcin[p][:],
                    in_=w_ins[wn][c * 128:(c + 1) * 128, hf * 512:(hf + 1) * 512]),
                  writes=[f"wcin{p}"], dma=True)
                G("vector", lambda v, p=p: v.tensor_copy(wcb[p][:], wcin[p][:]),
                  reads=[f"wcin{p}"], writes=[f"wcb{p}"])
                G("gpsimd", lambda g, pi=pi, c=c, hf=hf, p=p: g.dma_start(
                    out=wqkv_d.ap()[pi, c, :, hf * 512:(hf + 1) * 512],
                    in_=wcb[p][:]),
                  reads=[f"wcb{p}"], writes=[f"wqkv:{pi}:{c}"], dma=True)

    def rails_prep(h, oc, pp):
        rsl = rm_in[h, oc * 8:(oc + 1) * 8].rearrange("r i j -> (r i) j")
        G("sync", lambda e, rsl=rsl, pp=pp: e.dma_start(out=rmL[pp][:], in_=rsl),
          writes=[f"rmL{pp}"], dma=True)
        G("gpsimd", lambda g, pp=pp: g.tensor_copy(rmLb[pp][:], rmL[pp][:]),
          reads=[f"rmL{pp}"], writes=[f"rmLb{pp}"])
        tb = tpar[0] % 2; tpar[0] += 1
        G("tensor", lambda t, tb=tb, pp=pp: t.matmul(
            pstp2[tb][0:16, 0:128], rmLb[pp][:], ident_b[:], is_transpose=True),
          reads=[f"rmLb{pp}", "ident_b"], writes=[f"pstp{tb}"])
        G("vector", lambda v, tb=tb, pp=pp: v.tensor_copy(
            rmT[pp][:], pstp2[tb][0:16, 0:128]),
          reads=[f"pstp{tb}"], writes=[f"rmT{pp}"])
        for pr in range(4):
            r0 = oc * 8 + pr * 2
            usl = ru_in[h, r0:r0 + 2].rearrange("r k j -> (r k) j")
            G("sync", lambda e, usl=usl, pp=pp: e.dma_start(out=ruL[pp][:], in_=usl),
              writes=[f"ruL{pp}"], dma=True)
            G("gpsimd", lambda g, pp=pp: g.tensor_copy(ruLb[pp][:], ruL[pp][:]),
              reads=[f"ruL{pp}"], writes=[f"ruLb{pp}"])
            tb = tpar[0] % 2; tpar[0] += 1
            G("tensor", lambda t, tb=tb, pp=pp: t.matmul(
                pstp2[tb][0:16, 128:256], ruLb[pp][:], ident_b[:],
                is_transpose=True),
              reads=[f"ruLb{pp}", "ident_b"], writes=[f"pstp{tb}"])
            G("vector", lambda v, tb=tb, pp=pp, pr=pr: v.tensor_copy(
                ruT[pp][:, pr * 128:(pr + 1) * 128], pstp2[tb][0:16, 128:256]),
              reads=[f"pstp{tb}"], writes=[f"ruT{pp}"])

    def rails_comp(h, oc, pp):
        for pr in range(4):
            r0 = oc * 8 + pr * 2
            for q in range(2):
                r = r0 + q
                cs = (pr * 2 + q) % 2
                tbc = tpar[0] % 2
                cps = pstp2[tbc][0:16, 512:1024].bitcast(F32)
                G("tensor", lambda t, pr=pr, q=q, pp=pp, cps=cps: t.matmul(
                    cps[0:16, 0:64],
                    rmT[pp][0:16, (pr * 2 + q) * 16:(pr * 2 + q) * 16 + 16],
                    ruT[pp][0:16, pr * 128 + q * 64:pr * 128 + (q + 1) * 64]),
                  reads=[f"rmT{pp}", f"ruT{pp}"], writes=[f"pstp{tbc}"])
                G("vector", lambda v, cs=cs, cps=cps: v.tensor_copy(
                    cstage[cs][:], cps[0:16, 0:64]),
                  reads=[f"pstp{tbc}"], writes=[f"cstage{cs}"])
                pair, half = r // 2, r % 2
                a, cb = pair % 4, (h * 32 + pair) // 4
                dst = zw[32 * a + 16 * half:32 * a + 16 * half + 16,
                         cb * 128 + half * 64:cb * 128 + half * 64 + 64]
                G("scalar", lambda e, dst=dst, cs=cs: e.dma_start(
                    out=dst, in_=cstage[cs][:]),
                  reads=[f"cstage{cs}", "zw"], writes=["zw"], dma=True)

    def wcT_load(ec, ft, p):
        G("gpsimd", lambda g, ec=ec, ft=ft, p=p: g.dma_start(
            out=wcin[p][:],
            in_=wcp_in[ec * 128:(ec + 1) * 128, ft * 512:(ft + 1) * 512]),
          writes=[f"wcin{p}"], dma=True)
        G("vector", lambda v, p=p: v.tensor_copy(wcb[p][:], wcin[p][:]),
          reads=[f"wcin{p}"], writes=[f"wcb{p}"])

    def wcT_tr(ec, ft, p):
        tb = tpar[0] % 2; tpar[0] += 1
        for kk in range(4):
            G("tensor", lambda t, p=p, kk=kk, tb=tb: t.matmul(
                pstp2[tb][:, kk * 128:(kk + 1) * 128],
                wcb[p][:, kk * 128:(kk + 1) * 128],
                ident_b[:], is_transpose=True),
              reads=[f"wcb{p}", "ident_b"], writes=[f"pstp{tb}"])
        ps = 0
        G("vector", lambda v, ps=ps, tb=tb: v.tensor_copy(wcst[ps][:],
                                                          pstp2[tb][:, 0:512]),
          reads=[f"pstp{tb}"], writes=[f"wcst{ps}"])
        dst = wcT_d.ap()[4 * ft:4 * ft + 4, :, ec * 128:(ec + 1) * 128] \
            .rearrange("k p e -> p k e")
        srcw = wcst[ps][:].rearrange("p (k e) -> p k e", k=4)
        G("gpsimd", lambda e, dst=dst, srcw=srcw: e.dma_start(out=dst, in_=srcw),
          reads=[f"wcst{ps}"], writes=[f"wcT:{ec}:{ft}"], dma=True)

    def wload(pi, par, rtag):
        src = wqkv_d.ap()[pi].rearrange("c p n -> p c n")
        dst = wbuf[par][:].rearrange("p (c n) -> p c n", c=8)
        G("gpsimd", lambda e, src=src, dst=dst: e.dma_start(out=dst, in_=src),
          reads=[f"wqkv:{pi}:{c}" for c in range(8)],
          writes=[rtag], dma=True)

    def proj_mms(par, dest, bcol0, dtag, wtag, i):
        for c in range(8):
            bp = c % 2
            for kc in range(8):
                G("tensor", lambda t, par=par, c=c, kc=kc, bp=bp: t.matmul(
                    psa[:, bp * 512:(bp + 1) * 512],
                    wbuf[par][:, kc * 1024 + c * 128:kc * 1024 + c * 128 + 128],
                    xh1T[:, kc * 512:(kc + 1) * 512],
                    start=(kc == 0), stop=(kc == 7)),
                  reads=[wtag, f"xh1T{kc}"], writes=[f"psa{bp}"])
            extra_r = ["rl_all"] if dtag == "k_tile" and i > 0 else []
            G("vector", lambda v, dest=dest, c=c, bcol0=bcol0, bp=bp:
              v.tensor_scalar(dest[:, c * 512:(c + 1) * 512],
                              psa[:, bp * 512:(bp + 1) * 512],
                              bias_all[:, bcol0 + c:bcol0 + c + 1], None,
                              ALU.add),
              reads=[f"psa{bp}", f"bias{bcol0 + c}"] + extra_r,
              writes=[f"{dtag}:{c}"])

    lnres = [None, None]

    def kvload_early(j):
        sl = j % NSLOT
        G("sync", lambda e, j=j, sl=sl: e.dma_start(
            out=kread[sl][:], in_=k_d.ap()[j]),
          reads=[f"k_d:{j}"], writes=[f"ks{sl}", f"kvl:{j}"], dma=True)
        G("scalar", lambda e, j=j, sl=sl: e.dma_start(
            out=vread[sl][:], in_=v_d.ap()[j]),
          reads=[f"v_d:{j}"], writes=[f"vs{sl}", f"kvl:{j}"], dma=True)

    def kvload_early(j):
        sl = j % NSLOT
        G("sync", lambda e, j=j, sl=sl: e.dma_start(
            out=kread[sl][:], in_=k_d.ap()[j]),
          reads=[f"k_d:{j}"], writes=[f"ks{sl}", f"kvl:{j}"], dma=True)
        G("scalar", lambda e, j=j, sl=sl: e.dma_start(
            out=vread[sl][:], in_=v_d.ap()[j]),
          reads=[f"v_d:{j}"], writes=[f"vs{sl}", f"kvl:{j}"], dma=True)

    def p1a(i):
        wload(0, 0, "wbuf0")
        for s in range(4):
            row0 = i * TT + s * 128
            G("sync", lambda e, row0=row0, s=s: e.dma_start(
                out=x_tile[s % 2][:], in_=x_in[row0:row0 + 128, :]),
              writes=[f"x_tile{s % 2}"], dma=True)
            sp = s % 2
            ag, rs, lntags = ln_stats(0, x_tile[sp][:, 0:512],
                                      x_tile[sp][:, 512:1024], [f"x_tile{sp}"])
            lnres[sp] = (ag, rs, lntags)
            if sp == 1:
                for s2 in (s - 1, s):
                    sp2 = s2 % 2
                    ag, rs, lntags = lnres[sp2]
                    G("vector", lambda v, sp2=sp2, ag=ag, rs=rs: v.tensor_scalar(
                        xh1[sp2][:], x_tile[sp2][:], ag[:, 0:1], rs[:, 0:1],
                        ALU.subtract, ALU.mult),
                      reads=[f"x_tile{sp2}", *lntags], writes=[f"xh1{sp2}"])
                    tb = tpar[0] % 2; tpar[0] += 1
                    for c in range(8):
                        G("tensor", lambda t, sp2=sp2, c=c, tb=tb: t.matmul(
                            pstp2[tb][:, c * 128:(c + 1) * 128],
                            xh1[sp2][:, c * 128:(c + 1) * 128],
                            ident_b[:], is_transpose=True),
                          reads=[f"xh1{sp2}", "ident_b"], writes=[f"pstp{tb}"])
                    for c in range(8):
                        G("vector", lambda v, s2=s2, c=c, tb=tb: v.tensor_copy(
                            xh1T[:, c * 512 + s2 * 128:c * 512 + s2 * 128 + 128],
                            pstp2[tb][:, c * 128:(c + 1) * 128]),
                          reads=[f"pstp{tb}"], writes=[f"xh1T{c}"])
        proj_mms(0, q_sb, 0, "q_sb", "wbuf0", i)
        for j in range(min(JB, 4 * i + 4)):
            if j <= 4 * (i - 1) + 3:
                kvload_early(j)

    def p1b(i):
        wload(1, 1, "wbuf1")
        proj_mms(1, k_tile, 8, "k_tile", "wbuf1", i)
        wload(2, 0, "wbuf0")
        proj_mms(0, v_tile, 16, "v_tile", "wbuf0", i)
        for s in range(4):
            j = 4 * i + s
            srck = k_tile[:].rearrange("p (c n) -> p c n", c=8)[:, :, s * 128:(s + 1) * 128]
            dstk = k_d.ap()[j].rearrange("p (c n) -> p c n", c=8)
            G("scalar", lambda e, dstk=dstk, srck=srck: e.dma_start(
                out=dstk, in_=srck),
              reads=[f"k_tile:{c}" for c in range(8)],
              writes=[f"k_d:{j}"], dma=True)
        for s in range(4):
            j = 4 * i + s
            tb = tpar[0] % 2; tpar[0] += 1
            for c in range(8):
                G("tensor", lambda t, s=s, c=c, tb=tb: t.matmul(
                    pstp2[tb][:, c * 128:(c + 1) * 128],
                    v_tile[:, c * 512 + s * 128:c * 512 + s * 128 + 128],
                    ident_b[:], is_transpose=True),
                  reads=[f"v_tile:{c}", "ident_b"], writes=[f"pstp{tb}"])
            for c in range(8):
                for hh in range(2):
                    hgl = 2 * c + hh
                    G("vector", lambda v, c=c, hh=hh, hgl=hgl, tb=tb: v.tensor_copy(
                        v_stage[0][:, hgl * 65:hgl * 65 + 64],
                        pstp2[tb][:, c * 128 + hh * 64:c * 128 + hh * 64 + 64]),
                      reads=[f"pstp{tb}"], writes=["v_stage0"])
            G("scalar", lambda e, j=j: e.dma_start(
                out=v_d.ap()[j], in_=v_stage[0][:]),
              reads=["v_stage0"], writes=[f"v_d:{j}"], dma=True)
        wload(3, 1, "wbuf1")

    def attn(i):
        njc = 4 * i + 4
        jblocks = [(b0, min(b0 + JB, njc)) for b0 in range(0, njc, JB)]
        nb = len(jblocks)

        def kvload(j):
            sl = j % NSLOT
            G("sync", lambda e, j=j, sl=sl: e.dma_start(
                out=kread[sl][:], in_=k_d.ap()[j]),
              reads=[f"k_d:{j}"], writes=[f"ks{sl}"], dma=True)
            G("scalar", lambda e, j=j, sl=sl: e.dma_start(
                out=vread[sl][:], in_=v_d.ap()[j]),
              reads=[f"v_d:{j}"], writes=[f"vs{sl}"], dma=True)

        hoisted = set(j for j in range(min(JB, njc))
                      if j <= 4 * (i - 1) + 3)
        for bi, (j0, j1) in enumerate(jblocks):
            for j in range(j0, j1):
                if bi == 0 and j in hoisted:
                    continue
                kvload(j)
            def emit_sc(pr_, hh, j, ug):
                chunk = pr_
                hb = hh * 64
                slot = (ug % 2) * 2 + hh
                sl = j % NSLOT
                G("tensor", lambda t, sl=sl, slot=slot, chunk=chunk, hb=hb:
                  t.matmul(psc[:, slot * 512:(slot + 1) * 512],
                           kread[sl][hb:hb + 64, chunk * 128:chunk * 128 + 128],
                           q_sb[hb:hb + 64, chunk * 512:(chunk + 1) * 512],
                           start=True, stop=True),
                  reads=[f"ks{sl}", f"q_sb:{chunk}"],
                  writes=[f"psc{slot}"])

            def emit_exp(pr_, j, ug):
                sb2 = (ug % 2) * 2
                erg = ug % 4
                fence = ([f"k_d:{4 * i + ss}" for ss in range(4)]
                         if bi == 0 and pr_ == 0 and j == j0 else [])
                G("scalar", lambda s, sb2=sb2, erg=erg: s.activation(
                    expt[:, erg * 1024:(erg + 1) * 1024],
                    psc[:, sb2 * 512:(sb2 + 2) * 512],
                    AF.Exp, bias=consts[:, 0:1], scale=0.125),
                  reads=[f"psc{sb2}", f"psc{sb2 + 1}", "c:bexp"] + fence,
                  writes=[f"expt{erg}"])
                dd = j - 4 * i
                if dd >= 0:
                    ecol = erg * 1024
                    G("vector", lambda v, ecol=ecol, dd=dd, erg=erg:
                      v.tensor_tensor(
                          expt[:, ecol:ecol + 1024],
                          expt[:, ecol:ecol + 1024],
                          masks2[:, dd * 1024:(dd + 1) * 1024], ALU.mult),
                      reads=[f"expt{erg}", f"mask2_{dd}"],
                      writes=[f"expt{erg}"])

            def emit_av(pr_, j, ug):
                h0 = 2 * pr_
                erg = ug % 4
                for hh in range(2):
                    h = h0 + hh
                    sl = j % NSLOT
                    ecol = erg * 1024 + hh * 512
                    G("tensor", lambda t, sl=sl, ecol=ecol, h=h, hh=hh,
                      st=(j == j0), sp=(j == j1 - 1): t.matmul(
                          psa[0:65, hh * 512:(hh + 1) * 512],
                          vread[sl][:, h * 65:h * 65 + 65],
                          expt[:, ecol:ecol + 512],
                          start=st, stop=sp),
                      reads=[f"vs{sl}", f"expt{erg}"], writes=[f"psa{hh}"])

            def emit_evac(pr_):
                h0 = 2 * pr_
                chunk = pr_
                for hh in range(2):
                    h, hb = h0 + hh, hh * 64
                    asl = psa[:, hh * 512:(hh + 1) * 512]
                    if bi == 0:
                        G("vector", lambda v, asl=asl, hb=hb, chunk=chunk:
                          v.tensor_copy(
                              ao_acc[hb:hb + 64, chunk * 512:(chunk + 1) * 512],
                              asl[0:64, :]),
                          reads=[f"psa{hh}"], writes=[f"ao:{h}"])
                    else:
                        G("vector", lambda v, asl=asl, hb=hb, chunk=chunk:
                          v.tensor_add(
                              ao_acc[hb:hb + 64, chunk * 512:(chunk + 1) * 512],
                              ao_acc[hb:hb + 64, chunk * 512:(chunk + 1) * 512],
                              asl[0:64, :]),
                          reads=[f"psa{hh}", f"ao:{h}"], writes=[f"ao:{h}"])
                    G("scalar", lambda s, asl=asl: s.copy(ltmp[:], asl[64:65, :]),
                      reads=[f"psa{hh}"], writes=["ltmp"])
                    col = 0 if bi == 0 else TT
                    war = [f"ladd:{bi - 1}"] if bi >= 2 else []
                    G("scalar", lambda e, h=h, bi=bi, col=col: e.dma_start(
                        out=l_parts[h:h + 1, col:col + TT],
                        in_=ltmp[:]),
                      reads=["ltmp"] + war, writes=[f"lp:{h}:{bi}"], dma=True)

            units = [(pr_, j) for pr_ in range(8) for j in range(j0, j1)]
            pend = []
            for pr_, j in units:
                ug = ucnt[0]; ucnt[0] += 1
                emit_sc(pr_, 0, j, ug)
                emit_sc(pr_, 1, j, ug)
                emit_exp(pr_, j, ug)
                pend.append((pr_, j, ug))
                if len(pend) > 3:
                    pu = pend.pop(0)
                    emit_av(*pu)
                    if pu[1] == j1 - 1:
                        emit_evac(pu[0])
                rails_step()
            for pu in pend:
                emit_av(*pu)
                if pu[1] == j1 - 1:
                    emit_evac(pu[0])
            if bi > 0:
                G("vector", lambda v: v.tensor_add(
                    l_parts[:, 0:TT], l_parts[:, 0:TT], l_parts[:, TT:2 * TT]),
                  reads=[f"lp:{h}:{bi}" for h in range(H)] +
                        ([f"lp:{h}:0" for h in range(H)] if bi == 1 else ["lsum"]),
                  writes=["lsum", f"ladd:{bi}"])

        return nb

    def attn_norm(i, nb):
        lsrc = ["lsum"] if nb > 1 else [f"lp:{h}:0" for h in range(H)]
        G("scalar", lambda s: s.activation(rl_all[:], l_parts[:, 0:TT], AF.Ln),
          reads=lsrc, writes=["rlL"])
        G("scalar", lambda s: s.activation(rl_all[:], rl_all[:], AF.Exp, scale=-1.0),
          reads=["rlL"], writes=["rl_all"])
        G("vector", lambda v: v.tensor_copy(rl_bf[:], rl_all[:]),
          reads=["rl_all"], writes=["rl_bf"])
        G("sync", lambda e: e.dma_start(out=rl_d.ap()[:, :], in_=rl_bf[:]),
          reads=["rl_bf"], writes=["rl_dd"], dma=True)
        for chunk in range(8):
            h0, h1, rp = 2 * chunk, 2 * chunk + 1, chunk % 2
            bsrc0 = bass.AP(rl_d, h0 * TT, [[0, 64], [1, TT]])
            bsrc1 = bass.AP(rl_d, h1 * TT, [[0, 64], [1, TT]])
            G("sync", lambda e, bsrc0=bsrc0, rp=rp: e.dma_start(
                out=rlb[rp][0:64, :], in_=bsrc0),
              reads=["rl_dd"], writes=[f"rlb{rp}"], dma=True)
            G("sync", lambda e, bsrc1=bsrc1, rp=rp: e.dma_start(
                out=rlb[rp][64:128, :], in_=bsrc1),
              reads=["rl_dd"], writes=[f"rlb{rp}"], dma=True)
            G("vector", lambda v, chunk=chunk, rp=rp: v.tensor_tensor(
                aoT[:, chunk * 512:(chunk + 1) * 512],
                ao_acc[:, chunk * 512:(chunk + 1) * 512],
                rlb[rp][:], ALU.mult),
              reads=[f"ao:{h0}", f"ao:{h1}", f"rlb{rp}"],
              writes=[f"aoT:{chunk}"])

    def wo_s(i):
        def emit_x3T(s):
            row0 = i * TT + s * 128
            tb = tpar[0] % 2; tpar[0] += 1
            for c in range(8):
                G("tensor", lambda t, s=s, c=c, tb=tb: t.matmul(
                    pstp2[tb][:, c * 128:(c + 1) * 128],
                    x3h[s][:, c * 128:(c + 1) * 128],
                    ident_b[:], is_transpose=True),
                  reads=[f"x3h{s}", "ident_b"], writes=[f"pstp{tb}"])
            G("vector", lambda v, tb=tb: v.tensor_copy(x3st[0][:], pstp2[tb][:]),
              reads=[f"pstp{tb}"], writes=["x3st0"])
            dst = xh3_d.ap()[:, :, row0:row0 + 128].rearrange("c p n -> p c n")
            srcx = x3st[0][:].rearrange("p (c n) -> p c n", c=8)
            G("scalar", lambda e, dst=dst, srcx=srcx: e.dma_start(
                out=dst, in_=srcx),
              reads=["x3st0"], writes=[f"xh3:{row0}"], dma=True)

        for s in range(4):
            sp = s % 2
            sb2 = sp * 2
            row0 = i * TT + s * 128
            G("sync", lambda e, row0=row0, sp=sp: e.dma_start(
                out=xres[sp][:], in_=x_in[row0:row0 + 128, :]),
              writes=[f"xres{sp}"], dma=True)
            for half in range(2):
                slot = sb2 + half
                for kc in range(8):
                    G("tensor", lambda t, s=s, slot=slot, half=half, kc=kc: t.matmul(
                        psc[:, slot * 512:(slot + 1) * 512],
                        aoT[:, kc * 512 + s * 128:kc * 512 + s * 128 + 128],
                        wbuf[1][:, kc * 1024 + half * 512:kc * 1024 + (half + 1) * 512],
                        start=(kc == 0), stop=False),
                      reads=[f"aoT:{kc}", "wbuf1"], writes=[f"psc{slot}"])
                G("tensor", lambda t, slot=slot, half=half: t.matmul(
                    psc[:, slot * 512:(slot + 1) * 512],
                    ones_b[0:1, :], bo_bh[half][:],
                    start=False, stop=True),
                  reads=["ones_b", "bo_b"], writes=[f"psc{slot}"])
            if s > 0:
                emit_x3T(s - 1)
            ag, rs, lntags = ln_stats(0, psc[:, sb2 * 512:sb2 * 512 + 512],
                                      psc[:, sb2 * 512 + 512:sb2 * 512 + 1024],
                                      [f"psc{sb2}", f"psc{sb2 + 1}"])
            G("vector", lambda v, sp=sp, sb2=sb2, ag=ag, rs=rs: v.tensor_scalar(
                x2_sb[sp][:], psc[:, sb2 * 512:sb2 * 512 + 1024],
                ag[:, 0:1], rs[:, 0:1], ALU.subtract, ALU.mult),
              reads=[f"psc{sb2}", f"psc{sb2 + 1}", *lntags], writes=[f"x2sb{sp}"])
            G("vector", lambda v, sp=sp: v.tensor_add(
                x2_sb[sp][:], x2_sb[sp][:], xres[sp][:]),
              reads=[f"x2sb{sp}", f"xres{sp}"], writes=[f"x2sb{sp}"])
            G("scalar", lambda e, row0=row0, sp=sp: e.dma_start(
                out=x2_d.ap()[row0:row0 + 128, :], in_=x2_sb[sp][:]),
              reads=[f"x2sb{sp}"], writes=[f"x2_d:{row0}"], dma=True)
            ag, rs, lntags = ln_stats(1, x2_sb[sp][:, 0:512], x2_sb[sp][:, 512:1024],
                                      [f"x2sb{sp}"])
            G("vector", lambda v, s=s, sp=sp, ag=ag, rs=rs: v.tensor_scalar(
                x3h[s][:], x2_sb[sp][:], ag[:, 0:1], rs[:, 0:1],
                ALU.subtract, ALU.mult),
              reads=[f"x2sb{sp}", *lntags], writes=[f"x3h{s}"])
        emit_x3T(3)

    wct_seq = ([(ec, ft) for ec in range(8) for ft in range(32)]
               if do_prep_wct else [])
    rl_seq = ([(h, oc) for h in range(4) for oc in range(8)] if do_rails else [])
    wct_li, wct_ti = [0], [0]
    rl_li, rl_ti = [0], [0]
    nbg = max(1, len(wct_seq) + len(rl_seq))
    bg_cnt = [0]
    bg_thresh = 700.0 / nbg
    def bg_step_one():
        done_w = wct_ti[0] >= len(wct_seq)
        done_r = rl_ti[0] >= len(rl_seq)
        use_r = (not done_r) and (done_w or (wct_ti[0] % 8 == 7 and rl_ti[0] * 8 < wct_ti[0]))
        if use_r:
            while rl_li[0] < len(rl_seq) and rl_li[0] <= rl_ti[0] + 2:
                rails_prep(*rl_seq[rl_li[0]], rl_li[0] % 3)
                rl_li[0] += 1
            rails_comp(*rl_seq[rl_ti[0]], rl_ti[0] % 3)
            rl_ti[0] += 1
        elif not done_w:
            while wct_li[0] < len(wct_seq) and wct_li[0] <= wct_ti[0] + 2:
                wcT_load(*wct_seq[wct_li[0]], wct_li[0] % 3)
                wct_li[0] += 1
            wcT_tr(*wct_seq[wct_ti[0]], wct_ti[0] % 3)
            wct_ti[0] += 1

    def rails_step():
        bg_cnt[0] += 1
        while bg_cnt[0] >= bg_thresh:
            bg_cnt[0] -= bg_thresh
            bg_step_one()

    def bg_flush():
        while wct_ti[0] < len(wct_seq) or rl_ti[0] < len(rl_seq):
            bg_step_one()

    ucnt = [0]

    p1a(0)
    p1b(0)
    for i in range(ni):
        nb = attn(i)
        if i + 1 < ni:
            p1a(i + 1)
        attn_norm(i, nb)
        wo_s(i)
        if i + 1 < ni:
            p1b(i + 1)
    bg_flush()

    emit_schedule(nc, S, sems, qsems, inc_c, qlane_c, qlane_last)

    S2 = Sched()
    S2.last_w = dict(S.last_w)
    S2.readers = {k: set(v) for k, v in S.readers.items()}
    S = S2
    all_out = []
    if do_ph3:
        G("gpsimd", lambda g: g.memset(ident_f[:], 1.0), writes=["ident_f"])
        G("gpsimd", lambda g: g.affine_select(ident_f[:], ident_f[:], [[1, 128]],
                                              ALU.is_equal, 0.0, base=0,
                                              channel_multiplier=-1),
          reads=["ident_f"], writes=["ident_f"])
    for Q in range(NQ if do_ph3 else 0):
        for c in range(8):
            rows = [f"xh3:{r}" for r in range(Q * 1024, (Q + 1) * 1024, 128)]
            G("sync", lambda e, Q=Q, c=c: e.dma_start(
                out=xh3q[:, c * 1024:(c + 1) * 1024],
                in_=xh3_d.ap()[c, :, Q * 1024:(Q + 1) * 1024]),
              reads=rows, writes=[f"xh3q:{c}"], dma=True)
        for g in range(8):
            for hg in range(2):
                par = (g + hg) % 2
                fc0 = g * 16 + hg * 8
                src = wcT_d.ap()[fc0:fc0 + 8].rearrange("k p n -> p k n")
                dst = wcbuf[par][:].rearrange("p (k n) -> p k n", k=8)
                wct_tags = [f"wcT:{ec}:{ft}" for ec in range(8)
                            for ft in (fc0 // 4, fc0 // 4 + 1)]
                G("sync", lambda e, src=src, dst=dst: e.dma_start(out=dst, in_=src),
                  reads=wct_tags, writes=[f"wcbuf{par}"], dma=True)
            for s2 in range(2):
                hpar = s2
                zslots = (psc[:, 0:512], psc[:, 512:1024],
                          psc[:, 1024:1536], psc[:, 1536:2048])
                ztags = ("psc0", "psc1", "psc2", "psc3")
                for k in range(16):
                    fc = g * 16 + k
                    rr = (g % 2) * 16 + k
                    a, cb, cx = rr % 4, fc // 4, rr // 4
                    G("tensor", lambda t, a=a, cb=cb, cx=cx, s2=s2, k=k: t.matmul(
                        zslots[k % 4],
                        zw[32 * a:32 * a + 32, cb * 128:(cb + 1) * 128],
                        xh3q[32 * a:32 * a + 32,
                             cx * 1024 + s2 * 512:cx * 1024 + (s2 + 1) * 512],
                        start=True, stop=True, tile_position=(32 * a, 0)),
                      reads=["zw", f"xh3q:{cx}"], writes=[ztags[k % 4]])
                    if k % 2 == 1:
                        kb = k - 1
                        G("scalar", lambda s, kb=kb, hpar=hpar: s.activation(
                            h_tl[hpar][:, kb * 512:(kb + 2) * 512],
                            psc[:, (kb % 4) * 512:(kb % 4 + 2) * 512], AF.Gelu),
                          reads=[ztags[kb % 4], ztags[kb % 4 + 1]],
                          writes=[f"h{hpar}:{kb}", f"h{hpar}:{kb + 1}"])
                for e0, e1 in ((0, 2), (2, 4), (4, 6), (6, 8)):
                    for k in range(16):
                        par = (g + k // 8) % 2
                        for ee in range(e0, e1):
                            G("tensor", lambda t, k=k, ee=ee, e0=e0, par=par,
                              hpar=hpar: t.matmul(
                                  psa[:, (ee - e0) * 512:(ee - e0 + 1) * 512],
                                  wcbuf[par][:, (k % 8) * 1024 + ee * 128:
                                             (k % 8) * 1024 + ee * 128 + 128],
                                  h_tl[hpar][:, k * 512:(k + 1) * 512],
                                  start=(k == 0), stop=(k == 15)),
                              reads=[f"wcbuf{par}", f"h{hpar}:{k}"],
                              writes=[f"psa{ee - e0}"])
                    for ee in range(e0, e1):
                        ocol = ee * 1024 + s2 * 512
                        if g == 0:
                            G("vector", lambda v, ee=ee, e0=e0, ocol=ocol:
                              v.tensor_copy(out_q[:, ocol:ocol + 512],
                                            psa[:, (ee - e0) * 512:(ee - e0 + 1) * 512]),
                              reads=[f"psa{ee - e0}"], writes=[f"oq:{ee}:{s2}"])
                        else:
                            G("vector", lambda v, ee=ee, e0=e0, ocol=ocol:
                              v.tensor_add(out_q[:, ocol:ocol + 512],
                                           out_q[:, ocol:ocol + 512],
                                           psa[:, (ee - e0) * 512:(ee - e0 + 1) * 512]),
                              reads=[f"psa{ee - e0}", f"oq:{ee}:{s2}"],
                              writes=[f"oq:{ee}:{s2}"])
        for s in range(8):
            sp, s2 = s % 2, s // 4
            row0 = Q * 1024 + s * 128
            G("sync", lambda e, row0=row0, sp=sp: e.dma_start(
                out=x2r[sp][:], in_=x2_d.ap()[row0:row0 + 128, :]),
              reads=[f"x2_d:{row0}"], writes=[f"x2r{sp}"], dma=True)
            for ee in range(8):
                G("tensor", lambda t, ee=ee, s=s: t.matmul(
                    psa[:, ee * 128:(ee + 1) * 128],
                    out_q[:, ee * 1024 + s * 128:ee * 1024 + s * 128 + 128],
                    ident_f[:], is_transpose=True),
                  reads=[f"oq:{ee}:{s2}", "ident_f"],
                  writes=["psaT", "psa0", "psa1"])
            ag, rs, lntags = ln_stats(0, psa[:, 0:512], psa[:, 512:1024],
                                      ["psaT", "psa0", "psa1"])
            G("vector", lambda v, sp=sp, ag=ag, rs=rs: v.tensor_scalar(
                outst[sp][:], psa[:], ag[:, 0:1], rs[:, 0:1],
                ALU.subtract, ALU.mult),
              reads=["psaT", "psa0", "psa1", *lntags], writes=[f"outst{sp}"])
            G("vector", lambda v, sp=sp: v.tensor_add(
                outst[sp][:], outst[sp][:], x2r[sp][:]),
              reads=[f"outst{sp}", f"x2r{sp}"], writes=[f"outst{sp}"])
            G("scalar", lambda e, row0=row0, sp=sp: e.dma_start(
                out=out_hbm[row0:row0 + 128, :], in_=outst[sp][:]),
              reads=[f"outst{sp}"], writes=[f"outd:{row0}"], dma=True)
            all_out.append(f"outd:{row0}")

    S.add("sync", lambda e: None, reads=all_out)
    emit_schedule(nc, S, sems, qsems, inc_c, qlane_c, qlane_last)
    return nc


_NC_CACHE = None


def kernel(**inputs):
    global _NC_CACHE
    if _NC_CACHE is None:
        _NC_CACHE = build_program()
    nc = _NC_CACHE
    x = np.ascontiguousarray(np.asarray(inputs["L0_attn_x"], dtype=np.float32))
    common = {
        "wq": np.ascontiguousarray(np.asarray(inputs["L0_attn_wq"], np.float32)),
        "wk": np.ascontiguousarray(np.asarray(inputs["L0_attn_wk"], np.float32)),
        "wv": np.ascontiguousarray(np.asarray(inputs["L0_attn_wv"], np.float32)),
        "wo": np.ascontiguousarray(np.asarray(inputs["L0_attn_wo"], np.float32)),
        "bq": np.ascontiguousarray(np.asarray(inputs["L0_attn_bq"], np.float32)),
        "bk": np.ascontiguousarray(np.asarray(inputs["L0_attn_bk"], np.float32)),
        "bv": np.ascontiguousarray(np.asarray(inputs["L0_attn_bv"], np.float32)),
        "bo": np.ascontiguousarray(np.asarray(inputs["L0_attn_bo"], np.float32)),
        "rm": np.ascontiguousarray(np.asarray(inputs["L0_ffn_rail_maker"], np.float32)),
        "ru": np.ascontiguousarray(np.asarray(inputs["L0_ffn_rail_up_proj"], np.float32)),
        "wcp": np.ascontiguousarray(np.asarray(inputs["L0_ffn_w_cproj"], np.float32)),
    }
    in_maps = []
    for c in range(NCORES):
        m = dict(common)
        m["x"] = np.ascontiguousarray(x[c])
        in_maps.append(m)
    res = run_bass_kernel_spmd(nc, in_maps, core_ids=list(range(NCORES)))
    return np.stack([r["out"] for r in res.results], axis=0)


# version 27
# speedup vs baseline: 1.0459x; 1.0459x over previous
import numpy as np
import concourse.bass as bass
import concourse.mybir as mybir
from concourse.bass_utils import run_bass_kernel_spmd

F32 = mybir.dt.float32
BF16 = mybir.dt.bfloat16
AF = mybir.ActivationFunctionType
ALU = mybir.AluOpType

NCORES = 8
T = 4096
D = 1024
H = 16
NT = 8
TT = 512
JB = 8
NSLOT = 8
B_EXP = -20.0
EPS = 1e-5
MULT_F = 16384
NQ = 4

ENG_NAMES = ("sync", "tensor", "vector", "scalar", "gpsimd")


class Op:
    __slots__ = ("idx", "eng", "emit", "deps", "dma", "inc_no", "needed", "rdeps")

    def __init__(self, idx, eng, emit, deps, dma):
        self.idx = idx
        self.eng = eng
        self.emit = emit
        self.deps = deps
        self.dma = dma
        self.inc_no = None
        self.needed = False


class Sched:

    def __init__(self):
        self.ops = []
        self.last_w = {}
        self.readers = {}

    def add(self, eng, emit, reads=(), writes=(), dma=False):
        deps = set()
        for t in reads:
            w = self.last_w.get(t)
            if w is not None:
                deps.add(w)
        for t in writes:
            w = self.last_w.get(t)
            if w is not None:
                deps.add(w)
            deps.update(self.readers.get(t, ()))
        op = Op(len(self.ops), eng, emit, deps, dma)
        self.ops.append(op)
        for t in reads:
            self.readers.setdefault(t, set()).add(op)
        for t in writes:
            self.last_w[t] = op
            self.readers[t] = set()
        return op


NLANE = 12


def emit_schedule(nc, sched, sems, qsems, inc_counters, qlane_counters,
                  qlane_last):
    ops = sched.ops
    for op in ops:
        if op.dma:
            lanes = qlane_counters[op.eng]
            lane = min(range(NLANE), key=lambda s: lanes[s])
            lanes[lane] += 1
            op.inc_no = (lane, lanes[lane])
            prev = qlane_last[op.eng][lane]
            if prev is not None:
                op.deps.add(prev)
            qlane_last[op.eng][lane] = op
    for op in ops:
        rd = [d for d in op.deps
              if not (d.eng == "tensor" and op.eng == "tensor"
                      and not d.dma and not op.dma)]
        best = {}
        keep = []
        for d in rd:
            if d.dma:
                key = ("q", d.eng, d.inc_no[0])
            elif d.eng == "tensor":
                key = ("pe",)
            else:
                keep.append(d)
                continue
            if key not in best or best[key].idx < d.idx:
                best[key] = d
        op.rdeps = keep + list(best.values())
        for d in op.rdeps:
            d.needed = True
    for op in ops:
        if not op.dma and op.needed:
            inc_counters[op.eng] += 1
            op.inc_no = inc_counters[op.eng]

    wait_cache = {e: {} for e in ENG_NAMES}

    def body_for(engname):
        eops = [op for op in ops if op.eng == engname]

        def body(eng):
            cache = wait_cache[engname]
            for op in eops:
                for d in op.rdeps:
                    if d.inc_no is None:
                        continue
                    if d.dma:
                        lane, cnt = d.inc_no
                        sem, val = qsems[d.eng][lane], 16 * cnt
                    else:
                        sem, val = sems[d.eng], d.inc_no
                    key = id(sem)
                    if cache.get(key, -1) >= val:
                        continue
                    eng.wait_ge(sem, val)
                    cache[key] = val
                ins = op.emit(eng)
                if op.inc_no is not None:
                    assert ins is not None, f"op {op.idx} on {engname} needs inc"
                    if op.dma:
                        lane, cnt = op.inc_no
                        ins.then_inc(qsems[op.eng][lane], 16)
                    else:
                        ins.then_inc(sems[op.eng], 1)
        return body

    with nc.Block() as block:
        for engname in ENG_NAMES:
            if any(op.eng == engname for op in ops):
                getattr(block, engname)(body_for(engname))


def build_program(ni=NT, do_attn=True, do_ph3=True, do_prep_wct=True,
                  do_rails=True, p1_gate=127, do_wcast=True, do_bias=True):
    nc = bass.Bass("TRN2", target_bir_lowering=False)

    x_in = nc.dram_tensor("x", [T, D], F32, kind="ExternalInput").ap()
    w_ins = {n: nc.dram_tensor(n, [D, D], F32, kind="ExternalInput").ap()
             for n in ("wq", "wk", "wv", "wo")}
    b_ins = {n: nc.dram_tensor(n, [D], F32, kind="ExternalInput").ap()
             for n in ("bq", "bk", "bv", "bo")}
    rm_in = nc.dram_tensor("rm", [4, 64, 16, 16], F32, kind="ExternalInput").ap()
    ru_in = nc.dram_tensor("ru", [4, 64, 64, 16], F32, kind="ExternalInput").ap()
    wcp_in = nc.dram_tensor("wcp", [D, MULT_F], F32, kind="ExternalInput").ap()
    out_hbm = nc.dram_tensor("out", [T, D], F32, kind="ExternalOutput").ap()

    wqkv_d = nc.dram_tensor("wqkv_d", [4, 8, 128, D], BF16, kind="Internal")
    wcT_d = nc.dram_tensor("wcT_d", [128, 128, D], BF16, kind="Internal")
    k_d = nc.dram_tensor("k_d", [32, 128, D], BF16, kind="Internal")
    v_d = nc.dram_tensor("v_d", [32, 128, H * 65], BF16, kind="Internal")
    x2_d = nc.dram_tensor("x2_d", [T, D], F32, kind="Internal")
    xh3_d = nc.dram_tensor("xh3_d", [8, 128, T], BF16, kind="Internal")
    rl_d = nc.dram_tensor("rl_d", [16, TT], BF16, kind="Internal")

    consts = nc.alloc_sbuf_tensor("consts", [128, 4], F32)
    ident_b = nc.alloc_sbuf_tensor("ident_b", [128, 128], BF16)
    ones_b = nc.alloc_sbuf_tensor("ones_b", [128, 128], BF16)
    masks2 = nc.alloc_sbuf_tensor("masks2", [128, 8 * TT], BF16)
    bias_all = nc.alloc_sbuf_tensor("bias_all", [128, 24], F32)
    bo_bh = [nc.alloc_sbuf_tensor(f"bo_bh{p}", [1, TT], BF16) for p in range(2)]
    statsA = nc.alloc_sbuf_tensor("statsA", [128, 12], F32)
    statsB = nc.alloc_sbuf_tensor("statsB", [128, 12], F32)
    aggrA = nc.alloc_sbuf_tensor("aggrA", [128, 2], F32)
    aggrB = nc.alloc_sbuf_tensor("aggrB", [128, 2], F32)
    rstdA = nc.alloc_sbuf_tensor("rstdA", [128, 1], F32)
    rstdB = nc.alloc_sbuf_tensor("rstdB", [128, 1], F32)
    zw = nc.alloc_sbuf_tensor("zw", [128, 32 * 128], BF16)
    wcin = [nc.alloc_sbuf_tensor(f"wcin{p}", [128, 512], F32) for p in range(3)]
    wcb = [nc.alloc_sbuf_tensor(f"wcb{p}", [128, 512], BF16) for p in range(3)]
    wcst = [nc.alloc_sbuf_tensor(f"wcst{p}", [128, 512], BF16) for p in range(1)]
    statsC = nc.alloc_sbuf_tensor("statsC", [128, 12], F32)
    statsD = nc.alloc_sbuf_tensor("statsD", [128, 12], F32)
    aggrC = nc.alloc_sbuf_tensor("aggrC", [128, 2], F32)
    aggrD = nc.alloc_sbuf_tensor("aggrD", [128, 2], F32)
    rstdC = nc.alloc_sbuf_tensor("rstdC", [128, 1], F32)
    rstdD = nc.alloc_sbuf_tensor("rstdD", [128, 1], F32)
    rmL = [nc.alloc_sbuf_tensor(f"rmL{p}", [128, 16], F32) for p in range(3)]
    rmLb = [nc.alloc_sbuf_tensor(f"rmLb{p}", [128, 16], BF16) for p in range(3)]
    rmT = [nc.alloc_sbuf_tensor(f"rmT{p}", [16, 128], BF16) for p in range(3)]
    ruL = [nc.alloc_sbuf_tensor(f"ruL{p}", [128, 16], F32) for p in range(3)]
    ruLb = [nc.alloc_sbuf_tensor(f"ruLb{p}", [128, 16], BF16) for p in range(3)]
    ruT = [nc.alloc_sbuf_tensor(f"ruT{p}", [16, 512], BF16) for p in range(3)]
    cstage = [nc.alloc_sbuf_tensor(f"cstage{p}", [16, 64], BF16) for p in range(2)]
    ltmp = nc.alloc_sbuf_tensor("ltmp", [1, TT], F32)

    cursor = 229376 - nc.sbuf_bytes_remaining
    base = (cursor + 127) & ~127
    off = [0]

    def at(name, shape, dtype, nbytes):
        t = nc.alloc_sbuf_tensor_at(name, shape, dtype, offset=base + off[0])
        off[0] += (nbytes + 127) & ~127
        return t

    wbuf = [at(f"wbuf{p}", [128, 8 * 1024], BF16, 16384) for p in range(2)]
    q_sb = at("q_sb", [128, 8 * TT], BF16, 8192)
    xh1T = at("xh1T", [128, 8 * TT], BF16, 8192)
    aoT = at("aoT", [128, 8 * TT], BF16, 8192)
    x_tile = [at(f"x_tile{s}", [128, D], F32, 4096) for s in range(2)]
    xres = [at(f"xres{s}", [128, D], F32, 4096) for s in range(2)]
    xh1 = [at(f"xh1{p}", [128, D], BF16, 2048) for p in range(2)]
    k_tile = at("k_tile", [128, 8 * TT], BF16, 8192)
    expt = k_tile
    v_tile = at("v_tile", [128, 8 * TT], BF16, 8192)
    kread = [at(f"kread{p}", [128, D], BF16, 2048) for p in range(NSLOT)]
    vread = [at(f"vread{p}", [128, H * 65], BF16, 2112) for p in range(NSLOT)]
    ao_acc = at("ao_acc", [128, 8 * TT], F32, 16384)
    l_parts = at("l_parts", [16, 2 * TT], F32, 4096)
    rl_all = at("rl_all", [16, TT], F32, 2048)
    rl_bf = at("rl_bf", [16, TT], BF16, 1024)
    v_stage = [at(f"v_stage{p}", [128, H * 65], BF16, 2112) for p in range(1)]
    x2_sb = [at(f"x2_sb{p}", [128, D], F32, 4096) for p in range(2)]
    x3h = [at(f"x3h{p}", [128, D], BF16, 2048) for p in range(4)]
    x3st = [at(f"x3st{p}", [128, 8 * 128], BF16, 2048) for p in range(1)]
    rlb = [at(f"rlb{p}", [128, TT], BF16, 2048) for p in range(2)]
    ph12_end = off[0]

    off[0] = 0
    out_q = at("out_q", [128, 8 * 1024], F32, 32768)
    h_tl = [at(f"h_tl{p}", [128, 16 * TT], BF16, 16384) for p in range(2)]
    wcbuf = [at(f"wcbuf{p}", [128, 8 * 1024], BF16, 16384) for p in range(2)]
    xh3q = at("xh3q", [128, 8 * 1024], BF16, 16384)
    x2r = [at(f"x2r{p}", [128, D], F32, 4096) for p in range(2)]
    ident_f = at("ident_f", [128, 128], F32, 512)
    outst = [at(f"outst{p}", [128, D], F32, 4096) for p in range(2)]
    ph3_end = off[0]
    assert base + max(ph12_end, ph3_end) <= 229376, (base, ph12_end, ph3_end)

    psa = nc.alloc_psum_tensor("psa", [128, 1024], F32)
    psc = nc.alloc_psum_tensor("psc", [128, 2048], F32)
    pstp2 = [nc.alloc_psum_tensor("pstpA", [128, 1024], BF16),
             nc.alloc_psum_tensor("pstpB", [128, 1024], BF16)]
    tpar = [0]

    sems = {e: nc.alloc_semaphore(f"s_{e}") for e in ENG_NAMES}
    qsems = {e: [nc.alloc_semaphore(f"q_{e}{s}") for s in range(NLANE)]
             for e in ("sync", "scalar", "gpsimd")}
    inc_c = {e: 0 for e in ENG_NAMES}
    qlane_c = {e: [0] * NLANE for e in ("sync", "scalar", "gpsimd")}
    qlane_last = {e: [None] * NLANE for e in ("sync", "scalar", "gpsimd")}

    S = Sched()

    def G(eng, emit, reads=(), writes=(), dma=False):
        S.add(eng, emit, reads, writes, dma)

    ln_sets = ((statsA, aggrA, rstdA, "A"), (statsB, aggrB, rstdB, "B"),
               (statsC, aggrC, rstdC, "C"), (statsD, aggrD, rstdD, "D"))
    ln_rot = [0]

    def ln_stats(par, src0, src1, rtags):
        st, ag, rs, p = ln_sets[ln_rot[0] % 4]
        ln_rot[0] += 1
        G("vector", lambda v, a=src0, st=st: v.bn_stats(st[:, 0:6], a),
          reads=rtags, writes=[f"st0{p}"])
        G("vector", lambda v, a=src1, st=st: v.bn_stats(st[:, 6:12], a),
          reads=rtags, writes=[f"st1{p}"])
        G("vector", lambda v, st=st, ag=ag: v.bn_aggr(ag[:], st[:]),
          reads=[f"st0{p}", f"st1{p}"], writes=[f"ag{p}"])
        G("scalar", lambda s, ag=ag, rs=rs: s.activation(
            rs[:], ag[:, 1:2], AF.Ln, bias=consts[:, 1:2]),
          reads=[f"ag{p}", "c:eps"], writes=[f"rsL{p}"])
        G("scalar", lambda s, rs=rs: s.activation(rs[:], rs[:], AF.Exp, scale=-0.5),
          reads=[f"rsL{p}"], writes=[f"rs{p}"])
        return ag, rs, (f"ag{p}", f"rs{p}")

    G("gpsimd", lambda g: g.memset(consts[:, 0:1], B_EXP), writes=["c:bexp"])
    G("gpsimd", lambda g: g.memset(consts[:, 1:2], EPS), writes=["c:eps"])
    G("gpsimd", lambda g: g.memset(ones_b[:], 1.0), writes=["ones_b"])
    G("gpsimd", lambda g: g.affine_select(ident_b[:], ones_b[:], [[1, 128]],
                                          ALU.is_equal, 0.0, base=0,
                                          channel_multiplier=-1),
      reads=["ones_b"], writes=["ident_b"])
    G("gpsimd", lambda g: g.memset(masks2[:], 1.0), writes=["m2init"])
    for d in range(4):
        G("gpsimd", lambda g, d=d: g.affine_select(
            masks2[:, d * 1024:(d + 1) * 1024],
            masks2[:, d * 1024:(d + 1) * 1024], [[0, 2], [1, TT]],
            ALU.is_ge, 0.0, base=-128 * d, channel_multiplier=-1),
          reads=["m2init"], writes=[f"mask2_{d}"])
    G("gpsimd", lambda g: g.memset(zw[:], 0.0), writes=["zw"])
    G("gpsimd", lambda g: g.memset(v_stage[0][:], 1.0), writes=["v_stage0"])

    for pi, bn in enumerate(("bq", "bk", "bv") if do_bias else ()):
        for c in range(8):
            src = b_ins[bn][c * 128:(c + 1) * 128].rearrange("(p o) -> p o", o=1)
            col = pi * 8 + c
            G("gpsimd", lambda g, src=src, col=col: g.dma_start(
                out=bias_all[:, col:col + 1], in_=src),
              writes=[f"bias{col}"], dma=True)
    for hf in range(2):
        G("gpsimd", lambda g, hf=hf: g.dma_start(
            out=ltmp[:], in_=b_ins["bo"][hf * TT:(hf + 1) * TT]
            .rearrange("(o d) -> o d", o=1)),
          writes=["ltmp"], dma=True)
        G("gpsimd", lambda g, hf=hf: g.tensor_copy(bo_bh[hf][:], ltmp[:]),
          reads=["ltmp"], writes=["bo_b"])

    for pi, wn in enumerate(("wq", "wk", "wv", "wo") if do_wcast else ()):
        for c in range(8):
            for hf in range(2):
                p = hf
                G("gpsimd", lambda g, wn=wn, c=c, hf=hf, p=p: g.dma_start(
                    out=wcin[p][:],
                    in_=w_ins[wn][c * 128:(c + 1) * 128, hf * 512:(hf + 1) * 512]),
                  writes=[f"wcin{p}"], dma=True)
                G("vector", lambda v, p=p: v.tensor_copy(wcb[p][:], wcin[p][:]),
                  reads=[f"wcin{p}"], writes=[f"wcb{p}"])
                G("gpsimd", lambda g, pi=pi, c=c, hf=hf, p=p: g.dma_start(
                    out=wqkv_d.ap()[pi, c, :, hf * 512:(hf + 1) * 512],
                    in_=wcb[p][:]),
                  reads=[f"wcb{p}"], writes=[f"wqkv:{pi}:{c}"], dma=True)

    def rails_prep(h, oc, pp):
        rsl = rm_in[h, oc * 8:(oc + 1) * 8].rearrange("r i j -> (r i) j")
        G("sync", lambda e, rsl=rsl, pp=pp: e.dma_start(out=rmL[pp][:], in_=rsl),
          writes=[f"rmL{pp}"], dma=True)
        G("gpsimd", lambda g, pp=pp: g.tensor_copy(rmLb[pp][:], rmL[pp][:]),
          reads=[f"rmL{pp}"], writes=[f"rmLb{pp}"])
        tb = tpar[0] % 2; tpar[0] += 1
        G("tensor", lambda t, tb=tb, pp=pp: t.matmul(
            pstp2[tb][0:16, 0:128], rmLb[pp][:], ident_b[:], is_transpose=True),
          reads=[f"rmLb{pp}", "ident_b"], writes=[f"pstp{tb}"])
        G("vector", lambda v, tb=tb, pp=pp: v.tensor_copy(
            rmT[pp][:], pstp2[tb][0:16, 0:128]),
          reads=[f"pstp{tb}"], writes=[f"rmT{pp}"])
        for pr in range(4):
            r0 = oc * 8 + pr * 2
            usl = ru_in[h, r0:r0 + 2].rearrange("r k j -> (r k) j")
            G("sync", lambda e, usl=usl, pp=pp: e.dma_start(out=ruL[pp][:], in_=usl),
              writes=[f"ruL{pp}"], dma=True)
            G("gpsimd", lambda g, pp=pp: g.tensor_copy(ruLb[pp][:], ruL[pp][:]),
              reads=[f"ruL{pp}"], writes=[f"ruLb{pp}"])
            tb = tpar[0] % 2; tpar[0] += 1
            G("tensor", lambda t, tb=tb, pp=pp: t.matmul(
                pstp2[tb][0:16, 128:256], ruLb[pp][:], ident_b[:],
                is_transpose=True),
              reads=[f"ruLb{pp}", "ident_b"], writes=[f"pstp{tb}"])
            G("vector", lambda v, tb=tb, pp=pp, pr=pr: v.tensor_copy(
                ruT[pp][:, pr * 128:(pr + 1) * 128], pstp2[tb][0:16, 128:256]),
              reads=[f"pstp{tb}"], writes=[f"ruT{pp}"])

    def rails_comp(h, oc, pp):
        for pr in range(4):
            r0 = oc * 8 + pr * 2
            for q in range(2):
                r = r0 + q
                cs = (pr * 2 + q) % 2
                tbc = tpar[0] % 2
                cps = pstp2[tbc][0:16, 512:1024].bitcast(F32)
                G("tensor", lambda t, pr=pr, q=q, pp=pp, cps=cps: t.matmul(
                    cps[0:16, 0:64],
                    rmT[pp][0:16, (pr * 2 + q) * 16:(pr * 2 + q) * 16 + 16],
                    ruT[pp][0:16, pr * 128 + q * 64:pr * 128 + (q + 1) * 64]),
                  reads=[f"rmT{pp}", f"ruT{pp}"], writes=[f"pstp{tbc}"])
                G("vector", lambda v, cs=cs, cps=cps: v.tensor_copy(
                    cstage[cs][:], cps[0:16, 0:64]),
                  reads=[f"pstp{tbc}"], writes=[f"cstage{cs}"])
                pair, half = r // 2, r % 2
                a, cb = pair % 4, (h * 32 + pair) // 4
                dst = zw[32 * a + 16 * half:32 * a + 16 * half + 16,
                         cb * 128 + half * 64:cb * 128 + half * 64 + 64]
                G("scalar", lambda e, dst=dst, cs=cs: e.dma_start(
                    out=dst, in_=cstage[cs][:]),
                  reads=[f"cstage{cs}", "zw"], writes=["zw"], dma=True)

    def wcT_load(ec, ft, p):
        G("gpsimd", lambda g, ec=ec, ft=ft, p=p: g.dma_start(
            out=wcin[p][:],
            in_=wcp_in[ec * 128:(ec + 1) * 128, ft * 512:(ft + 1) * 512]),
          writes=[f"wcin{p}"], dma=True)
        G("vector", lambda v, p=p: v.tensor_copy(wcb[p][:], wcin[p][:]),
          reads=[f"wcin{p}"], writes=[f"wcb{p}"])

    def wcT_tr(ec, ft, p):
        tb = tpar[0] % 2; tpar[0] += 1
        for kk in range(4):
            G("tensor", lambda t, p=p, kk=kk, tb=tb: t.matmul(
                pstp2[tb][:, kk * 128:(kk + 1) * 128],
                wcb[p][:, kk * 128:(kk + 1) * 128],
                ident_b[:], is_transpose=True),
              reads=[f"wcb{p}", "ident_b"], writes=[f"pstp{tb}"])
        ps = 0
        G("vector", lambda v, ps=ps, tb=tb: v.tensor_copy(wcst[ps][:],
                                                          pstp2[tb][:, 0:512]),
          reads=[f"pstp{tb}"], writes=[f"wcst{ps}"])
        dst = wcT_d.ap()[4 * ft:4 * ft + 4, :, ec * 128:(ec + 1) * 128] \
            .rearrange("k p e -> p k e")
        srcw = wcst[ps][:].rearrange("p (k e) -> p k e", k=4)
        G("gpsimd", lambda e, dst=dst, srcw=srcw: e.dma_start(out=dst, in_=srcw),
          reads=[f"wcst{ps}"], writes=[f"wcT:{ec}:{ft}"], dma=True)

    def wload(pi, par, rtag):
        src = wqkv_d.ap()[pi].rearrange("c p n -> p c n")
        dst = wbuf[par][:].rearrange("p (c n) -> p c n", c=8)
        G("gpsimd", lambda e, src=src, dst=dst: e.dma_start(out=dst, in_=src),
          reads=[f"wqkv:{pi}:{c}" for c in range(8)],
          writes=[rtag], dma=True)

    def proj_mms(par, dest, bcol0, dtag, wtag, i):
        for c in range(8):
            bp = c % 2
            for kc in range(8):
                G("tensor", lambda t, par=par, c=c, kc=kc, bp=bp: t.matmul(
                    psa[:, bp * 512:(bp + 1) * 512],
                    wbuf[par][:, kc * 1024 + c * 128:kc * 1024 + c * 128 + 128],
                    xh1T[:, kc * 512:(kc + 1) * 512],
                    start=(kc == 0), stop=(kc == 7)),
                  reads=[wtag, f"xh1T{kc}"], writes=[f"psa{bp}"])
            extra_r = ["rl_all"] if dtag == "k_tile" and i > 0 else []
            G("vector", lambda v, dest=dest, c=c, bcol0=bcol0, bp=bp:
              v.tensor_scalar(dest[:, c * 512:(c + 1) * 512],
                              psa[:, bp * 512:(bp + 1) * 512],
                              bias_all[:, bcol0 + c:bcol0 + c + 1], None,
                              ALU.add),
              reads=[f"psa{bp}", f"bias{bcol0 + c}"] + extra_r,
              writes=[f"{dtag}:{c}"])

    lnres = [None, None]

    def kvload_early(j):
        sl = j % NSLOT
        G("sync", lambda e, j=j, sl=sl: e.dma_start(
            out=kread[sl][:], in_=k_d.ap()[j]),
          reads=[f"k_d:{j}"], writes=[f"ks{sl}", f"kvl:{j}"], dma=True)
        G("scalar", lambda e, j=j, sl=sl: e.dma_start(
            out=vread[sl][:], in_=v_d.ap()[j]),
          reads=[f"v_d:{j}"], writes=[f"vs{sl}", f"kvl:{j}"], dma=True)

    def kvload_early(j):
        sl = j % NSLOT
        G("sync", lambda e, j=j, sl=sl: e.dma_start(
            out=kread[sl][:], in_=k_d.ap()[j]),
          reads=[f"k_d:{j}"], writes=[f"ks{sl}", f"kvl:{j}"], dma=True)
        G("scalar", lambda e, j=j, sl=sl: e.dma_start(
            out=vread[sl][:], in_=v_d.ap()[j]),
          reads=[f"v_d:{j}"], writes=[f"vs{sl}", f"kvl:{j}"], dma=True)

    def p1a(i):
        wload(0, 0, "wbuf0")
        for s in range(4):
            row0 = i * TT + s * 128
            G("sync", lambda e, row0=row0, s=s: e.dma_start(
                out=x_tile[s % 2][:], in_=x_in[row0:row0 + 128, :]),
              writes=[f"x_tile{s % 2}"], dma=True)
            sp = s % 2
            ag, rs, lntags = ln_stats(0, x_tile[sp][:, 0:512],
                                      x_tile[sp][:, 512:1024], [f"x_tile{sp}"])
            lnres[sp] = (ag, rs, lntags)
            if sp == 1:
                for s2 in (s - 1, s):
                    sp2 = s2 % 2
                    ag, rs, lntags = lnres[sp2]
                    G("vector", lambda v, sp2=sp2, ag=ag, rs=rs: v.tensor_scalar(
                        xh1[sp2][:], x_tile[sp2][:], ag[:, 0:1], rs[:, 0:1],
                        ALU.subtract, ALU.mult),
                      reads=[f"x_tile{sp2}", *lntags], writes=[f"xh1{sp2}"])
                    tb = tpar[0] % 2; tpar[0] += 1
                    for c in range(8):
                        G("tensor", lambda t, sp2=sp2, c=c, tb=tb: t.matmul(
                            pstp2[tb][:, c * 128:(c + 1) * 128],
                            xh1[sp2][:, c * 128:(c + 1) * 128],
                            ident_b[:], is_transpose=True),
                          reads=[f"xh1{sp2}", "ident_b"], writes=[f"pstp{tb}"])
                    for c in range(8):
                        G("vector", lambda v, s2=s2, c=c, tb=tb: v.tensor_copy(
                            xh1T[:, c * 512 + s2 * 128:c * 512 + s2 * 128 + 128],
                            pstp2[tb][:, c * 128:(c + 1) * 128]),
                          reads=[f"pstp{tb}"], writes=[f"xh1T{c}"])
        proj_mms(0, q_sb, 0, "q_sb", "wbuf0", i)
        for j in range(min(JB, 4 * i + 4)):
            if j <= 4 * (i - 1) + 3:
                kvload_early(j)

    def p1b(i):
        wload(1, 1, "wbuf1")
        proj_mms(1, k_tile, 8, "k_tile", "wbuf1", i)
        wload(2, 0, "wbuf0")
        proj_mms(0, v_tile, 16, "v_tile", "wbuf0", i)
        for s in range(4):
            j = 4 * i + s
            srck = k_tile[:].rearrange("p (c n) -> p c n", c=8)[:, :, s * 128:(s + 1) * 128]
            dstk = k_d.ap()[j].rearrange("p (c n) -> p c n", c=8)
            G("scalar", lambda e, dstk=dstk, srck=srck: e.dma_start(
                out=dstk, in_=srck),
              reads=[f"k_tile:{c}" for c in range(8)],
              writes=[f"k_d:{j}"], dma=True)
        for s in range(4):
            j = 4 * i + s
            tb = tpar[0] % 2; tpar[0] += 1
            for c in range(8):
                G("tensor", lambda t, s=s, c=c, tb=tb: t.matmul(
                    pstp2[tb][:, c * 128:(c + 1) * 128],
                    v_tile[:, c * 512 + s * 128:c * 512 + s * 128 + 128],
                    ident_b[:], is_transpose=True),
                  reads=[f"v_tile:{c}", "ident_b"], writes=[f"pstp{tb}"])
            for c in range(8):
                for hh in range(2):
                    hgl = 2 * c + hh
                    G("vector", lambda v, c=c, hh=hh, hgl=hgl, tb=tb: v.tensor_copy(
                        v_stage[0][:, hgl * 65:hgl * 65 + 64],
                        pstp2[tb][:, c * 128 + hh * 64:c * 128 + hh * 64 + 64]),
                      reads=[f"pstp{tb}"], writes=["v_stage0"])
            G("scalar", lambda e, j=j: e.dma_start(
                out=v_d.ap()[j], in_=v_stage[0][:]),
              reads=["v_stage0"], writes=[f"v_d:{j}"], dma=True)
        wload(3, 1, "wbuf1")

    def attn(i):
        njc = 4 * i + 4
        jblocks = [(b0, min(b0 + JB, njc)) for b0 in range(0, njc, JB)]
        nb = len(jblocks)

        def kvload(j):
            sl = j % NSLOT
            G("sync", lambda e, j=j, sl=sl: e.dma_start(
                out=kread[sl][:], in_=k_d.ap()[j]),
              reads=[f"k_d:{j}"], writes=[f"ks{sl}"], dma=True)
            G("scalar", lambda e, j=j, sl=sl: e.dma_start(
                out=vread[sl][:], in_=v_d.ap()[j]),
              reads=[f"v_d:{j}"], writes=[f"vs{sl}"], dma=True)

        hoisted = set(j for j in range(min(JB, njc))
                      if j <= 4 * (i - 1) + 3)
        for bi, (j0, j1) in enumerate(jblocks):
            for j in range(j0, j1):
                if bi == 0 and j in hoisted:
                    continue
                kvload(j)
            def emit_sc(pr_, hh, j, ug):
                chunk = pr_
                hb = hh * 64
                slot = (ug % 2) * 2 + hh
                sl = j % NSLOT
                G("tensor", lambda t, sl=sl, slot=slot, chunk=chunk, hb=hb:
                  t.matmul(psc[:, slot * 512:(slot + 1) * 512],
                           kread[sl][hb:hb + 64, chunk * 128:chunk * 128 + 128],
                           q_sb[hb:hb + 64, chunk * 512:(chunk + 1) * 512],
                           start=True, stop=True),
                  reads=[f"ks{sl}", f"q_sb:{chunk}"],
                  writes=[f"psc{slot}"])

            def emit_exp(pr_, j, ug):
                sb2 = (ug % 2) * 2
                erg = ug % 4
                fence = ([f"k_d:{4 * i + ss}" for ss in range(4)]
                         if bi == 0 and pr_ == 0 and j == j0 else [])
                G("scalar", lambda s, sb2=sb2, erg=erg: s.activation(
                    expt[:, erg * 1024:(erg + 1) * 1024],
                    psc[:, sb2 * 512:(sb2 + 2) * 512],
                    AF.Exp, bias=consts[:, 0:1], scale=0.125),
                  reads=[f"psc{sb2}", f"psc{sb2 + 1}", "c:bexp"] + fence,
                  writes=[f"expt{erg}"])
                dd = j - 4 * i
                if dd >= 0:
                    ecol = erg * 1024
                    G("vector", lambda v, ecol=ecol, dd=dd, erg=erg:
                      v.tensor_tensor(
                          expt[:, ecol:ecol + 1024],
                          expt[:, ecol:ecol + 1024],
                          masks2[:, dd * 1024:(dd + 1) * 1024], ALU.mult),
                      reads=[f"expt{erg}", f"mask2_{dd}"],
                      writes=[f"expt{erg}"])

            def emit_av(pr_, j, ug):
                h0 = 2 * pr_
                erg = ug % 4
                for hh in range(2):
                    h = h0 + hh
                    sl = j % NSLOT
                    ecol = erg * 1024 + hh * 512
                    G("tensor", lambda t, sl=sl, ecol=ecol, h=h, hh=hh,
                      st=(j == j0), sp=(j == j1 - 1): t.matmul(
                          psa[0:65, hh * 512:(hh + 1) * 512],
                          vread[sl][:, h * 65:h * 65 + 65],
                          expt[:, ecol:ecol + 512],
                          start=st, stop=sp),
                      reads=[f"vs{sl}", f"expt{erg}"], writes=[f"psa{hh}"])

            def emit_evac(pr_):
                h0 = 2 * pr_
                chunk = pr_
                for hh in range(2):
                    h, hb = h0 + hh, hh * 64
                    asl = psa[:, hh * 512:(hh + 1) * 512]
                    if bi == 0:
                        G("vector", lambda v, asl=asl, hb=hb, chunk=chunk:
                          v.tensor_copy(
                              ao_acc[hb:hb + 64, chunk * 512:(chunk + 1) * 512],
                              asl[0:64, :]),
                          reads=[f"psa{hh}"], writes=[f"ao:{h}"])
                    else:
                        G("vector", lambda v, asl=asl, hb=hb, chunk=chunk:
                          v.tensor_add(
                              ao_acc[hb:hb + 64, chunk * 512:(chunk + 1) * 512],
                              ao_acc[hb:hb + 64, chunk * 512:(chunk + 1) * 512],
                              asl[0:64, :]),
                          reads=[f"psa{hh}", f"ao:{h}"], writes=[f"ao:{h}"])
                    G("scalar", lambda s, asl=asl: s.copy(ltmp[:], asl[64:65, :]),
                      reads=[f"psa{hh}"], writes=["ltmp"])
                    col = 0 if bi == 0 else TT
                    war = [f"ladd:{bi - 1}"] if bi >= 2 else []
                    G("scalar", lambda e, h=h, bi=bi, col=col: e.dma_start(
                        out=l_parts[h:h + 1, col:col + TT],
                        in_=ltmp[:]),
                      reads=["ltmp"] + war, writes=[f"lp:{h}:{bi}"], dma=True)

            units = [(pr_, j) for pr_ in range(8) for j in range(j0, j1)]
            pend = []
            for pr_, j in units:
                ug = ucnt[0]; ucnt[0] += 1
                emit_sc(pr_, 0, j, ug)
                emit_sc(pr_, 1, j, ug)
                emit_exp(pr_, j, ug)
                pend.append((pr_, j, ug))
                if len(pend) > 3:
                    pu = pend.pop(0)
                    emit_av(*pu)
                    if pu[1] == j1 - 1:
                        emit_evac(pu[0])
                rails_step()
            for pu in pend:
                emit_av(*pu)
                if pu[1] == j1 - 1:
                    emit_evac(pu[0])
            if bi > 0:
                G("vector", lambda v: v.tensor_add(
                    l_parts[:, 0:TT], l_parts[:, 0:TT], l_parts[:, TT:2 * TT]),
                  reads=[f"lp:{h}:{bi}" for h in range(H)] +
                        ([f"lp:{h}:0" for h in range(H)] if bi == 1 else ["lsum"]),
                  writes=["lsum", f"ladd:{bi}"])

        return nb

    def attn_norm(i, nb):
        lsrc = ["lsum"] if nb > 1 else [f"lp:{h}:0" for h in range(H)]
        G("scalar", lambda s: s.activation(rl_all[:], l_parts[:, 0:TT], AF.Ln),
          reads=lsrc, writes=["rlL"])
        G("scalar", lambda s: s.activation(rl_all[:], rl_all[:], AF.Exp, scale=-1.0),
          reads=["rlL"], writes=["rl_all"])
        G("vector", lambda v: v.tensor_copy(rl_bf[:], rl_all[:]),
          reads=["rl_all"], writes=["rl_bf"])
        G("sync", lambda e: e.dma_start(out=rl_d.ap()[:, :], in_=rl_bf[:]),
          reads=["rl_bf"], writes=["rl_dd"], dma=True)
        for chunk in range(8):
            h0, h1, rp = 2 * chunk, 2 * chunk + 1, chunk % 2
            bsrc0 = bass.AP(rl_d, h0 * TT, [[0, 64], [1, TT]])
            bsrc1 = bass.AP(rl_d, h1 * TT, [[0, 64], [1, TT]])
            G("sync", lambda e, bsrc0=bsrc0, rp=rp: e.dma_start(
                out=rlb[rp][0:64, :], in_=bsrc0),
              reads=["rl_dd"], writes=[f"rlb{rp}"], dma=True)
            G("sync", lambda e, bsrc1=bsrc1, rp=rp: e.dma_start(
                out=rlb[rp][64:128, :], in_=bsrc1),
              reads=["rl_dd"], writes=[f"rlb{rp}"], dma=True)
            G("vector", lambda v, chunk=chunk, rp=rp: v.tensor_tensor(
                aoT[:, chunk * 512:(chunk + 1) * 512],
                ao_acc[:, chunk * 512:(chunk + 1) * 512],
                rlb[rp][:], ALU.mult),
              reads=[f"ao:{h0}", f"ao:{h1}", f"rlb{rp}"],
              writes=[f"aoT:{chunk}"])

    def wo_s(i):
        def emit_x3T(s):
            row0 = i * TT + s * 128
            tb = tpar[0] % 2; tpar[0] += 1
            for c in range(8):
                G("tensor", lambda t, s=s, c=c, tb=tb: t.matmul(
                    pstp2[tb][:, c * 128:(c + 1) * 128],
                    x3h[s][:, c * 128:(c + 1) * 128],
                    ident_b[:], is_transpose=True),
                  reads=[f"x3h{s}", "ident_b"], writes=[f"pstp{tb}"])
            G("vector", lambda v, tb=tb: v.tensor_copy(x3st[0][:], pstp2[tb][:]),
              reads=[f"pstp{tb}"], writes=["x3st0"])
            dst = xh3_d.ap()[:, :, row0:row0 + 128].rearrange("c p n -> p c n")
            srcx = x3st[0][:].rearrange("p (c n) -> p c n", c=8)
            G("scalar", lambda e, dst=dst, srcx=srcx: e.dma_start(
                out=dst, in_=srcx),
              reads=["x3st0"], writes=[f"xh3:{row0}"], dma=True)

        for s in range(4):
            sp = s % 2
            sb2 = sp * 2
            row0 = i * TT + s * 128
            G("sync", lambda e, row0=row0, sp=sp: e.dma_start(
                out=xres[sp][:], in_=x_in[row0:row0 + 128, :]),
              writes=[f"xres{sp}"], dma=True)
            for half in range(2):
                slot = sb2 + half
                for kc in range(8):
                    G("tensor", lambda t, s=s, slot=slot, half=half, kc=kc: t.matmul(
                        psc[:, slot * 512:(slot + 1) * 512],
                        aoT[:, kc * 512 + s * 128:kc * 512 + s * 128 + 128],
                        wbuf[1][:, kc * 1024 + half * 512:kc * 1024 + (half + 1) * 512],
                        start=(kc == 0), stop=False),
                      reads=[f"aoT:{kc}", "wbuf1"], writes=[f"psc{slot}"])
                G("tensor", lambda t, slot=slot, half=half: t.matmul(
                    psc[:, slot * 512:(slot + 1) * 512],
                    ones_b[0:1, :], bo_bh[half][:],
                    start=False, stop=True),
                  reads=["ones_b", "bo_b"], writes=[f"psc{slot}"])
            if s > 0:
                emit_x3T(s - 1)
            ag, rs, lntags = ln_stats(0, psc[:, sb2 * 512:sb2 * 512 + 512],
                                      psc[:, sb2 * 512 + 512:sb2 * 512 + 1024],
                                      [f"psc{sb2}", f"psc{sb2 + 1}"])
            G("vector", lambda v, sp=sp, sb2=sb2, ag=ag, rs=rs: v.tensor_scalar(
                x2_sb[sp][:], psc[:, sb2 * 512:sb2 * 512 + 1024],
                ag[:, 0:1], rs[:, 0:1], ALU.subtract, ALU.mult),
              reads=[f"psc{sb2}", f"psc{sb2 + 1}", *lntags], writes=[f"x2sb{sp}"])
            G("vector", lambda v, sp=sp: v.tensor_add(
                x2_sb[sp][:], x2_sb[sp][:], xres[sp][:]),
              reads=[f"x2sb{sp}", f"xres{sp}"], writes=[f"x2sb{sp}"])
            G("scalar", lambda e, row0=row0, sp=sp: e.dma_start(
                out=x2_d.ap()[row0:row0 + 128, :], in_=x2_sb[sp][:]),
              reads=[f"x2sb{sp}"], writes=[f"x2_d:{row0}"], dma=True)
            ag, rs, lntags = ln_stats(1, x2_sb[sp][:, 0:512], x2_sb[sp][:, 512:1024],
                                      [f"x2sb{sp}"])
            G("vector", lambda v, s=s, sp=sp, ag=ag, rs=rs: v.tensor_scalar(
                x3h[s][:], x2_sb[sp][:], ag[:, 0:1], rs[:, 0:1],
                ALU.subtract, ALU.mult),
              reads=[f"x2sb{sp}", *lntags], writes=[f"x3h{s}"])
        emit_x3T(3)

    wct_seq = ([(ec, ft) for ec in range(8) for ft in range(32)]
               if do_prep_wct else [])
    rl_seq = ([(h, oc) for h in range(4) for oc in range(8)] if do_rails else [])
    wct_li, wct_ti = [0], [0]
    rl_li, rl_ti = [0], [0]
    nbg = max(1, len(wct_seq) + len(rl_seq))
    bg_cnt = [0]
    bg_thresh = 1100.0 / nbg
    def bg_step_one():
        done_w = wct_ti[0] >= len(wct_seq)
        done_r = rl_ti[0] >= len(rl_seq)
        use_r = (not done_r) and (done_w or (wct_ti[0] % 8 == 7 and rl_ti[0] * 8 < wct_ti[0]))
        if use_r:
            while rl_li[0] < len(rl_seq) and rl_li[0] <= rl_ti[0] + 2:
                rails_prep(*rl_seq[rl_li[0]], rl_li[0] % 3)
                rl_li[0] += 1
            rails_comp(*rl_seq[rl_ti[0]], rl_ti[0] % 3)
            rl_ti[0] += 1
        elif not done_w:
            while wct_li[0] < len(wct_seq) and wct_li[0] <= wct_ti[0] + 2:
                wcT_load(*wct_seq[wct_li[0]], wct_li[0] % 3)
                wct_li[0] += 1
            wcT_tr(*wct_seq[wct_ti[0]], wct_ti[0] % 3)
            wct_ti[0] += 1

    def rails_step():
        bg_cnt[0] += 1
        while bg_cnt[0] >= bg_thresh:
            bg_cnt[0] -= bg_thresh
            bg_step_one()

    def bg_flush():
        while wct_ti[0] < len(wct_seq) or rl_ti[0] < len(rl_seq):
            bg_step_one()

    ucnt = [0]

    p1a(0)
    p1b(0)
    for i in range(ni):
        nb = attn(i)
        if i + 1 < ni:
            p1a(i + 1)
        attn_norm(i, nb)
        wo_s(i)
        if i + 1 < ni:
            p1b(i + 1)
    bg_flush()

    emit_schedule(nc, S, sems, qsems, inc_c, qlane_c, qlane_last)

    S2 = Sched()
    S2.last_w = dict(S.last_w)
    S2.readers = {k: set(v) for k, v in S.readers.items()}
    S = S2
    all_out = []
    if do_ph3:
        G("gpsimd", lambda g: g.memset(ident_f[:], 1.0), writes=["ident_f"])
        G("gpsimd", lambda g: g.affine_select(ident_f[:], ident_f[:], [[1, 128]],
                                              ALU.is_equal, 0.0, base=0,
                                              channel_multiplier=-1),
          reads=["ident_f"], writes=["ident_f"])
    for Q in range(NQ if do_ph3 else 0):
        for c in range(8):
            rows = [f"xh3:{r}" for r in range(Q * 1024, (Q + 1) * 1024, 128)]
            G("sync", lambda e, Q=Q, c=c: e.dma_start(
                out=xh3q[:, c * 1024:(c + 1) * 1024],
                in_=xh3_d.ap()[c, :, Q * 1024:(Q + 1) * 1024]),
              reads=rows, writes=[f"xh3q:{c}"], dma=True)
        for g in range(8):
            for hg in range(2):
                par = (g + hg) % 2
                fc0 = g * 16 + hg * 8
                src = wcT_d.ap()[fc0:fc0 + 8].rearrange("k p n -> p k n")
                dst = wcbuf[par][:].rearrange("p (k n) -> p k n", k=8)
                wct_tags = [f"wcT:{ec}:{ft}" for ec in range(8)
                            for ft in (fc0 // 4, fc0 // 4 + 1)]
                G("sync", lambda e, src=src, dst=dst: e.dma_start(out=dst, in_=src),
                  reads=wct_tags, writes=[f"wcbuf{par}"], dma=True)
            for s2 in range(2):
                hpar = s2
                zslots = (psc[:, 0:512], psc[:, 512:1024],
                          psc[:, 1024:1536], psc[:, 1536:2048])
                ztags = ("psc0", "psc1", "psc2", "psc3")
                for k in range(16):
                    fc = g * 16 + k
                    rr = (g % 2) * 16 + k
                    a, cb, cx = rr % 4, fc // 4, rr // 4
                    G("tensor", lambda t, a=a, cb=cb, cx=cx, s2=s2, k=k: t.matmul(
                        zslots[k % 4],
                        zw[32 * a:32 * a + 32, cb * 128:(cb + 1) * 128],
                        xh3q[32 * a:32 * a + 32,
                             cx * 1024 + s2 * 512:cx * 1024 + (s2 + 1) * 512],
                        start=True, stop=True, tile_position=(32 * a, 0)),
                      reads=["zw", f"xh3q:{cx}"], writes=[ztags[k % 4]])
                    if k % 2 == 1:
                        kb = k - 1
                        G("scalar", lambda s, kb=kb, hpar=hpar: s.activation(
                            h_tl[hpar][:, kb * 512:(kb + 2) * 512],
                            psc[:, (kb % 4) * 512:(kb % 4 + 2) * 512], AF.Gelu),
                          reads=[ztags[kb % 4], ztags[kb % 4 + 1]],
                          writes=[f"h{hpar}:{kb}", f"h{hpar}:{kb + 1}"])
                for ee in range(8):
                    bk = ee % 2
                    for k in range(16):
                        par = (g + k // 8) % 2
                        G("tensor", lambda t, k=k, ee=ee, bk=bk, par=par,
                          hpar=hpar: t.matmul(
                              psa[:, bk * 512:(bk + 1) * 512],
                              wcbuf[par][:, (k % 8) * 1024 + ee * 128:
                                         (k % 8) * 1024 + ee * 128 + 128],
                              h_tl[hpar][:, k * 512:(k + 1) * 512],
                              start=(k == 0), stop=(k == 15)),
                          reads=[f"wcbuf{par}", f"h{hpar}:{k}"],
                          writes=[f"psa{bk}"])
                    ocol = ee * 1024 + s2 * 512
                    if g == 0:
                        G("vector", lambda v, ee=ee, bk=bk, ocol=ocol:
                          v.tensor_copy(out_q[:, ocol:ocol + 512],
                                        psa[:, bk * 512:(bk + 1) * 512]),
                          reads=[f"psa{bk}"], writes=[f"oq:{ee}:{s2}"])
                    else:
                        G("vector", lambda v, ee=ee, bk=bk, ocol=ocol:
                          v.tensor_add(out_q[:, ocol:ocol + 512],
                                       out_q[:, ocol:ocol + 512],
                                       psa[:, bk * 512:(bk + 1) * 512]),
                          reads=[f"psa{bk}", f"oq:{ee}:{s2}"],
                          writes=[f"oq:{ee}:{s2}"])
        for s in range(8):
            sp, s2 = s % 2, s // 4
            row0 = Q * 1024 + s * 128
            G("sync", lambda e, row0=row0, sp=sp: e.dma_start(
                out=x2r[sp][:], in_=x2_d.ap()[row0:row0 + 128, :]),
              reads=[f"x2_d:{row0}"], writes=[f"x2r{sp}"], dma=True)
            for ee in range(8):
                G("tensor", lambda t, ee=ee, s=s: t.matmul(
                    psa[:, ee * 128:(ee + 1) * 128],
                    out_q[:, ee * 1024 + s * 128:ee * 1024 + s * 128 + 128],
                    ident_f[:], is_transpose=True),
                  reads=[f"oq:{ee}:{s2}", "ident_f"],
                  writes=["psaT", "psa0", "psa1"])
            ag, rs, lntags = ln_stats(0, psa[:, 0:512], psa[:, 512:1024],
                                      ["psaT", "psa0", "psa1"])
            G("vector", lambda v, sp=sp, ag=ag, rs=rs: v.tensor_scalar(
                outst[sp][:], psa[:], ag[:, 0:1], rs[:, 0:1],
                ALU.subtract, ALU.mult),
              reads=["psaT", "psa0", "psa1", *lntags], writes=[f"outst{sp}"])
            G("vector", lambda v, sp=sp: v.tensor_add(
                outst[sp][:], outst[sp][:], x2r[sp][:]),
              reads=[f"outst{sp}", f"x2r{sp}"], writes=[f"outst{sp}"])
            G("scalar", lambda e, row0=row0, sp=sp: e.dma_start(
                out=out_hbm[row0:row0 + 128, :], in_=outst[sp][:]),
              reads=[f"outst{sp}"], writes=[f"outd:{row0}"], dma=True)
            all_out.append(f"outd:{row0}")

    S.add("sync", lambda e: None, reads=all_out)
    emit_schedule(nc, S, sems, qsems, inc_c, qlane_c, qlane_last)
    return nc


_NC_CACHE = None


def kernel(**inputs):
    global _NC_CACHE
    if _NC_CACHE is None:
        _NC_CACHE = build_program()
    nc = _NC_CACHE
    x = np.ascontiguousarray(np.asarray(inputs["L0_attn_x"], dtype=np.float32))
    common = {
        "wq": np.ascontiguousarray(np.asarray(inputs["L0_attn_wq"], np.float32)),
        "wk": np.ascontiguousarray(np.asarray(inputs["L0_attn_wk"], np.float32)),
        "wv": np.ascontiguousarray(np.asarray(inputs["L0_attn_wv"], np.float32)),
        "wo": np.ascontiguousarray(np.asarray(inputs["L0_attn_wo"], np.float32)),
        "bq": np.ascontiguousarray(np.asarray(inputs["L0_attn_bq"], np.float32)),
        "bk": np.ascontiguousarray(np.asarray(inputs["L0_attn_bk"], np.float32)),
        "bv": np.ascontiguousarray(np.asarray(inputs["L0_attn_bv"], np.float32)),
        "bo": np.ascontiguousarray(np.asarray(inputs["L0_attn_bo"], np.float32)),
        "rm": np.ascontiguousarray(np.asarray(inputs["L0_ffn_rail_maker"], np.float32)),
        "ru": np.ascontiguousarray(np.asarray(inputs["L0_ffn_rail_up_proj"], np.float32)),
        "wcp": np.ascontiguousarray(np.asarray(inputs["L0_ffn_w_cproj"], np.float32)),
    }
    in_maps = []
    for c in range(NCORES):
        m = dict(common)
        m["x"] = np.ascontiguousarray(x[c])
        in_maps.append(m)
    res = run_bass_kernel_spmd(nc, in_maps, core_ids=list(range(NCORES)))
    return np.stack([r["out"] for r in res.results], axis=0)


# version 28 (final)
# speedup vs baseline: 1.0688x; 1.0219x over previous
import numpy as np
import concourse.bass as bass
import concourse.mybir as mybir
from concourse.bass_utils import run_bass_kernel_spmd

F32 = mybir.dt.float32
BF16 = mybir.dt.bfloat16
AF = mybir.ActivationFunctionType
ALU = mybir.AluOpType

NCORES = 8
T = 4096
D = 1024
H = 16
NT = 8
TT = 512
JB = 8
NSLOT = 8
B_EXP = -20.0
EPS = 1e-5
MULT_F = 16384
NQ = 4

ENG_NAMES = ("sync", "tensor", "vector", "scalar", "gpsimd")


class Op:
    __slots__ = ("idx", "eng", "emit", "deps", "dma", "inc_no", "needed", "rdeps")

    def __init__(self, idx, eng, emit, deps, dma):
        self.idx = idx
        self.eng = eng
        self.emit = emit
        self.deps = deps
        self.dma = dma
        self.inc_no = None
        self.needed = False


class Sched:

    def __init__(self):
        self.ops = []
        self.last_w = {}
        self.readers = {}

    def add(self, eng, emit, reads=(), writes=(), dma=False):
        deps = set()
        for t in reads:
            w = self.last_w.get(t)
            if w is not None:
                deps.add(w)
        for t in writes:
            w = self.last_w.get(t)
            if w is not None:
                deps.add(w)
            deps.update(self.readers.get(t, ()))
        op = Op(len(self.ops), eng, emit, deps, dma)
        self.ops.append(op)
        for t in reads:
            self.readers.setdefault(t, set()).add(op)
        for t in writes:
            self.last_w[t] = op
            self.readers[t] = set()
        return op


NLANE = 8


def emit_schedule(nc, sched, sems, qsems, inc_counters, qlane_counters,
                  qlane_last):
    ops = sched.ops
    for op in ops:
        if op.dma:
            lanes = qlane_counters[op.eng]
            lane = min(range(NLANE), key=lambda s: lanes[s])
            lanes[lane] += 1
            op.inc_no = (lane, lanes[lane])
            prev = qlane_last[op.eng][lane]
            if prev is not None:
                op.deps.add(prev)
            qlane_last[op.eng][lane] = op
    for op in ops:
        rd = [d for d in op.deps
              if not (d.eng == "tensor" and op.eng == "tensor"
                      and not d.dma and not op.dma)]
        best = {}
        keep = []
        for d in rd:
            if d.dma:
                key = ("q", d.eng, d.inc_no[0])
            elif d.eng == "tensor":
                key = ("pe",)
            else:
                keep.append(d)
                continue
            if key not in best or best[key].idx < d.idx:
                best[key] = d
        op.rdeps = keep + list(best.values())
        for d in op.rdeps:
            d.needed = True
    for op in ops:
        if not op.dma and op.needed:
            inc_counters[op.eng] += 1
            op.inc_no = inc_counters[op.eng]

    wait_cache = {e: {} for e in ENG_NAMES}

    def body_for(engname):
        eops = [op for op in ops if op.eng == engname]

        def body(eng):
            cache = wait_cache[engname]
            for op in eops:
                for d in op.rdeps:
                    if d.inc_no is None:
                        continue
                    if d.dma:
                        lane, cnt = d.inc_no
                        sem, val = qsems[d.eng][lane], 16 * cnt
                    else:
                        sem, val = sems[d.eng], d.inc_no
                    key = id(sem)
                    if cache.get(key, -1) >= val:
                        continue
                    eng.wait_ge(sem, val)
                    cache[key] = val
                ins = op.emit(eng)
                if op.inc_no is not None:
                    assert ins is not None, f"op {op.idx} on {engname} needs inc"
                    if op.dma:
                        lane, cnt = op.inc_no
                        ins.then_inc(qsems[op.eng][lane], 16)
                    else:
                        ins.then_inc(sems[op.eng], 1)
        return body

    with nc.Block() as block:
        for engname in ENG_NAMES:
            if any(op.eng == engname for op in ops):
                getattr(block, engname)(body_for(engname))


def build_program(ni=NT, do_attn=True, do_ph3=True, do_prep_wct=True,
                  do_rails=True, p1_gate=127, do_wcast=True, do_bias=True):
    nc = bass.Bass("TRN2", target_bir_lowering=False)

    x_in = nc.dram_tensor("x", [T, D], F32, kind="ExternalInput").ap()
    w_ins = {n: nc.dram_tensor(n, [D, D], F32, kind="ExternalInput").ap()
             for n in ("wq", "wk", "wv", "wo")}
    b_ins = {n: nc.dram_tensor(n, [D], F32, kind="ExternalInput").ap()
             for n in ("bq", "bk", "bv", "bo")}
    rm_in = nc.dram_tensor("rm", [4, 64, 16, 16], F32, kind="ExternalInput").ap()
    ru_in = nc.dram_tensor("ru", [4, 64, 64, 16], F32, kind="ExternalInput").ap()
    wcp_in = nc.dram_tensor("wcp", [D, MULT_F], F32, kind="ExternalInput").ap()
    out_hbm = nc.dram_tensor("out", [T, D], F32, kind="ExternalOutput").ap()

    wqkv_d = nc.dram_tensor("wqkv_d", [4, 8, 128, D], BF16, kind="Internal")
    wcT_d = nc.dram_tensor("wcT_d", [128, 128, D], BF16, kind="Internal")
    k_d = nc.dram_tensor("k_d", [32, 128, D], BF16, kind="Internal")
    v_d = nc.dram_tensor("v_d", [32, 128, H * 65], BF16, kind="Internal")
    x2_d = nc.dram_tensor("x2_d", [T, D], F32, kind="Internal")
    xh3_d = nc.dram_tensor("xh3_d", [8, 128, T], BF16, kind="Internal")
    rl_d = nc.dram_tensor("rl_d", [16, TT], BF16, kind="Internal")

    consts = nc.alloc_sbuf_tensor("consts", [128, 4], F32)
    ident_b = nc.alloc_sbuf_tensor("ident_b", [128, 128], BF16)
    ones_b = nc.alloc_sbuf_tensor("ones_b", [128, 128], BF16)
    masks2 = nc.alloc_sbuf_tensor("masks2", [128, 8 * TT], BF16)
    bias_all = nc.alloc_sbuf_tensor("bias_all", [128, 24], F32)
    bo_bh = [nc.alloc_sbuf_tensor(f"bo_bh{p}", [1, TT], BF16) for p in range(2)]
    statsA = nc.alloc_sbuf_tensor("statsA", [128, 12], F32)
    statsB = nc.alloc_sbuf_tensor("statsB", [128, 12], F32)
    aggrA = nc.alloc_sbuf_tensor("aggrA", [128, 2], F32)
    aggrB = nc.alloc_sbuf_tensor("aggrB", [128, 2], F32)
    rstdA = nc.alloc_sbuf_tensor("rstdA", [128, 1], F32)
    rstdB = nc.alloc_sbuf_tensor("rstdB", [128, 1], F32)
    zw = nc.alloc_sbuf_tensor("zw", [128, 32 * 128], BF16)
    wcin = [nc.alloc_sbuf_tensor(f"wcin{p}", [128, 512], F32) for p in range(3)]
    wcb = [nc.alloc_sbuf_tensor(f"wcb{p}", [128, 512], BF16) for p in range(3)]
    wcst = [nc.alloc_sbuf_tensor(f"wcst{p}", [128, 512], BF16) for p in range(1)]
    statsC = nc.alloc_sbuf_tensor("statsC", [128, 12], F32)
    statsD = nc.alloc_sbuf_tensor("statsD", [128, 12], F32)
    aggrC = nc.alloc_sbuf_tensor("aggrC", [128, 2], F32)
    aggrD = nc.alloc_sbuf_tensor("aggrD", [128, 2], F32)
    rstdC = nc.alloc_sbuf_tensor("rstdC", [128, 1], F32)
    rstdD = nc.alloc_sbuf_tensor("rstdD", [128, 1], F32)
    rmL = [nc.alloc_sbuf_tensor(f"rmL{p}", [128, 16], F32) for p in range(3)]
    rmLb = [nc.alloc_sbuf_tensor(f"rmLb{p}", [128, 16], BF16) for p in range(3)]
    rmT = [nc.alloc_sbuf_tensor(f"rmT{p}", [16, 128], BF16) for p in range(3)]
    ruL = [nc.alloc_sbuf_tensor(f"ruL{p}", [128, 16], F32) for p in range(3)]
    ruLb = [nc.alloc_sbuf_tensor(f"ruLb{p}", [128, 16], BF16) for p in range(3)]
    ruT = [nc.alloc_sbuf_tensor(f"ruT{p}", [16, 512], BF16) for p in range(3)]
    cstage = [nc.alloc_sbuf_tensor(f"cstage{p}", [16, 64], BF16) for p in range(2)]
    ltmp = nc.alloc_sbuf_tensor("ltmp", [1, TT], F32)

    cursor = 229376 - nc.sbuf_bytes_remaining
    base = (cursor + 127) & ~127
    off = [0]

    def at(name, shape, dtype, nbytes):
        t = nc.alloc_sbuf_tensor_at(name, shape, dtype, offset=base + off[0])
        off[0] += (nbytes + 127) & ~127
        return t

    wbuf = [at(f"wbuf{p}", [128, 8 * 1024], BF16, 16384) for p in range(2)]
    q_sb = at("q_sb", [128, 8 * TT], BF16, 8192)
    xh1T = at("xh1T", [128, 8 * TT], BF16, 8192)
    aoT = at("aoT", [128, 8 * TT], BF16, 8192)
    x_tile = [at(f"x_tile{s}", [128, D], F32, 4096) for s in range(2)]
    xres = [at(f"xres{s}", [128, D], F32, 4096) for s in range(2)]
    xh1 = [at(f"xh1{p}", [128, D], BF16, 2048) for p in range(2)]
    k_tile = at("k_tile", [128, 8 * TT], BF16, 8192)
    expt = k_tile
    v_tile = at("v_tile", [128, 8 * TT], BF16, 8192)
    kread = [at(f"kread{p}", [128, D], BF16, 2048) for p in range(NSLOT)]
    vread = [at(f"vread{p}", [128, H * 65], BF16, 2112) for p in range(NSLOT)]
    ao_acc = at("ao_acc", [128, 8 * TT], F32, 16384)
    l_parts = at("l_parts", [16, 2 * TT], F32, 4096)
    rl_all = at("rl_all", [16, TT], F32, 2048)
    rl_bf = at("rl_bf", [16, TT], BF16, 1024)
    v_stage = [at(f"v_stage{p}", [128, H * 65], BF16, 2112) for p in range(1)]
    x2_sb = [at(f"x2_sb{p}", [128, D], F32, 4096) for p in range(2)]
    x3h = [at(f"x3h{p}", [128, D], BF16, 2048) for p in range(4)]
    x3st = [at(f"x3st{p}", [128, 8 * 128], BF16, 2048) for p in range(1)]
    rlb = [at(f"rlb{p}", [128, TT], BF16, 2048) for p in range(2)]
    ph12_end = off[0]

    off[0] = 0
    out_q = at("out_q", [128, 8 * 1024], F32, 32768)
    h_tl = [at(f"h_tl{p}", [128, 16 * TT], BF16, 16384) for p in range(2)]
    wcbuf = [at(f"wcbuf{p}", [128, 8 * 1024], BF16, 16384) for p in range(2)]
    xh3q = at("xh3q", [128, 8 * 1024], BF16, 16384)
    x2r = [at(f"x2r{p}", [128, D], F32, 4096) for p in range(2)]
    ident_f = at("ident_f", [128, 128], F32, 512)
    outst = [at(f"outst{p}", [128, D], F32, 4096) for p in range(2)]
    ph3_end = off[0]
    assert base + max(ph12_end, ph3_end) <= 229376, (base, ph12_end, ph3_end)

    psa = nc.alloc_psum_tensor("psa", [128, 1024], F32)
    psc = nc.alloc_psum_tensor("psc", [128, 2048], F32)
    pstp2 = [nc.alloc_psum_tensor("pstpA", [128, 1024], BF16),
             nc.alloc_psum_tensor("pstpB", [128, 1024], BF16)]
    tpar = [0]

    sems = {e: nc.alloc_semaphore(f"s_{e}") for e in ENG_NAMES}
    qsems = {e: [nc.alloc_semaphore(f"q_{e}{s}") for s in range(NLANE)]
             for e in ("sync", "scalar", "gpsimd")}
    inc_c = {e: 0 for e in ENG_NAMES}
    qlane_c = {e: [0] * NLANE for e in ("sync", "scalar", "gpsimd")}
    qlane_last = {e: [None] * NLANE for e in ("sync", "scalar", "gpsimd")}

    S = Sched()

    def G(eng, emit, reads=(), writes=(), dma=False):
        S.add(eng, emit, reads, writes, dma)

    ln_sets = ((statsA, aggrA, rstdA, "A"), (statsB, aggrB, rstdB, "B"),
               (statsC, aggrC, rstdC, "C"), (statsD, aggrD, rstdD, "D"))
    ln_rot = [0]

    def ln_stats(par, src0, src1, rtags):
        st, ag, rs, p = ln_sets[ln_rot[0] % 4]
        ln_rot[0] += 1
        G("vector", lambda v, a=src0, st=st: v.bn_stats(st[:, 0:6], a),
          reads=rtags, writes=[f"st0{p}"])
        G("vector", lambda v, a=src1, st=st: v.bn_stats(st[:, 6:12], a),
          reads=rtags, writes=[f"st1{p}"])
        G("vector", lambda v, st=st, ag=ag: v.bn_aggr(ag[:], st[:]),
          reads=[f"st0{p}", f"st1{p}"], writes=[f"ag{p}"])
        G("scalar", lambda s, ag=ag, rs=rs: s.activation(
            rs[:], ag[:, 1:2], AF.Ln, bias=consts[:, 1:2]),
          reads=[f"ag{p}", "c:eps"], writes=[f"rsL{p}"])
        G("scalar", lambda s, rs=rs: s.activation(rs[:], rs[:], AF.Exp, scale=-0.5),
          reads=[f"rsL{p}"], writes=[f"rs{p}"])
        return ag, rs, (f"ag{p}", f"rs{p}")

    G("gpsimd", lambda g: g.memset(consts[:, 0:1], B_EXP), writes=["c:bexp"])
    G("gpsimd", lambda g: g.memset(consts[:, 1:2], EPS), writes=["c:eps"])
    G("gpsimd", lambda g: g.memset(ones_b[:], 1.0), writes=["ones_b"])
    G("gpsimd", lambda g: g.affine_select(ident_b[:], ones_b[:], [[1, 128]],
                                          ALU.is_equal, 0.0, base=0,
                                          channel_multiplier=-1),
      reads=["ones_b"], writes=["ident_b"])
    G("gpsimd", lambda g: g.memset(masks2[:], 1.0), writes=["m2init"])
    for d in range(4):
        G("gpsimd", lambda g, d=d: g.affine_select(
            masks2[:, d * 1024:(d + 1) * 1024],
            masks2[:, d * 1024:(d + 1) * 1024], [[0, 2], [1, TT]],
            ALU.is_ge, 0.0, base=-128 * d, channel_multiplier=-1),
          reads=["m2init"], writes=[f"mask2_{d}"])
    G("gpsimd", lambda g: g.memset(zw[:], 0.0), writes=["zw"])
    G("gpsimd", lambda g: g.memset(v_stage[0][:], 1.0), writes=["v_stage0"])

    for pi, bn in enumerate(("bq", "bk", "bv") if do_bias else ()):
        for c in range(8):
            src = b_ins[bn][c * 128:(c + 1) * 128].rearrange("(p o) -> p o", o=1)
            col = pi * 8 + c
            G("gpsimd", lambda g, src=src, col=col: g.dma_start(
                out=bias_all[:, col:col + 1], in_=src),
              writes=[f"bias{col}"], dma=True)
    for hf in range(2):
        G("gpsimd", lambda g, hf=hf: g.dma_start(
            out=ltmp[:], in_=b_ins["bo"][hf * TT:(hf + 1) * TT]
            .rearrange("(o d) -> o d", o=1)),
          writes=["ltmp"], dma=True)
        G("gpsimd", lambda g, hf=hf: g.tensor_copy(bo_bh[hf][:], ltmp[:]),
          reads=["ltmp"], writes=["bo_b"])

    for pi, wn in enumerate(("wq", "wk", "wv", "wo") if do_wcast else ()):
        for c in range(8):
            for hf in range(2):
                p = hf
                G("gpsimd", lambda g, wn=wn, c=c, hf=hf, p=p: g.dma_start(
                    out=wcin[p][:],
                    in_=w_ins[wn][c * 128:(c + 1) * 128, hf * 512:(hf + 1) * 512]),
                  writes=[f"wcin{p}"], dma=True)
                G("vector", lambda v, p=p: v.tensor_copy(wcb[p][:], wcin[p][:]),
                  reads=[f"wcin{p}"], writes=[f"wcb{p}"])
                G("gpsimd", lambda g, pi=pi, c=c, hf=hf, p=p: g.dma_start(
                    out=wqkv_d.ap()[pi, c, :, hf * 512:(hf + 1) * 512],
                    in_=wcb[p][:]),
                  reads=[f"wcb{p}"], writes=[f"wqkv:{pi}:{c}"], dma=True)

    def rails_prep(h, oc, pp):
        rsl = rm_in[h, oc * 8:(oc + 1) * 8].rearrange("r i j -> (r i) j")
        G("sync", lambda e, rsl=rsl, pp=pp: e.dma_start(out=rmL[pp][:], in_=rsl),
          writes=[f"rmL{pp}"], dma=True)
        G("gpsimd", lambda g, pp=pp: g.tensor_copy(rmLb[pp][:], rmL[pp][:]),
          reads=[f"rmL{pp}"], writes=[f"rmLb{pp}"])
        tb = tpar[0] % 2; tpar[0] += 1
        G("tensor", lambda t, tb=tb, pp=pp: t.matmul(
            pstp2[tb][0:16, 0:128], rmLb[pp][:], ident_b[:], is_transpose=True),
          reads=[f"rmLb{pp}", "ident_b"], writes=[f"pstp{tb}"])
        G("vector", lambda v, tb=tb, pp=pp: v.tensor_copy(
            rmT[pp][:], pstp2[tb][0:16, 0:128]),
          reads=[f"pstp{tb}"], writes=[f"rmT{pp}"])
        for pr in range(4):
            r0 = oc * 8 + pr * 2
            usl = ru_in[h, r0:r0 + 2].rearrange("r k j -> (r k) j")
            G("sync", lambda e, usl=usl, pp=pp: e.dma_start(out=ruL[pp][:], in_=usl),
              writes=[f"ruL{pp}"], dma=True)
            G("gpsimd", lambda g, pp=pp: g.tensor_copy(ruLb[pp][:], ruL[pp][:]),
              reads=[f"ruL{pp}"], writes=[f"ruLb{pp}"])
            tb = tpar[0] % 2; tpar[0] += 1
            G("tensor", lambda t, tb=tb, pp=pp: t.matmul(
                pstp2[tb][0:16, 128:256], ruLb[pp][:], ident_b[:],
                is_transpose=True),
              reads=[f"ruLb{pp}", "ident_b"], writes=[f"pstp{tb}"])
            G("vector", lambda v, tb=tb, pp=pp, pr=pr: v.tensor_copy(
                ruT[pp][:, pr * 128:(pr + 1) * 128], pstp2[tb][0:16, 128:256]),
              reads=[f"pstp{tb}"], writes=[f"ruT{pp}"])

    def rails_comp(h, oc, pp):
        for pr in range(4):
            r0 = oc * 8 + pr * 2
            for q in range(2):
                r = r0 + q
                cs = (pr * 2 + q) % 2
                tbc = tpar[0] % 2
                cps = pstp2[tbc][0:16, 512:1024].bitcast(F32)
                G("tensor", lambda t, pr=pr, q=q, pp=pp, cps=cps: t.matmul(
                    cps[0:16, 0:64],
                    rmT[pp][0:16, (pr * 2 + q) * 16:(pr * 2 + q) * 16 + 16],
                    ruT[pp][0:16, pr * 128 + q * 64:pr * 128 + (q + 1) * 64]),
                  reads=[f"rmT{pp}", f"ruT{pp}"], writes=[f"pstp{tbc}"])
                G("vector", lambda v, cs=cs, cps=cps: v.tensor_copy(
                    cstage[cs][:], cps[0:16, 0:64]),
                  reads=[f"pstp{tbc}"], writes=[f"cstage{cs}"])
                pair, half = r // 2, r % 2
                a, cb = pair % 4, (h * 32 + pair) // 4
                dst = zw[32 * a + 16 * half:32 * a + 16 * half + 16,
                         cb * 128 + half * 64:cb * 128 + half * 64 + 64]
                G("scalar", lambda e, dst=dst, cs=cs: e.dma_start(
                    out=dst, in_=cstage[cs][:]),
                  reads=[f"cstage{cs}", "zw"], writes=["zw"], dma=True)

    def wcT_load(ec, ft, p):
        G("gpsimd", lambda g, ec=ec, ft=ft, p=p: g.dma_start(
            out=wcin[p][:],
            in_=wcp_in[ec * 128:(ec + 1) * 128, ft * 512:(ft + 1) * 512]),
          writes=[f"wcin{p}"], dma=True)
        G("vector", lambda v, p=p: v.tensor_copy(wcb[p][:], wcin[p][:]),
          reads=[f"wcin{p}"], writes=[f"wcb{p}"])

    def wcT_tr(ec, ft, p):
        tb = tpar[0] % 2; tpar[0] += 1
        for kk in range(4):
            G("tensor", lambda t, p=p, kk=kk, tb=tb: t.matmul(
                pstp2[tb][:, kk * 128:(kk + 1) * 128],
                wcb[p][:, kk * 128:(kk + 1) * 128],
                ident_b[:], is_transpose=True),
              reads=[f"wcb{p}", "ident_b"], writes=[f"pstp{tb}"])
        ps = 0
        G("vector", lambda v, ps=ps, tb=tb: v.tensor_copy(wcst[ps][:],
                                                          pstp2[tb][:, 0:512]),
          reads=[f"pstp{tb}"], writes=[f"wcst{ps}"])
        dst = wcT_d.ap()[4 * ft:4 * ft + 4, :, ec * 128:(ec + 1) * 128] \
            .rearrange("k p e -> p k e")
        srcw = wcst[ps][:].rearrange("p (k e) -> p k e", k=4)
        G("gpsimd", lambda e, dst=dst, srcw=srcw: e.dma_start(out=dst, in_=srcw),
          reads=[f"wcst{ps}"], writes=[f"wcT:{ec}:{ft}"], dma=True)

    def wload(pi, par, rtag):
        src = wqkv_d.ap()[pi].rearrange("c p n -> p c n")
        dst = wbuf[par][:].rearrange("p (c n) -> p c n", c=8)
        G("gpsimd", lambda e, src=src, dst=dst: e.dma_start(out=dst, in_=src),
          reads=[f"wqkv:{pi}:{c}" for c in range(8)],
          writes=[rtag], dma=True)

    def proj_mms(par, dest, bcol0, dtag, wtag, i):
        for c in range(8):
            bp = c % 4
            for kc in range(8):
                G("tensor", lambda t, par=par, c=c, kc=kc, bp=bp: t.matmul(
                    psc[:, bp * 512:(bp + 1) * 512],
                    wbuf[par][:, kc * 1024 + c * 128:kc * 1024 + c * 128 + 128],
                    xh1T[:, kc * 512:(kc + 1) * 512],
                    start=(kc == 0), stop=(kc == 7)),
                  reads=[wtag, f"xh1T{kc}"], writes=[f"psc{bp}"])
            extra_r = ["rl_all"] if dtag == "k_tile" and i > 0 else []
            G("vector", lambda v, dest=dest, c=c, bcol0=bcol0, bp=bp:
              v.tensor_scalar(dest[:, c * 512:(c + 1) * 512],
                              psc[:, bp * 512:(bp + 1) * 512],
                              bias_all[:, bcol0 + c:bcol0 + c + 1], None,
                              ALU.add),
              reads=[f"psc{bp}", f"bias{bcol0 + c}"] + extra_r,
              writes=[f"{dtag}:{c}"])

    lnres = [None, None]

    def kvload_early(j):
        sl = j % NSLOT
        G("sync", lambda e, j=j, sl=sl: e.dma_start(
            out=kread[sl][:], in_=k_d.ap()[j]),
          reads=[f"k_d:{j}"], writes=[f"ks{sl}", f"kvl:{j}"], dma=True)
        G("scalar", lambda e, j=j, sl=sl: e.dma_start(
            out=vread[sl][:], in_=v_d.ap()[j]),
          reads=[f"v_d:{j}"], writes=[f"vs{sl}", f"kvl:{j}"], dma=True)

    def kvload_early(j):
        sl = j % NSLOT
        G("sync", lambda e, j=j, sl=sl: e.dma_start(
            out=kread[sl][:], in_=k_d.ap()[j]),
          reads=[f"k_d:{j}"], writes=[f"ks{sl}", f"kvl:{j}"], dma=True)
        G("scalar", lambda e, j=j, sl=sl: e.dma_start(
            out=vread[sl][:], in_=v_d.ap()[j]),
          reads=[f"v_d:{j}"], writes=[f"vs{sl}", f"kvl:{j}"], dma=True)

    def p1a(i):
        wload(0, 0, "wbuf0")
        for s in range(4):
            row0 = i * TT + s * 128
            G("sync", lambda e, row0=row0, s=s: e.dma_start(
                out=x_tile[s % 2][:], in_=x_in[row0:row0 + 128, :]),
              writes=[f"x_tile{s % 2}"], dma=True)
            sp = s % 2
            ag, rs, lntags = ln_stats(0, x_tile[sp][:, 0:512],
                                      x_tile[sp][:, 512:1024], [f"x_tile{sp}"])
            lnres[sp] = (ag, rs, lntags)
            if sp == 1:
                for s2 in (s - 1, s):
                    sp2 = s2 % 2
                    ag, rs, lntags = lnres[sp2]
                    G("vector", lambda v, sp2=sp2, ag=ag, rs=rs: v.tensor_scalar(
                        xh1[sp2][:], x_tile[sp2][:], ag[:, 0:1], rs[:, 0:1],
                        ALU.subtract, ALU.mult),
                      reads=[f"x_tile{sp2}", *lntags], writes=[f"xh1{sp2}"])
                    tb = tpar[0] % 2; tpar[0] += 1
                    for c in range(8):
                        G("tensor", lambda t, sp2=sp2, c=c, tb=tb: t.matmul(
                            pstp2[tb][:, c * 128:(c + 1) * 128],
                            xh1[sp2][:, c * 128:(c + 1) * 128],
                            ident_b[:], is_transpose=True),
                          reads=[f"xh1{sp2}", "ident_b"], writes=[f"pstp{tb}"])
                    for c in range(8):
                        G("vector", lambda v, s2=s2, c=c, tb=tb: v.tensor_copy(
                            xh1T[:, c * 512 + s2 * 128:c * 512 + s2 * 128 + 128],
                            pstp2[tb][:, c * 128:(c + 1) * 128]),
                          reads=[f"pstp{tb}"], writes=[f"xh1T{c}"])
        proj_mms(0, q_sb, 0, "q_sb", "wbuf0", i)
        for j in range(min(JB, 4 * i + 4)):
            if j <= 4 * (i - 1) + 3:
                kvload_early(j)

    def p1b(i):
        wload(1, 1, "wbuf1")
        proj_mms(1, k_tile, 8, "k_tile", "wbuf1", i)
        wload(2, 0, "wbuf0")
        proj_mms(0, v_tile, 16, "v_tile", "wbuf0", i)
        for s in range(4):
            j = 4 * i + s
            srck = k_tile[:].rearrange("p (c n) -> p c n", c=8)[:, :, s * 128:(s + 1) * 128]
            dstk = k_d.ap()[j].rearrange("p (c n) -> p c n", c=8)
            G("scalar", lambda e, dstk=dstk, srck=srck: e.dma_start(
                out=dstk, in_=srck),
              reads=[f"k_tile:{c}" for c in range(8)],
              writes=[f"k_d:{j}"], dma=True)
        for s in range(4):
            j = 4 * i + s
            tb = tpar[0] % 2; tpar[0] += 1
            for c in range(8):
                G("tensor", lambda t, s=s, c=c, tb=tb: t.matmul(
                    pstp2[tb][:, c * 128:(c + 1) * 128],
                    v_tile[:, c * 512 + s * 128:c * 512 + s * 128 + 128],
                    ident_b[:], is_transpose=True),
                  reads=[f"v_tile:{c}", "ident_b"], writes=[f"pstp{tb}"])
            for c in range(8):
                for hh in range(2):
                    hgl = 2 * c + hh
                    G("vector", lambda v, c=c, hh=hh, hgl=hgl, tb=tb: v.tensor_copy(
                        v_stage[0][:, hgl * 65:hgl * 65 + 64],
                        pstp2[tb][:, c * 128 + hh * 64:c * 128 + hh * 64 + 64]),
                      reads=[f"pstp{tb}"], writes=["v_stage0"])
            G("scalar", lambda e, j=j: e.dma_start(
                out=v_d.ap()[j], in_=v_stage[0][:]),
              reads=["v_stage0"], writes=[f"v_d:{j}"], dma=True)
        wload(3, 1, "wbuf1")

    def attn(i):
        njc = 4 * i + 4
        jblocks = [(b0, min(b0 + JB, njc)) for b0 in range(0, njc, JB)]
        nb = len(jblocks)

        def kvload(j):
            sl = j % NSLOT
            G("sync", lambda e, j=j, sl=sl: e.dma_start(
                out=kread[sl][:], in_=k_d.ap()[j]),
              reads=[f"k_d:{j}"], writes=[f"ks{sl}"], dma=True)
            G("scalar", lambda e, j=j, sl=sl: e.dma_start(
                out=vread[sl][:], in_=v_d.ap()[j]),
              reads=[f"v_d:{j}"], writes=[f"vs{sl}"], dma=True)

        hoisted = set(j for j in range(min(JB, njc))
                      if j <= 4 * (i - 1) + 3)
        for bi, (j0, j1) in enumerate(jblocks):
            for j in range(j0, j1):
                if bi == 0 and j in hoisted:
                    continue
                kvload(j)
            def emit_sc(pr_, hh, j, ug):
                chunk = pr_
                hb = hh * 64
                slot = (ug % 2) * 2 + hh
                sl = j % NSLOT
                G("tensor", lambda t, sl=sl, slot=slot, chunk=chunk, hb=hb:
                  t.matmul(psc[:, slot * 512:(slot + 1) * 512],
                           kread[sl][hb:hb + 64, chunk * 128:chunk * 128 + 128],
                           q_sb[hb:hb + 64, chunk * 512:(chunk + 1) * 512],
                           start=True, stop=True),
                  reads=[f"ks{sl}", f"q_sb:{chunk}"],
                  writes=[f"psc{slot}"])

            def emit_exp(pr_, j, ug):
                sb2 = (ug % 2) * 2
                erg = ug % 4
                fence = ([f"k_d:{4 * i + ss}" for ss in range(4)]
                         if bi == 0 and pr_ == 0 and j == j0 else [])
                G("scalar", lambda s, sb2=sb2, erg=erg: s.activation(
                    expt[:, erg * 1024:(erg + 1) * 1024],
                    psc[:, sb2 * 512:(sb2 + 2) * 512],
                    AF.Exp, bias=consts[:, 0:1], scale=0.125),
                  reads=[f"psc{sb2}", f"psc{sb2 + 1}", "c:bexp"] + fence,
                  writes=[f"expt{erg}"])
                dd = j - 4 * i
                if dd >= 0:
                    ecol = erg * 1024
                    G("vector", lambda v, ecol=ecol, dd=dd, erg=erg:
                      v.tensor_tensor(
                          expt[:, ecol:ecol + 1024],
                          expt[:, ecol:ecol + 1024],
                          masks2[:, dd * 1024:(dd + 1) * 1024], ALU.mult),
                      reads=[f"expt{erg}", f"mask2_{dd}"],
                      writes=[f"expt{erg}"])

            def emit_av(pr_, j, ug):
                h0 = 2 * pr_
                erg = ug % 4
                for hh in range(2):
                    h = h0 + hh
                    sl = j % NSLOT
                    ecol = erg * 1024 + hh * 512
                    G("tensor", lambda t, sl=sl, ecol=ecol, h=h, hh=hh,
                      st=(j == j0), sp=(j == j1 - 1): t.matmul(
                          psa[0:65, hh * 512:(hh + 1) * 512],
                          vread[sl][:, h * 65:h * 65 + 65],
                          expt[:, ecol:ecol + 512],
                          start=st, stop=sp),
                      reads=[f"vs{sl}", f"expt{erg}"], writes=[f"psa{hh}"])

            def emit_evac(pr_):
                h0 = 2 * pr_
                chunk = pr_
                for hh in range(2):
                    h, hb = h0 + hh, hh * 64
                    asl = psa[:, hh * 512:(hh + 1) * 512]
                    if bi == 0:
                        G("vector", lambda v, asl=asl, hb=hb, chunk=chunk:
                          v.tensor_copy(
                              ao_acc[hb:hb + 64, chunk * 512:(chunk + 1) * 512],
                              asl[0:64, :]),
                          reads=[f"psa{hh}"], writes=[f"ao:{h}"])
                    else:
                        G("vector", lambda v, asl=asl, hb=hb, chunk=chunk:
                          v.tensor_add(
                              ao_acc[hb:hb + 64, chunk * 512:(chunk + 1) * 512],
                              ao_acc[hb:hb + 64, chunk * 512:(chunk + 1) * 512],
                              asl[0:64, :]),
                          reads=[f"psa{hh}", f"ao:{h}"], writes=[f"ao:{h}"])
                    G("scalar", lambda s, asl=asl: s.copy(ltmp[:], asl[64:65, :]),
                      reads=[f"psa{hh}"], writes=["ltmp"])
                    col = 0 if bi == 0 else TT
                    war = [f"ladd:{bi - 1}"] if bi >= 2 else []
                    G("scalar", lambda e, h=h, bi=bi, col=col: e.dma_start(
                        out=l_parts[h:h + 1, col:col + TT],
                        in_=ltmp[:]),
                      reads=["ltmp"] + war, writes=[f"lp:{h}:{bi}"], dma=True)

            units = [(pr_, j) for pr_ in range(8) for j in range(j0, j1)]
            pend = []
            for pr_, j in units:
                ug = ucnt[0]; ucnt[0] += 1
                emit_sc(pr_, 0, j, ug)
                emit_sc(pr_, 1, j, ug)
                emit_exp(pr_, j, ug)
                pend.append((pr_, j, ug))
                if len(pend) > 3:
                    pu = pend.pop(0)
                    emit_av(*pu)
                    if pu[1] == j1 - 1:
                        emit_evac(pu[0])
                rails_step()
            for pu in pend:
                emit_av(*pu)
                if pu[1] == j1 - 1:
                    emit_evac(pu[0])
            if bi > 0:
                G("vector", lambda v: v.tensor_add(
                    l_parts[:, 0:TT], l_parts[:, 0:TT], l_parts[:, TT:2 * TT]),
                  reads=[f"lp:{h}:{bi}" for h in range(H)] +
                        ([f"lp:{h}:0" for h in range(H)] if bi == 1 else ["lsum"]),
                  writes=["lsum", f"ladd:{bi}"])

        return nb

    def attn_norm(i, nb):
        lsrc = ["lsum"] if nb > 1 else [f"lp:{h}:0" for h in range(H)]
        G("scalar", lambda s: s.activation(rl_all[:], l_parts[:, 0:TT], AF.Ln),
          reads=lsrc, writes=["rlL"])
        G("scalar", lambda s: s.activation(rl_all[:], rl_all[:], AF.Exp, scale=-1.0),
          reads=["rlL"], writes=["rl_all"])
        G("vector", lambda v: v.tensor_copy(rl_bf[:], rl_all[:]),
          reads=["rl_all"], writes=["rl_bf"])
        G("sync", lambda e: e.dma_start(out=rl_d.ap()[:, :], in_=rl_bf[:]),
          reads=["rl_bf"], writes=["rl_dd"], dma=True)
        for chunk in range(8):
            h0, h1, rp = 2 * chunk, 2 * chunk + 1, chunk % 2
            bsrc0 = bass.AP(rl_d, h0 * TT, [[0, 64], [1, TT]])
            bsrc1 = bass.AP(rl_d, h1 * TT, [[0, 64], [1, TT]])
            G("sync", lambda e, bsrc0=bsrc0, rp=rp: e.dma_start(
                out=rlb[rp][0:64, :], in_=bsrc0),
              reads=["rl_dd"], writes=[f"rlb{rp}"], dma=True)
            G("sync", lambda e, bsrc1=bsrc1, rp=rp: e.dma_start(
                out=rlb[rp][64:128, :], in_=bsrc1),
              reads=["rl_dd"], writes=[f"rlb{rp}"], dma=True)
            G("vector", lambda v, chunk=chunk, rp=rp: v.tensor_tensor(
                aoT[:, chunk * 512:(chunk + 1) * 512],
                ao_acc[:, chunk * 512:(chunk + 1) * 512],
                rlb[rp][:], ALU.mult),
              reads=[f"ao:{h0}", f"ao:{h1}", f"rlb{rp}"],
              writes=[f"aoT:{chunk}"])

    def wo_s(i):
        def emit_x3T(s):
            row0 = i * TT + s * 128
            tb = tpar[0] % 2; tpar[0] += 1
            for c in range(8):
                G("tensor", lambda t, s=s, c=c, tb=tb: t.matmul(
                    pstp2[tb][:, c * 128:(c + 1) * 128],
                    x3h[s][:, c * 128:(c + 1) * 128],
                    ident_b[:], is_transpose=True),
                  reads=[f"x3h{s}", "ident_b"], writes=[f"pstp{tb}"])
            G("vector", lambda v, tb=tb: v.tensor_copy(x3st[0][:], pstp2[tb][:]),
              reads=[f"pstp{tb}"], writes=["x3st0"])
            dst = xh3_d.ap()[:, :, row0:row0 + 128].rearrange("c p n -> p c n")
            srcx = x3st[0][:].rearrange("p (c n) -> p c n", c=8)
            G("scalar", lambda e, dst=dst, srcx=srcx: e.dma_start(
                out=dst, in_=srcx),
              reads=["x3st0"], writes=[f"xh3:{row0}"], dma=True)

        for s in range(4):
            sp = s % 2
            sb2 = sp * 2
            row0 = i * TT + s * 128
            G("sync", lambda e, row0=row0, sp=sp: e.dma_start(
                out=xres[sp][:], in_=x_in[row0:row0 + 128, :]),
              writes=[f"xres{sp}"], dma=True)
            for half in range(2):
                slot = sb2 + half
                for kc in range(8):
                    G("tensor", lambda t, s=s, slot=slot, half=half, kc=kc: t.matmul(
                        psc[:, slot * 512:(slot + 1) * 512],
                        aoT[:, kc * 512 + s * 128:kc * 512 + s * 128 + 128],
                        wbuf[1][:, kc * 1024 + half * 512:kc * 1024 + (half + 1) * 512],
                        start=(kc == 0), stop=False),
                      reads=[f"aoT:{kc}", "wbuf1"], writes=[f"psc{slot}"])
                G("tensor", lambda t, slot=slot, half=half: t.matmul(
                    psc[:, slot * 512:(slot + 1) * 512],
                    ones_b[0:1, :], bo_bh[half][:],
                    start=False, stop=True),
                  reads=["ones_b", "bo_b"], writes=[f"psc{slot}"])
            if s > 0:
                emit_x3T(s - 1)
            ag, rs, lntags = ln_stats(0, psc[:, sb2 * 512:sb2 * 512 + 512],
                                      psc[:, sb2 * 512 + 512:sb2 * 512 + 1024],
                                      [f"psc{sb2}", f"psc{sb2 + 1}"])
            G("vector", lambda v, sp=sp, sb2=sb2, ag=ag, rs=rs: v.tensor_scalar(
                x2_sb[sp][:], psc[:, sb2 * 512:sb2 * 512 + 1024],
                ag[:, 0:1], rs[:, 0:1], ALU.subtract, ALU.mult),
              reads=[f"psc{sb2}", f"psc{sb2 + 1}", *lntags], writes=[f"x2sb{sp}"])
            G("vector", lambda v, sp=sp: v.tensor_add(
                x2_sb[sp][:], x2_sb[sp][:], xres[sp][:]),
              reads=[f"x2sb{sp}", f"xres{sp}"], writes=[f"x2sb{sp}"])
            G("scalar", lambda e, row0=row0, sp=sp: e.dma_start(
                out=x2_d.ap()[row0:row0 + 128, :], in_=x2_sb[sp][:]),
              reads=[f"x2sb{sp}"], writes=[f"x2_d:{row0}"], dma=True)
            ag, rs, lntags = ln_stats(1, x2_sb[sp][:, 0:512], x2_sb[sp][:, 512:1024],
                                      [f"x2sb{sp}"])
            G("vector", lambda v, s=s, sp=sp, ag=ag, rs=rs: v.tensor_scalar(
                x3h[s][:], x2_sb[sp][:], ag[:, 0:1], rs[:, 0:1],
                ALU.subtract, ALU.mult),
              reads=[f"x2sb{sp}", *lntags], writes=[f"x3h{s}"])
        emit_x3T(3)

    wct_seq = ([(ec, ft) for ec in range(8) for ft in range(32)]
               if do_prep_wct else [])
    rl_seq = ([(h, oc) for h in range(4) for oc in range(8)] if do_rails else [])
    wct_li, wct_ti = [0], [0]
    rl_li, rl_ti = [0], [0]
    nbg = max(1, len(wct_seq) + len(rl_seq))
    bg_cnt = [0]
    bg_thresh = 1100.0 / nbg
    def bg_step_one():
        done_w = wct_ti[0] >= len(wct_seq)
        done_r = rl_ti[0] >= len(rl_seq)
        use_r = (not done_r) and (done_w or (wct_ti[0] % 8 == 7 and rl_ti[0] * 8 < wct_ti[0]))
        if use_r:
            while rl_li[0] < len(rl_seq) and rl_li[0] <= rl_ti[0] + 2:
                rails_prep(*rl_seq[rl_li[0]], rl_li[0] % 3)
                rl_li[0] += 1
            rails_comp(*rl_seq[rl_ti[0]], rl_ti[0] % 3)
            rl_ti[0] += 1
        elif not done_w:
            while wct_li[0] < len(wct_seq) and wct_li[0] <= wct_ti[0] + 2:
                wcT_load(*wct_seq[wct_li[0]], wct_li[0] % 3)
                wct_li[0] += 1
            wcT_tr(*wct_seq[wct_ti[0]], wct_ti[0] % 3)
            wct_ti[0] += 1

    def rails_step():
        bg_cnt[0] += 1
        while bg_cnt[0] >= bg_thresh:
            bg_cnt[0] -= bg_thresh
            bg_step_one()

    def bg_flush():
        while wct_ti[0] < len(wct_seq) or rl_ti[0] < len(rl_seq):
            bg_step_one()

    ucnt = [0]

    p1a(0)
    p1b(0)
    for i in range(ni):
        nb = attn(i)
        if i + 1 < ni:
            p1a(i + 1)
        attn_norm(i, nb)
        wo_s(i)
        if i + 1 < ni:
            p1b(i + 1)
    bg_flush()

    emit_schedule(nc, S, sems, qsems, inc_c, qlane_c, qlane_last)

    S2 = Sched()
    S2.last_w = dict(S.last_w)
    S2.readers = {k: set(v) for k, v in S.readers.items()}
    S = S2
    all_out = []
    cslot = (psa[:, 0:512], psa[:, 512:1024],
             pstp2[0][:, 0:1024].bitcast(F32), pstp2[1][:, 0:1024].bitcast(F32))
    ctags = ("psa0", "psa1", "pstp0", "pstp1")
    for Q in range(NQ if do_ph3 else 0):
        for c in range(8):
            rows = [f"xh3:{r}" for r in range(Q * 1024, (Q + 1) * 1024, 128)]
            G("sync", lambda e, Q=Q, c=c: e.dma_start(
                out=xh3q[:, c * 1024:(c + 1) * 1024],
                in_=xh3_d.ap()[c, :, Q * 1024:(Q + 1) * 1024]),
              reads=rows, writes=[f"xh3q:{c}"], dma=True)
        for g in range(8):
            for hg in range(2):
                par = (g + hg) % 2
                fc0 = g * 16 + hg * 8
                src = wcT_d.ap()[fc0:fc0 + 8].rearrange("k p n -> p k n")
                dst = wcbuf[par][:].rearrange("p (k n) -> p k n", k=8)
                wct_tags = [f"wcT:{ec}:{ft}" for ec in range(8)
                            for ft in (fc0 // 4, fc0 // 4 + 1)]
                G("sync", lambda e, src=src, dst=dst: e.dma_start(out=dst, in_=src),
                  reads=wct_tags, writes=[f"wcbuf{par}"], dma=True)
            for s2 in range(2):
                hpar = s2
                zslots = (psc[:, 0:512], psc[:, 512:1024],
                          psc[:, 1024:1536], psc[:, 1536:2048])
                ztags = ("psc0", "psc1", "psc2", "psc3")
                for k in range(16):
                    fc = g * 16 + k
                    rr = (g % 2) * 16 + k
                    a, cb, cx = rr % 4, fc // 4, rr // 4
                    G("tensor", lambda t, a=a, cb=cb, cx=cx, s2=s2, k=k: t.matmul(
                        zslots[k % 4],
                        zw[32 * a:32 * a + 32, cb * 128:(cb + 1) * 128],
                        xh3q[32 * a:32 * a + 32,
                             cx * 1024 + s2 * 512:cx * 1024 + (s2 + 1) * 512],
                        start=True, stop=True, tile_position=(32 * a, 0)),
                      reads=["zw", f"xh3q:{cx}"], writes=[ztags[k % 4]])
                    if k % 2 == 1:
                        kb = k - 1
                        G("scalar", lambda s, kb=kb, hpar=hpar: s.activation(
                            h_tl[hpar][:, kb * 512:(kb + 2) * 512],
                            psc[:, (kb % 4) * 512:(kb % 4 + 2) * 512], AF.Gelu),
                          reads=[ztags[kb % 4], ztags[kb % 4 + 1]],
                          writes=[f"h{hpar}:{kb}", f"h{hpar}:{kb + 1}"])
                for tc in range(4):
                    sg = s2 * 4 + tc
                    for k in range(16):
                        par = (g + k // 8) % 2
                        for sl in range(2):
                            ci = (tc % 2) * 2 + sl
                            G("tensor", lambda t, k=k, tc=tc, sl=sl, ci=ci,
                              par=par, hpar=hpar: t.matmul(
                                  cslot[ci],
                                  h_tl[hpar][:, k * 512 + tc * 128:
                                              k * 512 + tc * 128 + 128],
                                  wcbuf[par][:, (k % 8) * 1024 + sl * 512:
                                             (k % 8) * 1024 + (sl + 1) * 512],
                                  start=(k == 0), stop=(k == 15)),
                              reads=[f"wcbuf{par}", f"h{hpar}:{k}"],
                              writes=[ctags[ci]])
                    for sl in range(2):
                        ci = (tc % 2) * 2 + sl
                        ocol = sg * 1024 + sl * 512
                        if g == 0:
                            G("vector", lambda v, ci=ci, ocol=ocol:
                              v.tensor_copy(out_q[:, ocol:ocol + 512], cslot[ci]),
                              reads=[ctags[ci]], writes=[f"oq:{ocol}"])
                        else:
                            G("vector", lambda v, ci=ci, ocol=ocol:
                              v.tensor_add(out_q[:, ocol:ocol + 512],
                                           out_q[:, ocol:ocol + 512], cslot[ci]),
                              reads=[ctags[ci], f"oq:{ocol}"],
                              writes=[f"oq:{ocol}"])
        for s in range(8):
            sp = s % 2
            row0 = Q * 1024 + s * 128
            G("sync", lambda e, row0=row0, sp=sp: e.dma_start(
                out=x2r[sp][:], in_=x2_d.ap()[row0:row0 + 128, :]),
              reads=[f"x2_d:{row0}"], writes=[f"x2r{sp}"], dma=True)
            oqt = [f"oq:{s * 1024}", f"oq:{s * 1024 + 512}"]
            ag, rs, lntags = ln_stats(0, out_q[:, s * 1024:s * 1024 + 512],
                                      out_q[:, s * 1024 + 512:(s + 1) * 1024],
                                      oqt)
            G("vector", lambda v, s=s, sp=sp, ag=ag, rs=rs: v.tensor_scalar(
                outst[sp][:], out_q[:, s * 1024:(s + 1) * 1024], ag[:, 0:1],
                rs[:, 0:1], ALU.subtract, ALU.mult),
              reads=[*oqt, *lntags], writes=[f"outst{sp}"])
            G("vector", lambda v, sp=sp: v.tensor_add(
                outst[sp][:], outst[sp][:], x2r[sp][:]),
              reads=[f"outst{sp}", f"x2r{sp}"], writes=[f"outst{sp}"])
            G("scalar", lambda e, row0=row0, sp=sp: e.dma_start(
                out=out_hbm[row0:row0 + 128, :], in_=outst[sp][:]),
              reads=[f"outst{sp}"], writes=[f"outd:{row0}"], dma=True)
            all_out.append(f"outd:{row0}")

    S.add("sync", lambda e: None, reads=all_out)
    emit_schedule(nc, S, sems, qsems, inc_c, qlane_c, qlane_last)
    return nc


_NC_CACHE = None


def kernel(**inputs):
    global _NC_CACHE
    if _NC_CACHE is None:
        _NC_CACHE = build_program()
    nc = _NC_CACHE
    x = np.ascontiguousarray(np.asarray(inputs["L0_attn_x"], dtype=np.float32))
    common = {
        "wq": np.ascontiguousarray(np.asarray(inputs["L0_attn_wq"], np.float32)),
        "wk": np.ascontiguousarray(np.asarray(inputs["L0_attn_wk"], np.float32)),
        "wv": np.ascontiguousarray(np.asarray(inputs["L0_attn_wv"], np.float32)),
        "wo": np.ascontiguousarray(np.asarray(inputs["L0_attn_wo"], np.float32)),
        "bq": np.ascontiguousarray(np.asarray(inputs["L0_attn_bq"], np.float32)),
        "bk": np.ascontiguousarray(np.asarray(inputs["L0_attn_bk"], np.float32)),
        "bv": np.ascontiguousarray(np.asarray(inputs["L0_attn_bv"], np.float32)),
        "bo": np.ascontiguousarray(np.asarray(inputs["L0_attn_bo"], np.float32)),
        "rm": np.ascontiguousarray(np.asarray(inputs["L0_ffn_rail_maker"], np.float32)),
        "ru": np.ascontiguousarray(np.asarray(inputs["L0_ffn_rail_up_proj"], np.float32)),
        "wcp": np.ascontiguousarray(np.asarray(inputs["L0_ffn_w_cproj"], np.float32)),
    }
    in_maps = []
    for c in range(NCORES):
        m = dict(common)
        m["x"] = np.ascontiguousarray(x[c])
        in_maps.append(m)
    res = run_bass_kernel_spmd(nc, in_maps, core_ids=list(range(NCORES)))
    return np.stack([r["out"] for r in res.results], axis=0)


# version 29
# speedup vs baseline: 1.0701x; 1.0012x over previous
import numpy as np
import concourse.bass as bass
import concourse.mybir as mybir
from concourse.bass_utils import run_bass_kernel_spmd

F32 = mybir.dt.float32
BF16 = mybir.dt.bfloat16
AF = mybir.ActivationFunctionType
ALU = mybir.AluOpType

NCORES = 8
T = 4096
D = 1024
H = 16
NT = 8
TT = 512
JB = 8
NSLOT = 8
B_EXP = -20.0
EPS = 1e-5
MULT_F = 16384
NQ = 4

ENG_NAMES = ("sync", "tensor", "vector", "scalar", "gpsimd")


class Op:
    __slots__ = ("idx", "eng", "emit", "deps", "dma", "inc_no", "needed", "rdeps")

    def __init__(self, idx, eng, emit, deps, dma):
        self.idx = idx
        self.eng = eng
        self.emit = emit
        self.deps = deps
        self.dma = dma
        self.inc_no = None
        self.needed = False


class Sched:

    def __init__(self):
        self.ops = []
        self.last_w = {}
        self.readers = {}

    def add(self, eng, emit, reads=(), writes=(), dma=False):
        deps = set()
        for t in reads:
            w = self.last_w.get(t)
            if w is not None:
                deps.add(w)
        for t in writes:
            w = self.last_w.get(t)
            if w is not None:
                deps.add(w)
            deps.update(self.readers.get(t, ()))
        op = Op(len(self.ops), eng, emit, deps, dma)
        self.ops.append(op)
        for t in reads:
            self.readers.setdefault(t, set()).add(op)
        for t in writes:
            self.last_w[t] = op
            self.readers[t] = set()
        return op


NLANE = 8


def emit_schedule(nc, sched, sems, qsems, inc_counters, qlane_counters,
                  qlane_last):
    ops = sched.ops
    for op in ops:
        if op.dma:
            lanes = qlane_counters[op.eng]
            lane = min(range(NLANE), key=lambda s: lanes[s])
            lanes[lane] += 1
            op.inc_no = (lane, lanes[lane])
            prev = qlane_last[op.eng][lane]
            if prev is not None:
                op.deps.add(prev)
            qlane_last[op.eng][lane] = op
    for op in ops:
        rd = [d for d in op.deps
              if not (d.eng == "tensor" and op.eng == "tensor"
                      and not d.dma and not op.dma)]
        best = {}
        keep = []
        for d in rd:
            if d.dma:
                key = ("q", d.eng, d.inc_no[0])
            elif d.eng == "tensor":
                key = ("pe",)
            else:
                keep.append(d)
                continue
            if key not in best or best[key].idx < d.idx:
                best[key] = d
        op.rdeps = keep + list(best.values())
        for d in op.rdeps:
            d.needed = True
    for op in ops:
        if not op.dma and op.needed:
            inc_counters[op.eng] += 1
            op.inc_no = inc_counters[op.eng]

    wait_cache = {e: {} for e in ENG_NAMES}

    def body_for(engname):
        eops = [op for op in ops if op.eng == engname]

        def body(eng):
            cache = wait_cache[engname]
            for op in eops:
                for d in op.rdeps:
                    if d.inc_no is None:
                        continue
                    if d.dma:
                        lane, cnt = d.inc_no
                        sem, val = qsems[d.eng][lane], 16 * cnt
                    else:
                        sem, val = sems[d.eng], d.inc_no
                    key = id(sem)
                    if cache.get(key, -1) >= val:
                        continue
                    eng.wait_ge(sem, val)
                    cache[key] = val
                ins = op.emit(eng)
                if op.inc_no is not None:
                    assert ins is not None, f"op {op.idx} on {engname} needs inc"
                    if op.dma:
                        lane, cnt = op.inc_no
                        ins.then_inc(qsems[op.eng][lane], 16)
                    else:
                        ins.then_inc(sems[op.eng], 1)
        return body

    with nc.Block() as block:
        for engname in ENG_NAMES:
            if any(op.eng == engname for op in ops):
                getattr(block, engname)(body_for(engname))


def build_program(ni=NT, do_attn=True, do_ph3=True, do_prep_wct=True,
                  do_rails=True, p1_gate=127, do_wcast=True, do_bias=True):
    nc = bass.Bass("TRN2", target_bir_lowering=False)

    x_in = nc.dram_tensor("x", [T, D], F32, kind="ExternalInput").ap()
    w_ins = {n: nc.dram_tensor(n, [D, D], F32, kind="ExternalInput").ap()
             for n in ("wq", "wk", "wv", "wo")}
    b_ins = {n: nc.dram_tensor(n, [D], F32, kind="ExternalInput").ap()
             for n in ("bq", "bk", "bv", "bo")}
    rm_in = nc.dram_tensor("rm", [4, 64, 16, 16], F32, kind="ExternalInput").ap()
    ru_in = nc.dram_tensor("ru", [4, 64, 64, 16], F32, kind="ExternalInput").ap()
    wcp_in = nc.dram_tensor("wcp", [D, MULT_F], F32, kind="ExternalInput").ap()
    out_hbm = nc.dram_tensor("out", [T, D], F32, kind="ExternalOutput").ap()

    wqkv_d = nc.dram_tensor("wqkv_d", [4, 8, 128, D], BF16, kind="Internal")
    wcT_d = nc.dram_tensor("wcT_d", [128, 128, D], BF16, kind="Internal")
    k_d = nc.dram_tensor("k_d", [32, 128, D], BF16, kind="Internal")
    v_d = nc.dram_tensor("v_d", [32, 128, H * 65], BF16, kind="Internal")
    x2_d = nc.dram_tensor("x2_d", [T, D], F32, kind="Internal")
    xh3_d = nc.dram_tensor("xh3_d", [8, 128, T], BF16, kind="Internal")
    rl_d = nc.dram_tensor("rl_d", [16, TT], BF16, kind="Internal")

    consts = nc.alloc_sbuf_tensor("consts", [128, 4], F32)
    ident_b = nc.alloc_sbuf_tensor("ident_b", [128, 128], BF16)
    ones_b = nc.alloc_sbuf_tensor("ones_b", [128, 128], BF16)
    masks2 = nc.alloc_sbuf_tensor("masks2", [128, 8 * TT], BF16)
    bias_all = nc.alloc_sbuf_tensor("bias_all", [128, 24], F32)
    bo_bh = [nc.alloc_sbuf_tensor(f"bo_bh{p}", [1, TT], BF16) for p in range(2)]
    statsA = nc.alloc_sbuf_tensor("statsA", [128, 12], F32)
    statsB = nc.alloc_sbuf_tensor("statsB", [128, 12], F32)
    aggrA = nc.alloc_sbuf_tensor("aggrA", [128, 2], F32)
    aggrB = nc.alloc_sbuf_tensor("aggrB", [128, 2], F32)
    rstdA = nc.alloc_sbuf_tensor("rstdA", [128, 1], F32)
    rstdB = nc.alloc_sbuf_tensor("rstdB", [128, 1], F32)
    zw = nc.alloc_sbuf_tensor("zw", [128, 32 * 128], BF16)
    wcin = [nc.alloc_sbuf_tensor(f"wcin{p}", [128, 512], F32) for p in range(3)]
    wcb = [nc.alloc_sbuf_tensor(f"wcb{p}", [128, 512], BF16) for p in range(3)]
    wcst = [nc.alloc_sbuf_tensor(f"wcst{p}", [128, 512], BF16) for p in range(1)]
    statsC = nc.alloc_sbuf_tensor("statsC", [128, 12], F32)
    statsD = nc.alloc_sbuf_tensor("statsD", [128, 12], F32)
    aggrC = nc.alloc_sbuf_tensor("aggrC", [128, 2], F32)
    aggrD = nc.alloc_sbuf_tensor("aggrD", [128, 2], F32)
    rstdC = nc.alloc_sbuf_tensor("rstdC", [128, 1], F32)
    rstdD = nc.alloc_sbuf_tensor("rstdD", [128, 1], F32)
    rmL = [nc.alloc_sbuf_tensor(f"rmL{p}", [128, 16], F32) for p in range(3)]
    rmLb = [nc.alloc_sbuf_tensor(f"rmLb{p}", [128, 16], BF16) for p in range(3)]
    rmT = [nc.alloc_sbuf_tensor(f"rmT{p}", [16, 128], BF16) for p in range(3)]
    ruL = [nc.alloc_sbuf_tensor(f"ruL{p}", [128, 16], F32) for p in range(3)]
    ruLb = [nc.alloc_sbuf_tensor(f"ruLb{p}", [128, 16], BF16) for p in range(3)]
    ruT = [nc.alloc_sbuf_tensor(f"ruT{p}", [16, 512], BF16) for p in range(3)]
    cstage = [nc.alloc_sbuf_tensor(f"cstage{p}", [16, 64], BF16) for p in range(2)]
    ltmp = nc.alloc_sbuf_tensor("ltmp", [1, TT], F32)

    cursor = 229376 - nc.sbuf_bytes_remaining
    base = (cursor + 127) & ~127
    off = [0]

    def at(name, shape, dtype, nbytes):
        t = nc.alloc_sbuf_tensor_at(name, shape, dtype, offset=base + off[0])
        off[0] += (nbytes + 127) & ~127
        return t

    wbuf = [at(f"wbuf{p}", [128, 8 * 1024], BF16, 16384) for p in range(2)]
    q_sb = at("q_sb", [128, 8 * TT], BF16, 8192)
    xh1T = at("xh1T", [128, 8 * TT], BF16, 8192)
    aoT = at("aoT", [128, 8 * TT], BF16, 8192)
    x_tile = [at(f"x_tile{s}", [128, D], F32, 4096) for s in range(2)]
    xres = [at(f"xres{s}", [128, D], F32, 4096) for s in range(2)]
    xh1 = [at(f"xh1{p}", [128, D], BF16, 2048) for p in range(2)]
    k_tile = at("k_tile", [128, 8 * TT], BF16, 8192)
    expt = k_tile
    v_tile = at("v_tile", [128, 8 * TT], BF16, 8192)
    kread = [at(f"kread{p}", [128, D], BF16, 2048) for p in range(NSLOT)]
    vread = [at(f"vread{p}", [128, H * 65], BF16, 2112) for p in range(NSLOT)]
    ao_acc = at("ao_acc", [128, 8 * TT], F32, 16384)
    l_parts = at("l_parts", [16, 2 * TT], F32, 4096)
    rl_all = at("rl_all", [16, TT], F32, 2048)
    rl_bf = at("rl_bf", [16, TT], BF16, 1024)
    v_stage = [at(f"v_stage{p}", [128, H * 65], BF16, 2112) for p in range(1)]
    x2_sb = [at(f"x2_sb{p}", [128, D], F32, 4096) for p in range(2)]
    x3h = [at(f"x3h{p}", [128, D], BF16, 2048) for p in range(4)]
    x3st = [at(f"x3st{p}", [128, 8 * 128], BF16, 2048) for p in range(1)]
    rlb = [at(f"rlb{p}", [128, TT], BF16, 2048) for p in range(2)]
    ph12_end = off[0]

    off[0] = 0
    out_q = at("out_q", [128, 8 * 1024], F32, 32768)
    h_tl = [at(f"h_tl{p}", [128, 16 * TT], BF16, 16384) for p in range(2)]
    wcbuf = [at(f"wcbuf{p}", [128, 8 * 1024], BF16, 16384) for p in range(2)]
    xh3q = at("xh3q", [128, 8 * 1024], BF16, 16384)
    x2r = [at(f"x2r{p}", [128, D], F32, 4096) for p in range(2)]
    ident_f = at("ident_f", [128, 128], F32, 512)
    outst = [at(f"outst{p}", [128, D], F32, 4096) for p in range(2)]
    ph3_end = off[0]
    assert base + max(ph12_end, ph3_end) <= 229376, (base, ph12_end, ph3_end)

    psa = nc.alloc_psum_tensor("psa", [128, 1024], F32)
    psc = nc.alloc_psum_tensor("psc", [128, 2048], F32)
    pstp2 = [nc.alloc_psum_tensor("pstpA", [128, 1024], BF16),
             nc.alloc_psum_tensor("pstpB", [128, 1024], BF16)]
    tpar = [0]

    sems = {e: nc.alloc_semaphore(f"s_{e}") for e in ENG_NAMES}
    qsems = {e: [nc.alloc_semaphore(f"q_{e}{s}") for s in range(NLANE)]
             for e in ("sync", "scalar", "gpsimd")}
    inc_c = {e: 0 for e in ENG_NAMES}
    qlane_c = {e: [0] * NLANE for e in ("sync", "scalar", "gpsimd")}
    qlane_last = {e: [None] * NLANE for e in ("sync", "scalar", "gpsimd")}

    S = Sched()

    def G(eng, emit, reads=(), writes=(), dma=False):
        S.add(eng, emit, reads, writes, dma)

    ln_sets = ((statsA, aggrA, rstdA, "A"), (statsB, aggrB, rstdB, "B"),
               (statsC, aggrC, rstdC, "C"), (statsD, aggrD, rstdD, "D"))
    ln_rot = [0]

    def ln_stats(par, src0, src1, rtags):
        st, ag, rs, p = ln_sets[ln_rot[0] % 4]
        ln_rot[0] += 1
        G("vector", lambda v, a=src0, st=st: v.bn_stats(st[:, 0:6], a),
          reads=rtags, writes=[f"st0{p}"])
        G("vector", lambda v, a=src1, st=st: v.bn_stats(st[:, 6:12], a),
          reads=rtags, writes=[f"st1{p}"])
        G("vector", lambda v, st=st, ag=ag: v.bn_aggr(ag[:], st[:]),
          reads=[f"st0{p}", f"st1{p}"], writes=[f"ag{p}"])
        G("scalar", lambda s, ag=ag, rs=rs: s.activation(
            rs[:], ag[:, 1:2], AF.Ln, bias=consts[:, 1:2]),
          reads=[f"ag{p}", "c:eps"], writes=[f"rsL{p}"])
        G("scalar", lambda s, rs=rs: s.activation(rs[:], rs[:], AF.Exp, scale=-0.5),
          reads=[f"rsL{p}"], writes=[f"rs{p}"])
        return ag, rs, (f"ag{p}", f"rs{p}")

    G("gpsimd", lambda g: g.memset(consts[:, 0:1], B_EXP), writes=["c:bexp"])
    G("gpsimd", lambda g: g.memset(consts[:, 1:2], EPS), writes=["c:eps"])
    G("gpsimd", lambda g: g.memset(ones_b[:], 1.0), writes=["ones_b"])
    G("gpsimd", lambda g: g.affine_select(ident_b[:], ones_b[:], [[1, 128]],
                                          ALU.is_equal, 0.0, base=0,
                                          channel_multiplier=-1),
      reads=["ones_b"], writes=["ident_b"])
    G("gpsimd", lambda g: g.memset(masks2[:], 1.0), writes=["m2init"])
    for d in range(4):
        G("gpsimd", lambda g, d=d: g.affine_select(
            masks2[:, d * 1024:(d + 1) * 1024],
            masks2[:, d * 1024:(d + 1) * 1024], [[0, 2], [1, TT]],
            ALU.is_ge, 0.0, base=-128 * d, channel_multiplier=-1),
          reads=["m2init"], writes=[f"mask2_{d}"])
    G("gpsimd", lambda g: g.memset(zw[:], 0.0), writes=["zw"])
    G("gpsimd", lambda g: g.memset(v_stage[0][:], 1.0), writes=["v_stage0"])

    for pi, bn in enumerate(("bq", "bk", "bv") if do_bias else ()):
        for c in range(8):
            src = b_ins[bn][c * 128:(c + 1) * 128].rearrange("(p o) -> p o", o=1)
            col = pi * 8 + c
            G("gpsimd", lambda g, src=src, col=col: g.dma_start(
                out=bias_all[:, col:col + 1], in_=src),
              writes=[f"bias{col}"], dma=True)
    for hf in range(2):
        G("gpsimd", lambda g, hf=hf: g.dma_start(
            out=ltmp[:], in_=b_ins["bo"][hf * TT:(hf + 1) * TT]
            .rearrange("(o d) -> o d", o=1)),
          writes=["ltmp"], dma=True)
        G("gpsimd", lambda g, hf=hf: g.tensor_copy(bo_bh[hf][:], ltmp[:]),
          reads=["ltmp"], writes=["bo_b"])

    for pi, wn in enumerate(("wq", "wk", "wv", "wo") if do_wcast else ()):
        for c in range(8):
            for hf in range(2):
                p = hf
                G("gpsimd", lambda g, wn=wn, c=c, hf=hf, p=p: g.dma_start(
                    out=wcin[p][:],
                    in_=w_ins[wn][c * 128:(c + 1) * 128, hf * 512:(hf + 1) * 512]),
                  writes=[f"wcin{p}"], dma=True)
                G("vector", lambda v, p=p: v.tensor_copy(wcb[p][:], wcin[p][:]),
                  reads=[f"wcin{p}"], writes=[f"wcb{p}"])
                G("gpsimd", lambda g, pi=pi, c=c, hf=hf, p=p: g.dma_start(
                    out=wqkv_d.ap()[pi, c, :, hf * 512:(hf + 1) * 512],
                    in_=wcb[p][:]),
                  reads=[f"wcb{p}"], writes=[f"wqkv:{pi}:{c}"], dma=True)

    def rails_prep(h, oc, pp):
        rsl = rm_in[h, oc * 8:(oc + 1) * 8].rearrange("r i j -> (r i) j")
        G("sync", lambda e, rsl=rsl, pp=pp: e.dma_start(out=rmL[pp][:], in_=rsl),
          writes=[f"rmL{pp}"], dma=True)
        G("gpsimd", lambda g, pp=pp: g.tensor_copy(rmLb[pp][:], rmL[pp][:]),
          reads=[f"rmL{pp}"], writes=[f"rmLb{pp}"])
        tb = tpar[0] % 2; tpar[0] += 1
        G("tensor", lambda t, tb=tb, pp=pp: t.matmul(
            pstp2[tb][0:16, 0:128], rmLb[pp][:], ident_b[:], is_transpose=True),
          reads=[f"rmLb{pp}", "ident_b"], writes=[f"pstp{tb}"])
        G("vector", lambda v, tb=tb, pp=pp: v.tensor_copy(
            rmT[pp][:], pstp2[tb][0:16, 0:128]),
          reads=[f"pstp{tb}"], writes=[f"rmT{pp}"])
        for pr in range(4):
            r0 = oc * 8 + pr * 2
            usl = ru_in[h, r0:r0 + 2].rearrange("r k j -> (r k) j")
            G("sync", lambda e, usl=usl, pp=pp: e.dma_start(out=ruL[pp][:], in_=usl),
              writes=[f"ruL{pp}"], dma=True)
            G("gpsimd", lambda g, pp=pp: g.tensor_copy(ruLb[pp][:], ruL[pp][:]),
              reads=[f"ruL{pp}"], writes=[f"ruLb{pp}"])
            tb = tpar[0] % 2; tpar[0] += 1
            G("tensor", lambda t, tb=tb, pp=pp: t.matmul(
                pstp2[tb][0:16, 128:256], ruLb[pp][:], ident_b[:],
                is_transpose=True),
              reads=[f"ruLb{pp}", "ident_b"], writes=[f"pstp{tb}"])
            G("vector", lambda v, tb=tb, pp=pp, pr=pr: v.tensor_copy(
                ruT[pp][:, pr * 128:(pr + 1) * 128], pstp2[tb][0:16, 128:256]),
              reads=[f"pstp{tb}"], writes=[f"ruT{pp}"])

    def rails_comp(h, oc, pp):
        for pr in range(4):
            r0 = oc * 8 + pr * 2
            for q in range(2):
                r = r0 + q
                cs = (pr * 2 + q) % 2
                tbc = tpar[0] % 2
                cps = pstp2[tbc][0:16, 512:1024].bitcast(F32)
                G("tensor", lambda t, pr=pr, q=q, pp=pp, cps=cps: t.matmul(
                    cps[0:16, 0:64],
                    rmT[pp][0:16, (pr * 2 + q) * 16:(pr * 2 + q) * 16 + 16],
                    ruT[pp][0:16, pr * 128 + q * 64:pr * 128 + (q + 1) * 64]),
                  reads=[f"rmT{pp}", f"ruT{pp}"], writes=[f"pstp{tbc}"])
                G("vector", lambda v, cs=cs, cps=cps: v.tensor_copy(
                    cstage[cs][:], cps[0:16, 0:64]),
                  reads=[f"pstp{tbc}"], writes=[f"cstage{cs}"])
                pair, half = r // 2, r % 2
                a, cb = pair % 4, (h * 32 + pair) // 4
                dst = zw[32 * a + 16 * half:32 * a + 16 * half + 16,
                         cb * 128 + half * 64:cb * 128 + half * 64 + 64]
                G("scalar", lambda e, dst=dst, cs=cs: e.dma_start(
                    out=dst, in_=cstage[cs][:]),
                  reads=[f"cstage{cs}", "zw"], writes=["zw"], dma=True)

    def wcT_load(ec, ft, p):
        G("gpsimd", lambda g, ec=ec, ft=ft, p=p: g.dma_start(
            out=wcin[p][:],
            in_=wcp_in[ec * 128:(ec + 1) * 128, ft * 512:(ft + 1) * 512]),
          writes=[f"wcin{p}"], dma=True)
        G("vector", lambda v, p=p: v.tensor_copy(wcb[p][:], wcin[p][:]),
          reads=[f"wcin{p}"], writes=[f"wcb{p}"])

    def wcT_tr(ec, ft, p):
        tb = tpar[0] % 2; tpar[0] += 1
        for kk in range(4):
            G("tensor", lambda t, p=p, kk=kk, tb=tb: t.matmul(
                pstp2[tb][:, kk * 128:(kk + 1) * 128],
                wcb[p][:, kk * 128:(kk + 1) * 128],
                ident_b[:], is_transpose=True),
              reads=[f"wcb{p}", "ident_b"], writes=[f"pstp{tb}"])
        ps = 0
        G("vector", lambda v, ps=ps, tb=tb: v.tensor_copy(wcst[ps][:],
                                                          pstp2[tb][:, 0:512]),
          reads=[f"pstp{tb}"], writes=[f"wcst{ps}"])
        dst = wcT_d.ap()[4 * ft:4 * ft + 4, :, ec * 128:(ec + 1) * 128] \
            .rearrange("k p e -> p k e")
        srcw = wcst[ps][:].rearrange("p (k e) -> p k e", k=4)
        G("gpsimd", lambda e, dst=dst, srcw=srcw: e.dma_start(out=dst, in_=srcw),
          reads=[f"wcst{ps}"], writes=[f"wcT:{ec}:{ft}"], dma=True)

    def wload(pi, par, rtag):
        src = wqkv_d.ap()[pi].rearrange("c p n -> p c n")
        dst = wbuf[par][:].rearrange("p (c n) -> p c n", c=8)
        G("gpsimd", lambda e, src=src, dst=dst: e.dma_start(out=dst, in_=src),
          reads=[f"wqkv:{pi}:{c}" for c in range(8)],
          writes=[rtag], dma=True)

    def proj_mms(par, dest, bcol0, dtag, wtag, i):
        for c in range(8):
            bp = c % 4
            for kc in range(8):
                G("tensor", lambda t, par=par, c=c, kc=kc, bp=bp: t.matmul(
                    psc[:, bp * 512:(bp + 1) * 512],
                    wbuf[par][:, kc * 1024 + c * 128:kc * 1024 + c * 128 + 128],
                    xh1T[:, kc * 512:(kc + 1) * 512],
                    start=(kc == 0), stop=(kc == 7)),
                  reads=[wtag, f"xh1T{kc}"], writes=[f"psc{bp}"])
            extra_r = ["rl_all"] if dtag == "k_tile" and i > 0 else []
            G("vector", lambda v, dest=dest, c=c, bcol0=bcol0, bp=bp:
              v.tensor_scalar(dest[:, c * 512:(c + 1) * 512],
                              psc[:, bp * 512:(bp + 1) * 512],
                              bias_all[:, bcol0 + c:bcol0 + c + 1], None,
                              ALU.add),
              reads=[f"psc{bp}", f"bias{bcol0 + c}"] + extra_r,
              writes=[f"{dtag}:{c}"])

    lnres = [None, None]

    def kvload_early(j):
        sl = j % NSLOT
        G("sync", lambda e, j=j, sl=sl: e.dma_start(
            out=kread[sl][:], in_=k_d.ap()[j]),
          reads=[f"k_d:{j}"], writes=[f"ks{sl}", f"kvl:{j}"], dma=True)
        G("scalar", lambda e, j=j, sl=sl: e.dma_start(
            out=vread[sl][:], in_=v_d.ap()[j]),
          reads=[f"v_d:{j}"], writes=[f"vs{sl}", f"kvl:{j}"], dma=True)

    def kvload_early(j):
        sl = j % NSLOT
        G("sync", lambda e, j=j, sl=sl: e.dma_start(
            out=kread[sl][:], in_=k_d.ap()[j]),
          reads=[f"k_d:{j}"], writes=[f"ks{sl}", f"kvl:{j}"], dma=True)
        G("scalar", lambda e, j=j, sl=sl: e.dma_start(
            out=vread[sl][:], in_=v_d.ap()[j]),
          reads=[f"v_d:{j}"], writes=[f"vs{sl}", f"kvl:{j}"], dma=True)

    def p1a(i):
        wload(0, 0, "wbuf0")
        for s in range(4):
            row0 = i * TT + s * 128
            G("sync", lambda e, row0=row0, s=s: e.dma_start(
                out=x_tile[s % 2][:], in_=x_in[row0:row0 + 128, :]),
              writes=[f"x_tile{s % 2}"], dma=True)
            sp = s % 2
            ag, rs, lntags = ln_stats(0, x_tile[sp][:, 0:512],
                                      x_tile[sp][:, 512:1024], [f"x_tile{sp}"])
            lnres[sp] = (ag, rs, lntags)
            if sp == 1:
                for s2 in (s - 1, s):
                    sp2 = s2 % 2
                    ag, rs, lntags = lnres[sp2]
                    G("vector", lambda v, sp2=sp2, ag=ag, rs=rs: v.tensor_scalar(
                        xh1[sp2][:], x_tile[sp2][:], ag[:, 0:1], rs[:, 0:1],
                        ALU.subtract, ALU.mult),
                      reads=[f"x_tile{sp2}", *lntags], writes=[f"xh1{sp2}"])
                    tb = tpar[0] % 2; tpar[0] += 1
                    for c in range(8):
                        G("tensor", lambda t, sp2=sp2, c=c, tb=tb: t.matmul(
                            pstp2[tb][:, c * 128:(c + 1) * 128],
                            xh1[sp2][:, c * 128:(c + 1) * 128],
                            ident_b[:], is_transpose=True),
                          reads=[f"xh1{sp2}", "ident_b"], writes=[f"pstp{tb}"])
                    for c in range(8):
                        G("vector", lambda v, s2=s2, c=c, tb=tb: v.tensor_copy(
                            xh1T[:, c * 512 + s2 * 128:c * 512 + s2 * 128 + 128],
                            pstp2[tb][:, c * 128:(c + 1) * 128]),
                          reads=[f"pstp{tb}"], writes=[f"xh1T{c}"])
        proj_mms(0, q_sb, 0, "q_sb", "wbuf0", i)
        for j in range(min(JB, 4 * i + 4)):
            if j <= 4 * (i - 1) + 3:
                kvload_early(j)

    def p1b(i):
        wload(1, 1, "wbuf1")
        proj_mms(1, k_tile, 8, "k_tile", "wbuf1", i)
        wload(2, 0, "wbuf0")
        proj_mms(0, v_tile, 16, "v_tile", "wbuf0", i)
        for s in range(4):
            j = 4 * i + s
            srck = k_tile[:].rearrange("p (c n) -> p c n", c=8)[:, :, s * 128:(s + 1) * 128]
            dstk = k_d.ap()[j].rearrange("p (c n) -> p c n", c=8)
            G("scalar", lambda e, dstk=dstk, srck=srck: e.dma_start(
                out=dstk, in_=srck),
              reads=[f"k_tile:{c}" for c in range(8)],
              writes=[f"k_d:{j}"], dma=True)
        for s in range(4):
            j = 4 * i + s
            tb = tpar[0] % 2; tpar[0] += 1
            for c in range(8):
                G("tensor", lambda t, s=s, c=c, tb=tb: t.matmul(
                    pstp2[tb][:, c * 128:(c + 1) * 128],
                    v_tile[:, c * 512 + s * 128:c * 512 + s * 128 + 128],
                    ident_b[:], is_transpose=True),
                  reads=[f"v_tile:{c}", "ident_b"], writes=[f"pstp{tb}"])
            for c in range(8):
                for hh in range(2):
                    hgl = 2 * c + hh
                    G("vector", lambda v, c=c, hh=hh, hgl=hgl, tb=tb: v.tensor_copy(
                        v_stage[0][:, hgl * 65:hgl * 65 + 64],
                        pstp2[tb][:, c * 128 + hh * 64:c * 128 + hh * 64 + 64]),
                      reads=[f"pstp{tb}"], writes=["v_stage0"])
            G("scalar", lambda e, j=j: e.dma_start(
                out=v_d.ap()[j], in_=v_stage[0][:]),
              reads=["v_stage0"], writes=[f"v_d:{j}"], dma=True)
        wload(3, 1, "wbuf1")

    def attn(i):
        njc = 4 * i + 4
        jblocks = [(b0, min(b0 + JB, njc)) for b0 in range(0, njc, JB)]
        nb = len(jblocks)

        def kvload(j):
            sl = j % NSLOT
            G("sync", lambda e, j=j, sl=sl: e.dma_start(
                out=kread[sl][:], in_=k_d.ap()[j]),
              reads=[f"k_d:{j}"], writes=[f"ks{sl}"], dma=True)
            G("scalar", lambda e, j=j, sl=sl: e.dma_start(
                out=vread[sl][:], in_=v_d.ap()[j]),
              reads=[f"v_d:{j}"], writes=[f"vs{sl}"], dma=True)

        hoisted = set(j for j in range(min(JB, njc))
                      if j <= 4 * (i - 1) + 3)
        for bi, (j0, j1) in enumerate(jblocks):
            for j in range(j0, j1):
                if bi == 0 and j in hoisted:
                    continue
                kvload(j)
            def emit_sc(pr_, hh, j, ug):
                chunk = pr_
                hb = hh * 64
                slot = (ug % 2) * 2 + hh
                sl = j % NSLOT
                G("tensor", lambda t, sl=sl, slot=slot, chunk=chunk, hb=hb:
                  t.matmul(psc[:, slot * 512:(slot + 1) * 512],
                           kread[sl][hb:hb + 64, chunk * 128:chunk * 128 + 128],
                           q_sb[hb:hb + 64, chunk * 512:(chunk + 1) * 512],
                           start=True, stop=True),
                  reads=[f"ks{sl}", f"q_sb:{chunk}"],
                  writes=[f"psc{slot}"])

            def emit_exp(pr_, j, ug):
                sb2 = (ug % 2) * 2
                erg = ug % 4
                fence = ([f"k_d:{4 * i + ss}" for ss in range(4)]
                         if bi == 0 and pr_ == 0 and j == j0 else [])
                G("scalar", lambda s, sb2=sb2, erg=erg: s.activation(
                    expt[:, erg * 1024:(erg + 1) * 1024],
                    psc[:, sb2 * 512:(sb2 + 2) * 512],
                    AF.Exp, bias=consts[:, 0:1], scale=0.125),
                  reads=[f"psc{sb2}", f"psc{sb2 + 1}", "c:bexp"] + fence,
                  writes=[f"expt{erg}"])
                dd = j - 4 * i
                if dd >= 0:
                    ecol = erg * 1024
                    G("vector", lambda v, ecol=ecol, dd=dd, erg=erg:
                      v.tensor_tensor(
                          expt[:, ecol:ecol + 1024],
                          expt[:, ecol:ecol + 1024],
                          masks2[:, dd * 1024:(dd + 1) * 1024], ALU.mult),
                      reads=[f"expt{erg}", f"mask2_{dd}"],
                      writes=[f"expt{erg}"])

            def emit_av(pr_, j, ug):
                h0 = 2 * pr_
                erg = ug % 4
                for hh in range(2):
                    h = h0 + hh
                    sl = j % NSLOT
                    ecol = erg * 1024 + hh * 512
                    G("tensor", lambda t, sl=sl, ecol=ecol, h=h, hh=hh,
                      st=(j == j0), sp=(j == j1 - 1): t.matmul(
                          psa[0:65, hh * 512:(hh + 1) * 512],
                          vread[sl][:, h * 65:h * 65 + 65],
                          expt[:, ecol:ecol + 512],
                          start=st, stop=sp),
                      reads=[f"vs{sl}", f"expt{erg}"], writes=[f"psa{hh}"])

            def emit_evac(pr_):
                h0 = 2 * pr_
                chunk = pr_
                for hh in range(2):
                    h, hb = h0 + hh, hh * 64
                    asl = psa[:, hh * 512:(hh + 1) * 512]
                    if bi == 0:
                        G("vector", lambda v, asl=asl, hb=hb, chunk=chunk:
                          v.tensor_copy(
                              ao_acc[hb:hb + 64, chunk * 512:(chunk + 1) * 512],
                              asl[0:64, :]),
                          reads=[f"psa{hh}"], writes=[f"ao:{h}"])
                    else:
                        G("vector", lambda v, asl=asl, hb=hb, chunk=chunk:
                          v.tensor_add(
                              ao_acc[hb:hb + 64, chunk * 512:(chunk + 1) * 512],
                              ao_acc[hb:hb + 64, chunk * 512:(chunk + 1) * 512],
                              asl[0:64, :]),
                          reads=[f"psa{hh}", f"ao:{h}"], writes=[f"ao:{h}"])
                    G("scalar", lambda s, asl=asl: s.copy(ltmp[:], asl[64:65, :]),
                      reads=[f"psa{hh}"], writes=["ltmp"])
                    col = 0 if bi == 0 else TT
                    war = [f"ladd:{bi - 1}"] if bi >= 2 else []
                    G("scalar", lambda e, h=h, bi=bi, col=col: e.dma_start(
                        out=l_parts[h:h + 1, col:col + TT],
                        in_=ltmp[:]),
                      reads=["ltmp"] + war, writes=[f"lp:{h}:{bi}"], dma=True)

            units = [(pr_, j) for pr_ in range(8) for j in range(j0, j1)]
            pend = []
            for pr_, j in units:
                ug = ucnt[0]; ucnt[0] += 1
                emit_sc(pr_, 0, j, ug)
                emit_sc(pr_, 1, j, ug)
                emit_exp(pr_, j, ug)
                pend.append((pr_, j, ug))
                if len(pend) > 3:
                    pu = pend.pop(0)
                    emit_av(*pu)
                    if pu[1] == j1 - 1:
                        emit_evac(pu[0])
                rails_step()
            for pu in pend:
                emit_av(*pu)
                if pu[1] == j1 - 1:
                    emit_evac(pu[0])
            if bi > 0:
                G("vector", lambda v: v.tensor_add(
                    l_parts[:, 0:TT], l_parts[:, 0:TT], l_parts[:, TT:2 * TT]),
                  reads=[f"lp:{h}:{bi}" for h in range(H)] +
                        ([f"lp:{h}:0" for h in range(H)] if bi == 1 else ["lsum"]),
                  writes=["lsum", f"ladd:{bi}"])

        return nb

    def attn_norm(i, nb):
        lsrc = ["lsum"] if nb > 1 else [f"lp:{h}:0" for h in range(H)]
        G("scalar", lambda s: s.activation(rl_all[:], l_parts[:, 0:TT], AF.Ln),
          reads=lsrc, writes=["rlL"])
        G("scalar", lambda s: s.activation(rl_all[:], rl_all[:], AF.Exp, scale=-1.0),
          reads=["rlL"], writes=["rl_all"])
        G("vector", lambda v: v.tensor_copy(rl_bf[:], rl_all[:]),
          reads=["rl_all"], writes=["rl_bf"])
        G("sync", lambda e: e.dma_start(out=rl_d.ap()[:, :], in_=rl_bf[:]),
          reads=["rl_bf"], writes=["rl_dd"], dma=True)
        for chunk in range(8):
            h0, h1, rp = 2 * chunk, 2 * chunk + 1, chunk % 2
            bsrc0 = bass.AP(rl_d, h0 * TT, [[0, 64], [1, TT]])
            bsrc1 = bass.AP(rl_d, h1 * TT, [[0, 64], [1, TT]])
            G("sync", lambda e, bsrc0=bsrc0, rp=rp: e.dma_start(
                out=rlb[rp][0:64, :], in_=bsrc0),
              reads=["rl_dd"], writes=[f"rlb{rp}"], dma=True)
            G("sync", lambda e, bsrc1=bsrc1, rp=rp: e.dma_start(
                out=rlb[rp][64:128, :], in_=bsrc1),
              reads=["rl_dd"], writes=[f"rlb{rp}"], dma=True)
            G("vector", lambda v, chunk=chunk, rp=rp: v.tensor_tensor(
                aoT[:, chunk * 512:(chunk + 1) * 512],
                ao_acc[:, chunk * 512:(chunk + 1) * 512],
                rlb[rp][:], ALU.mult),
              reads=[f"ao:{h0}", f"ao:{h1}", f"rlb{rp}"],
              writes=[f"aoT:{chunk}"])

    def wo_s(i):
        def emit_x3T(s):
            row0 = i * TT + s * 128
            tb = tpar[0] % 2; tpar[0] += 1
            for c in range(8):
                G("tensor", lambda t, s=s, c=c, tb=tb: t.matmul(
                    pstp2[tb][:, c * 128:(c + 1) * 128],
                    x3h[s][:, c * 128:(c + 1) * 128],
                    ident_b[:], is_transpose=True),
                  reads=[f"x3h{s}", "ident_b"], writes=[f"pstp{tb}"])
            G("vector", lambda v, tb=tb: v.tensor_copy(x3st[0][:], pstp2[tb][:]),
              reads=[f"pstp{tb}"], writes=["x3st0"])
            dst = xh3_d.ap()[:, :, row0:row0 + 128].rearrange("c p n -> p c n")
            srcx = x3st[0][:].rearrange("p (c n) -> p c n", c=8)
            G("scalar", lambda e, dst=dst, srcx=srcx: e.dma_start(
                out=dst, in_=srcx),
              reads=["x3st0"], writes=[f"xh3:{row0}"], dma=True)

        for s in range(4):
            sp = s % 2
            sb2 = sp * 2
            row0 = i * TT + s * 128
            G("sync", lambda e, row0=row0, sp=sp: e.dma_start(
                out=xres[sp][:], in_=x_in[row0:row0 + 128, :]),
              writes=[f"xres{sp}"], dma=True)
            for half in range(2):
                slot = sb2 + half
                for kc in range(8):
                    G("tensor", lambda t, s=s, slot=slot, half=half, kc=kc: t.matmul(
                        psc[:, slot * 512:(slot + 1) * 512],
                        aoT[:, kc * 512 + s * 128:kc * 512 + s * 128 + 128],
                        wbuf[1][:, kc * 1024 + half * 512:kc * 1024 + (half + 1) * 512],
                        start=(kc == 0), stop=False),
                      reads=[f"aoT:{kc}", "wbuf1"], writes=[f"psc{slot}"])
                G("tensor", lambda t, slot=slot, half=half: t.matmul(
                    psc[:, slot * 512:(slot + 1) * 512],
                    ones_b[0:1, :], bo_bh[half][:],
                    start=False, stop=True),
                  reads=["ones_b", "bo_b"], writes=[f"psc{slot}"])
            if s > 0:
                emit_x3T(s - 1)
            ag, rs, lntags = ln_stats(0, psc[:, sb2 * 512:sb2 * 512 + 512],
                                      psc[:, sb2 * 512 + 512:sb2 * 512 + 1024],
                                      [f"psc{sb2}", f"psc{sb2 + 1}"])
            G("vector", lambda v, sp=sp, sb2=sb2, ag=ag, rs=rs: v.tensor_scalar(
                x2_sb[sp][:], psc[:, sb2 * 512:sb2 * 512 + 1024],
                ag[:, 0:1], rs[:, 0:1], ALU.subtract, ALU.mult),
              reads=[f"psc{sb2}", f"psc{sb2 + 1}", *lntags], writes=[f"x2sb{sp}"])
            G("vector", lambda v, sp=sp: v.tensor_add(
                x2_sb[sp][:], x2_sb[sp][:], xres[sp][:]),
              reads=[f"x2sb{sp}", f"xres{sp}"], writes=[f"x2sb{sp}"])
            G("scalar", lambda e, row0=row0, sp=sp: e.dma_start(
                out=x2_d.ap()[row0:row0 + 128, :], in_=x2_sb[sp][:]),
              reads=[f"x2sb{sp}"], writes=[f"x2_d:{row0}"], dma=True)
            ag, rs, lntags = ln_stats(1, x2_sb[sp][:, 0:512], x2_sb[sp][:, 512:1024],
                                      [f"x2sb{sp}"])
            G("vector", lambda v, s=s, sp=sp, ag=ag, rs=rs: v.tensor_scalar(
                x3h[s][:], x2_sb[sp][:], ag[:, 0:1], rs[:, 0:1],
                ALU.subtract, ALU.mult),
              reads=[f"x2sb{sp}", *lntags], writes=[f"x3h{s}"])
        emit_x3T(3)

    wct_seq = ([(ec, ft) for ec in range(8) for ft in range(32)]
               if do_prep_wct else [])
    rl_seq = ([(h, oc) for h in range(4) for oc in range(8)] if do_rails else [])
    wct_li, wct_ti = [0], [0]
    rl_li, rl_ti = [0], [0]
    nbg = max(1, len(wct_seq) + len(rl_seq))
    bg_cnt = [0]
    bg_thresh = 1100.0 / nbg
    def bg_step_one():
        done_w = wct_ti[0] >= len(wct_seq)
        done_r = rl_ti[0] >= len(rl_seq)
        use_r = (not done_r) and (done_w or (wct_ti[0] % 8 == 7 and rl_ti[0] * 8 < wct_ti[0]))
        if use_r:
            while rl_li[0] < len(rl_seq) and rl_li[0] <= rl_ti[0] + 2:
                rails_prep(*rl_seq[rl_li[0]], rl_li[0] % 3)
                rl_li[0] += 1
            rails_comp(*rl_seq[rl_ti[0]], rl_ti[0] % 3)
            rl_ti[0] += 1
        elif not done_w:
            while wct_li[0] < len(wct_seq) and wct_li[0] <= wct_ti[0] + 2:
                wcT_load(*wct_seq[wct_li[0]], wct_li[0] % 3)
                wct_li[0] += 1
            wcT_tr(*wct_seq[wct_ti[0]], wct_ti[0] % 3)
            wct_ti[0] += 1

    def rails_step():
        bg_cnt[0] += 1
        while bg_cnt[0] >= bg_thresh:
            bg_cnt[0] -= bg_thresh
            bg_step_one()

    def bg_flush():
        while wct_ti[0] < len(wct_seq) or rl_ti[0] < len(rl_seq):
            bg_step_one()

    ucnt = [0]

    p1a(0)
    p1b(0)
    for i in range(ni):
        nb = attn(i)
        if i + 1 < ni:
            p1a(i + 1)
        attn_norm(i, nb)
        wo_s(i)
        if i + 1 < ni:
            p1b(i + 1)
    bg_flush()

    emit_schedule(nc, S, sems, qsems, inc_c, qlane_c, qlane_last)

    S2 = Sched()
    S2.last_w = dict(S.last_w)
    S2.readers = {k: set(v) for k, v in S.readers.items()}
    S = S2
    all_out = []
    cslot = (psa[:, 0:512], psa[:, 512:1024],
             pstp2[0][:, 0:1024].bitcast(F32), pstp2[1][:, 0:1024].bitcast(F32))
    ctags = ("psa0", "psa1", "pstp0", "pstp1")
    ztags = ("psc0", "psc1", "psc2", "psc3")

    def xh3q_loads(Q):
        for c in range(8):
            rows = [f"xh3:{r}" for r in range(Q * 1024, (Q + 1) * 1024, 128)]
            G("sync", lambda e, Q=Q, c=c: e.dma_start(
                out=xh3q[:, c * 1024:(c + 1) * 1024],
                in_=xh3_d.ap()[c, :, Q * 1024:(Q + 1) * 1024]),
              reads=rows, writes=[f"xh3q:{c}"], dma=True)

    def z_unit(g, s2, k):
        zslots = (psc[:, 0:512], psc[:, 512:1024],
                  psc[:, 1024:1536], psc[:, 1536:2048])
        fc = g * 16 + k
        rr = (g % 2) * 16 + k
        a, cb, cx = rr % 4, fc // 4, rr // 4
        G("tensor", lambda t, a=a, cb=cb, cx=cx, s2=s2, k=k: t.matmul(
            zslots[k % 4],
            zw[32 * a:32 * a + 32, cb * 128:(cb + 1) * 128],
            xh3q[32 * a:32 * a + 32,
                 cx * 1024 + s2 * 512:cx * 1024 + (s2 + 1) * 512],
            start=True, stop=True, tile_position=(32 * a, 0)),
          reads=["zw", f"xh3q:{cx}"], writes=[ztags[k % 4]])
        if k % 2 == 1:
            kb = k - 1
            G("scalar", lambda s, kb=kb, hpar=s2: s.activation(
                h_tl[hpar][:, kb * 512:(kb + 2) * 512],
                psc[:, (kb % 4) * 512:(kb % 4 + 2) * 512], AF.Gelu),
              reads=[ztags[kb % 4], ztags[kb % 4 + 1]],
              writes=[f"h{s2}:{kb}", f"h{s2}:{kb + 1}"])

    tiles = [(Q, g, s2) for Q in range(NQ if do_ph3 else 0)
             for g in range(8) for s2 in range(2)]
    for ti, (Q, g, s2) in enumerate(tiles):
        if ti == 0:
            xh3q_loads(Q)
        if s2 == 0:
            for hg in range(2):
                par = (g + hg) % 2
                fc0 = g * 16 + hg * 8
                src = wcT_d.ap()[fc0:fc0 + 8].rearrange("k p n -> p k n")
                dst = wcbuf[par][:].rearrange("p (k n) -> p k n", k=8)
                wct_tags = [f"wcT:{ec}:{ft}" for ec in range(8)
                            for ft in (fc0 // 4, fc0 // 4 + 1)]
                G("sync", lambda e, src=src, dst=dst: e.dma_start(out=dst, in_=src),
                  reads=wct_tags, writes=[f"wcbuf{par}"], dma=True)
        if ti == 0:
            for k in range(16):
                z_unit(g, s2, k)
        nxt = tiles[ti + 1] if ti + 1 < len(tiles) else None
        qcross = nxt is not None and nxt[0] != Q
        if qcross:
            xh3q_loads(nxt[0])
        hpar = s2
        for tc in range(4):
            sg = s2 * 4 + tc
            for k in range(16):
                par = (g + k // 8) % 2
                for sl in range(2):
                    ci = (tc % 2) * 2 + sl
                    G("tensor", lambda t, k=k, tc=tc, sl=sl, ci=ci,
                      par=par, hpar=hpar: t.matmul(
                          cslot[ci],
                          h_tl[hpar][:, k * 512 + tc * 128:
                                      k * 512 + tc * 128 + 128],
                          wcbuf[par][:, (k % 8) * 1024 + sl * 512:
                                     (k % 8) * 1024 + (sl + 1) * 512],
                          start=(k == 0), stop=(k == 15)),
                      reads=[f"wcbuf{par}", f"h{hpar}:{k}"],
                      writes=[ctags[ci]])
            for sl in range(2):
                ci = (tc % 2) * 2 + sl
                ocol = sg * 1024 + sl * 512
                if g == 0:
                    G("vector", lambda v, ci=ci, ocol=ocol:
                      v.tensor_copy(out_q[:, ocol:ocol + 512], cslot[ci]),
                      reads=[ctags[ci]], writes=[f"oq:{ocol}"])
                else:
                    G("vector", lambda v, ci=ci, ocol=ocol:
                      v.tensor_add(out_q[:, ocol:ocol + 512],
                                   out_q[:, ocol:ocol + 512], cslot[ci]),
                      reads=[ctags[ci], f"oq:{ocol}"],
                      writes=[f"oq:{ocol}"])
            if nxt is not None:
                if qcross:
                    kr = range((tc - 2) * 8, (tc - 1) * 8) if tc >= 2 else ()
                else:
                    kr = range(tc * 4, (tc + 1) * 4)
                for k in kr:
                    z_unit(nxt[1], nxt[2], k)
        if g == 7 and s2 == 1:
            for s in range(8):
                sp = s % 2
                row0 = Q * 1024 + s * 128
                G("sync", lambda e, row0=row0, sp=sp: e.dma_start(
                    out=x2r[sp][:], in_=x2_d.ap()[row0:row0 + 128, :]),
                  reads=[f"x2_d:{row0}"], writes=[f"x2r{sp}"], dma=True)
                oqt = [f"oq:{s * 1024}", f"oq:{s * 1024 + 512}"]
                ag, rs, lntags = ln_stats(0, out_q[:, s * 1024:s * 1024 + 512],
                                          out_q[:, s * 1024 + 512:(s + 1) * 1024],
                                          oqt)
                G("vector", lambda v, s=s, sp=sp, ag=ag, rs=rs: v.tensor_scalar(
                    outst[sp][:], out_q[:, s * 1024:(s + 1) * 1024], ag[:, 0:1],
                    rs[:, 0:1], ALU.subtract, ALU.mult),
                  reads=[*oqt, *lntags], writes=[f"outst{sp}"])
                G("vector", lambda v, sp=sp: v.tensor_add(
                    outst[sp][:], outst[sp][:], x2r[sp][:]),
                  reads=[f"outst{sp}", f"x2r{sp}"], writes=[f"outst{sp}"])
                G("scalar", lambda e, row0=row0, sp=sp: e.dma_start(
                    out=out_hbm[row0:row0 + 128, :], in_=outst[sp][:]),
                  reads=[f"outst{sp}"], writes=[f"outd:{row0}"], dma=True)
                all_out.append(f"outd:{row0}")

    S.add("sync", lambda e: None, reads=all_out)
    emit_schedule(nc, S, sems, qsems, inc_c, qlane_c, qlane_last)
    return nc


_NC_CACHE = None


def kernel(**inputs):
    global _NC_CACHE
    if _NC_CACHE is None:
        _NC_CACHE = build_program()
    nc = _NC_CACHE
    x = np.ascontiguousarray(np.asarray(inputs["L0_attn_x"], dtype=np.float32))
    common = {
        "wq": np.ascontiguousarray(np.asarray(inputs["L0_attn_wq"], np.float32)),
        "wk": np.ascontiguousarray(np.asarray(inputs["L0_attn_wk"], np.float32)),
        "wv": np.ascontiguousarray(np.asarray(inputs["L0_attn_wv"], np.float32)),
        "wo": np.ascontiguousarray(np.asarray(inputs["L0_attn_wo"], np.float32)),
        "bq": np.ascontiguousarray(np.asarray(inputs["L0_attn_bq"], np.float32)),
        "bk": np.ascontiguousarray(np.asarray(inputs["L0_attn_bk"], np.float32)),
        "bv": np.ascontiguousarray(np.asarray(inputs["L0_attn_bv"], np.float32)),
        "bo": np.ascontiguousarray(np.asarray(inputs["L0_attn_bo"], np.float32)),
        "rm": np.ascontiguousarray(np.asarray(inputs["L0_ffn_rail_maker"], np.float32)),
        "ru": np.ascontiguousarray(np.asarray(inputs["L0_ffn_rail_up_proj"], np.float32)),
        "wcp": np.ascontiguousarray(np.asarray(inputs["L0_ffn_w_cproj"], np.float32)),
    }
    in_maps = []
    for c in range(NCORES):
        m = dict(common)
        m["x"] = np.ascontiguousarray(x[c])
        in_maps.append(m)
    res = run_bass_kernel_spmd(nc, in_maps, core_ids=list(range(NCORES)))
    return np.stack([r["out"] for r in res.results], axis=0)
